# version 10
# baseline (speedup 1.0000x reference)
import contextlib
import numpy as np
import ml_dtypes

import concourse.bass as bass
import concourse.mybir as mybir
from concourse.bass_utils import run_bass_kernel_spmd
from concourse.library_config import mlp as LIB_MLP

F32 = mybir.dt.float32
BF16 = mybir.dt.bfloat16
I32 = mybir.dt.int32
I16 = mybir.dt.int16
ALU = mybir.AluOpType
ACTF = mybir.ActivationFunctionType

NCORES = 8
FULL = dict(B=2, S=4096, D=4096, KVH=8, G=4, IFF=11008, CBN=65536)
MINI = dict(B=2, S=2048, D=1024, KVH=2, G=2, IFF=2048, CBN=512)
EXP_SHIFT = -30.0
LN_EPS = 1e-5


def derive(cfg):
    c = dict(cfg)
    c["NH"] = c["KVH"] * c["G"]
    c["QD"] = c["NH"] * 128
    c["KVD"] = c["KVH"] * 128
    c["QKVD"] = c["QD"] + 2 * c["KVD"]
    c["OWN"] = c["S"] // 4
    c["NQ"] = c["OWN"]
    c["HCH"] = c["D"] // 128
    c["QDCH"] = c["QD"] // 128
    c["OTT"] = c["OWN"] // 128
    c["SKC"] = c["S"] // 128
    c["OKC"] = c["OWN"] // 128
    c["ICH"] = c["IFF"] // 128
    c["NQH"] = max(1, c["NQ"] // 512)
    c["GJ"] = c["D"] // 8 // NCORES
    c["DJ"] = c["IFF"] // 8 // NCORES
    assert c["GJ"] * 8 * NCORES == c["D"] and c["DJ"] * 8 * NCORES == c["IFF"]
    ich = c["ICH"]
    nb = max(1, (ich + 21) // 22)
    base, rem = ich // nb, ich % nb
    c["IBLOCKS"] = [base + (1 if i < rem else 0) for i in range(nb)]
    return c


def runs_of(n, r):
    out = []
    while n > 0:
        out.append(min(r, n))
        n -= out[-1]
    return out


def build_program(cfg, phases=9, dbg_no_prep=False):
    C = derive(cfg)
    S, D = C["S"], C["D"]
    KVH, G, NH = C["KVH"], C["G"], C["NH"]
    QD, KVD, QKVD, QDCH = C["QD"], C["KVD"], C["QKVD"], C["QDCH"]
    OWN, NQ, HCH, OTT = C["OWN"], C["NQ"], C["HCH"], C["OTT"]
    SKC, OKC, ICH, IFF = C["SKC"], C["OKC"], C["ICH"], C["IFF"]
    NQH, GJ, DJ, CBN = C["NQH"], C["GJ"], C["DJ"], C["CBN"]
    IBLOCKS = C["IBLOCKS"]
    NIB = len(IBLOCKS)
    KC_ALL = SKC + OKC
    SCALE = float(1.0 / np.sqrt(128.0))
    OCD = D // 512
    NQ2 = NQ // 2
    NROW = NQ // 512

    dq_calls = []
    for w, nj, nidx in (("g", GJ, IFF), ("u", GJ, IFF)):
        for j in range(nj):
            dq_calls.append((w, j, nidx, nidx // 128))
    for j in range(DJ):
        dq_calls.append(("d", j, D, D // 128))
    NDQ = len(dq_calls)
    GMAX = 1024
    dq_chunks = []
    for w, j, nidx, rawc in dq_calls:
        chs, r0 = [], 0
        while r0 < rawc:
            crawc = min(GMAX // 128, rawc - r0)
            chs.append((r0, crawc))
            r0 += crawc
        dq_chunks.append(chs)
    NGQ = 4
    DQR = 2
    call_qnums = []
    _gch = 0
    for ci in range(NDQ):
        qs = []
        for _ in dq_chunks[ci]:
            qs.append(_gch % NGQ)
            _gch += 1
        call_qnums.append(qs)
    _qr_cum = {}
    call_qwaits = []
    for ci in range(NDQ):
        r = ci % DQR
        per_q = {}
        for q in call_qnums[ci]:
            _qr_cum[(q, r)] = _qr_cum.get((q, r), 0) + 1
            per_q[q] = _qr_cum[(q, r)]
        call_qwaits.append([(q, r, c) for q, c in sorted(per_q.items())])
    gqr_tot = dict(_qr_cum)
    dq_groups = []
    for w, nj, rawc in (("g", GJ, IFF // 128), ("u", GJ, IFF // 128),
                        ("d", DJ, D // 128)):
        j0 = 0
        for gg in runs_of(nj, 16):
            dq_groups.append((w, j0, gg, rawc, j0 * 8))
            j0 += gg
    NGRP = len(dq_groups)
    grp_last_call = []
    ci0 = 0
    for w, j0, gg, rawc, rb in dq_groups:
        ci0 += gg
        grp_last_call.append(ci0)
    call_group = []
    for k, (w, j0, gg, rawc, rb) in enumerate(dq_groups):
        call_group += [k] * gg
    call_head = [min(NH - 1, (ci * NH) // NDQ) for ci in range(NDQ)]
    grp_head = [call_head[grp_last_call[k] - 1] for k in range(NGRP)]
    for _k in range(2, NGRP):
        assert call_head[grp_last_call[_k - 1]] > grp_head[_k - 2], \
            "dq schedule too tight for double-buffered d_asm"
    head_calls = {h: [ci for ci in range(NDQ) if call_head[ci] == h]
                  for h in range(NH)}
    head_groups = {h: [k for k in range(NGRP) if grp_head[k] == h]
                   for h in range(NH)}
    NQH5 = NQ // 512
    QT5 = 4
    att_sched = []
    att_pm = {}
    att_group_first = {}
    att_group_last = {}
    head_last_entry = {}
    for qh_ in range(NH):
        for h5_ in range(NQH5):
            g5 = qh_ * NQH5 + h5_
            for kc_ in range(KC_ALL):
                if kc_ < SKC:
                    cat = "rect"
                else:
                    m_ = (kc_ - SKC) - 4 * h5_
                    if m_ < 0:
                        cat = "full"
                    elif m_ <= 3:
                        cat = "part"
                    else:
                        continue
                ei = len(att_sched)
                if cat == "part":
                    att_pm[ei] = len(att_pm)
                if g5 not in att_group_first:
                    att_group_first[g5] = ei
                att_group_last[g5] = ei
                att_sched.append((qh_, h5_, kc_, cat))
        head_last_entry[qh_] = len(att_sched)
    att_entry = {}
    for ei, (qh_, h5_, kc_, cat) in enumerate(att_sched):
        att_entry[(qh_, h5_, kc_)] = ei
    exp_ops = []
    expop_after = {}
    for qh_ in range(NH):
        for h5_ in range(NQH5):
            ents5_ = [att_entry[(qh_, h5_, kc_)] for kc_ in range(KC_ALL)
                      if (qh_, h5_, kc_) in att_entry]
            z_ = 0
            while z_ < len(ents5_):
                ei_ = ents5_[z_]
                n_ = 2 if (ei_ % 2 == 0 and z_ + 1 < len(ents5_)) else 1
                exp_ops.append((ei_, n_))
                for e2 in range(ei_, ei_ + n_):
                    expop_after[e2] = len(exp_ops)
                z_ += n_

    ag_issue_after = {"g": 2 * GJ, "u": 2 * GJ + DJ // 2, "d": NDQ}
    ngrp_w = {w: len([1 for gw in dq_groups if gw[0] == w]) for w in ("g", "u", "d")}

    NSLOT = 2
    tp_order = []
    tp_index = {}

    def tp_add(key):
        tp_index[key] = len(tp_order)
        tp_order.append(key)

    for tt in range(OTT):
        for hc in range(HCH):
            tp_add(("xnT", tt, hc))
    for j in range(2 * KVH * OTT):
        tp_add(("kvp", j))
    for qh in range(NH):
        for qt in range(OTT):
            tp_add(("at", qh, qt))
        for k in head_groups[qh]:
            for ic in range(dq_groups[k][3]):
                tp_add(("dq", k, ic))
    for tt in range(OTT):
        for hc in range(HCH):
            tp_add(("x2T", tt, hc))
    ffn_runs = []
    for ib in range(NIB):
        icb = sum(IBLOCKS[:ib])
        c0 = 0
        for run_i, r in enumerate(runs_of(IBLOCKS[ib], 2)):
            ffn_runs.append((ib, run_i, icb + c0, c0, r))
            c0 += r
    gu_plan = []
    for ri, (ib, run_i, ic0, c0, r) in enumerate(ffn_runs):
        for tt in range(OTT):
            gu_plan.append((len(gu_plan), ib, ri, ic0, c0, r, tt))
    for (gidx, ib, ri, ic0, c0, r, tt) in gu_plan:
        for rc in range(r):
            tp_add(("hT", ib, c0 + rc, tt))

    def tpi(key):
        return tp_index[key]

    nc = bass.Bass("TRN2", target_bir_lowering=False, num_swdge_queues=NGQ)

    x_own = nc.dram_tensor("x_own", [OWN, D], F32, kind="ExternalInput").ap()
    wqkvT = nc.dram_tensor("wqkvT", [D, QKVD], F32, kind="ExternalInput").ap()
    wdenT = nc.dram_tensor("wdenT", [QD, D], F32, kind="ExternalInput").ap()
    cbp = nc.dram_tensor("cbp", [CBN // 2, 64], F32, kind="ExternalInput").ap()
    idx16 = {}
    idx128 = {}
    for w, nj, nidx in (("g", GJ, IFF), ("u", GJ, IFF), ("d", DJ, D)):
        idx16[w] = nc.dram_tensor("idx%s16" % w, [128, nj * nidx // 64], I32,
                                  kind="ExternalInput").ap()
        idx128[w] = nc.dram_tensor("idx%s128" % w, [128, nj * nidx // 128], I32,
                                   kind="ExternalInput").ap()
    TOT16 = (2 * GJ * IFF + DJ * D) // 16
    TOTL = (2 * GJ * IFF + DJ * D) // 128
    hi_dram = nc.dram_tensor("hi_dram", [128, TOT16], I16).ap()
    lo_dram = nc.dram_tensor("lo_dram", [128, TOTL], F32).ap()
    off16 = []
    offL = []
    o16 = oL = 0
    for w, j, nidx, rawc in dq_calls:
        off16.append(o16)
        offL.append(oL)
        o16 += nidx // 16
        oL += rawc
    PCH = 1024
    prep_chunks = []
    wo16 = 0
    for w, nj, nidx in (("g", GJ, IFF), ("u", GJ, IFF), ("d", DJ, D)):
        f4 = nj * nidx // 64
        c0 = 0
        for cn in runs_of(f4, PCH):
            prep_chunks.append(("hi", w, c0, cn, wo16))
            c0 += cn
        wo16 += nj * nidx // 16
    woL = 0
    for w, nj, nidx in (("g", GJ, IFF), ("u", GJ, IFF), ("d", DJ, D)):
        fl = nj * nidx // 128
        c0 = 0
        for cn in runs_of(fl, PCH):
            prep_chunks.append(("lo", w, c0, cn, woL))
            c0 += cn
        woL += fl
    if dbg_no_prep:
        prep_chunks = []
    NPREP = len(prep_chunks)
    pst_after = []
    acc = {0: 0, 1: 0}
    for k, (kind, w, c0, cn, doff) in enumerate(prep_chunks):
        acc[k % 2] += 256 if kind == "hi" else 16
        pst_after.append(dict(acc))
    validin = nc.dram_tensor("validin", [128, SKC], F32, kind="ExternalInput").ap()
    maskin = nc.dram_tensor("maskin", [128, 4, 512], BF16, kind="ExternalInput").ap()
    out = nc.dram_tensor("out", [OWN, D], F32, kind="ExternalOutput").ap()

    wq_bf = nc.dram_tensor("wq_bf", [D, QKVD], BF16).ap()
    wd_bf = nc.dram_tensor("wd_bf", [QD, D], BF16).ap()
    qT_dram = nc.dram_tensor("qT_dram", [QD, NQ], BF16).ap()
    k_own = nc.dram_tensor("k_own", [OWN, KVD], BF16).ap()
    v_own = nc.dram_tensor("v_own", [OWN, KVD], BF16).ap()
    k_all = nc.dram_tensor("k_all", [S, KVD], BF16).ap()
    v_all = nc.dram_tensor("v_all", [S, KVD], BF16).ap()
    attnT_dram = nc.dram_tensor("attnT_dram", [QD, NQ], BF16).ap()
    x1_dram = nc.dram_tensor("x1_dram", [OWN, D], BF16).ap()
    ag_in = {"g": nc.dram_tensor("g_in", [D // 8, IFF], BF16).ap(),
             "u": nc.dram_tensor("u_in", [D // 8, IFF], BF16).ap(),
             "d": nc.dram_tensor("d_in", [IFF // 8, D], BF16).ap()}
    ag_out = {"g": nc.dram_tensor("gT_all", [D, IFF], BF16).ap(),
              "u": nc.dram_tensor("uT_all", [D, IFF], BF16).ap(),
              "d": nc.dram_tensor("dT_all", [IFF, D], BF16).ap()}
    y2p = nc.dram_tensor("y2p", [NIB, OWN, D], BF16).ap()

    stack = contextlib.ExitStack()
    sem_tab = {}

    def sem(name):
        if name not in sem_tab:
            sem_tab[name] = stack.enter_context(nc.semaphore(name))
        return sem_tab[name]

    def sbuf(name, shape, dt):
        return stack.enter_context(nc.sbuf_tensor(name, shape, dt))

    with stack:
        psA = stack.enter_context(nc.psum_tensor("psA", [128, 1024], F32))
        psB = stack.enter_context(nc.psum_tensor("psB", [128, 1024], F32))
        psC = stack.enter_context(nc.psum_tensor("psC", [128, 1024], F32))
        psD2 = stack.enter_context(nc.psum_tensor("psD2", [128, 128], BF16))
        psD = stack.enter_context(nc.psum_tensor("psD", [128, 128], BF16))

        def psum_ab(par):
            return psA if par % 2 == 0 else psB

        def psd_slot(idx, rows=128):
            if idx % 2 == 0:
                return psD[0:rows, 0:128]
            return psD2[0:rows, 0:128]

        def psBC_slot256(t):
            assert t < 4
            if t < 2:
                return psB[:, t * 512:t * 512 + 512]
            return psC[:, (t - 2) * 512:(t - 2) * 512 + 512]

        def psBC_att(qt):
            assert qt < 4
            if qt < 2:
                return psB[:, qt * 512:qt * 512 + 129]
            return psC[:, (qt - 2) * 512:(qt - 2) * 512 + 129]

        ident = sbuf("ident", [128, 128], BF16)
        eps_t = sbuf("eps_t", [128, 1], F32)
        ebias = sbuf("ebias", [128, 1], F32)
        scAB = stack.enter_context(contextlib.ExitStack())
        sbAB = lambda n, sh, dt: scAB.enter_context(nc.sbuf_tensor(n, sh, dt))
        s_valid = sbAB("s_valid", [128, SKC], F32)
        masks = sbAB("masks", [128, 4, 512], BF16)
        kT_own = sbAB("kT_own", [128, KVH, OWN], BF16)
        vdiag = sbAB("vdiag", [128, KVH, OKC, 129], BF16)


        def dbg_terminal(level):
            with nc.Block() as blk:
                @blk.scalar
                def _(sc):
                    sc.wait_ge(sem("st_wqbf"), 16 * HCH)
                    sc.wait_ge(sem("st_wdbf"), 16 * QDCH)
                    sc.wait_ge(sem("st_q0"), 16 * ((QDCH + 1) // 2))
                    sc.wait_ge(sem("st_q1"), 16 * (QDCH // 2))
                    sc.wait_ge(sem("st_kv0"), 16 * KVH * OTT)
                    sc.wait_ge(sem("st_kv1"), 16 * KVH * OTT)
                    if NPREP:
                        sc.wait_ge(sem("g_pst0"), pst_after[NPREP - 1][0])
                        sc.wait_ge(sem("g_pst1"), pst_after[NPREP - 1][1])
                    sc.wait_ge(sem("ld_valid"), 32)
                    if level >= 3:
                        sc.wait_ge(sem("st_at0"), 16 * (NH // 2))
                        sc.wait_ge(sem("st_at1"), 16 * (NH // 2))
                        sc.wait_ge(sem("st_dq"), 16 * NGRP)
                        sc.wait_ge(sem("cc_w"), 3)
                        sc.wait_ge(sem("cc_kv"), 2)
                        for (qq, rr), tt2 in sorted(gqr_tot.items()):
                            sc.wait_ge(sem("g_gq%d_%d" % (qq, rr)), 16 * tt2)
                    if level >= 5:
                        sc.wait_ge(sem("st_x1"), 16 * OTT)
                    if level >= 6:
                        sc.wait_ge(sem("st_y20"), 16 * (NIB * OCD * OTT // 2))
                        sc.wait_ge(sem("st_y21"),
                                   16 * (NIB * OCD * OTT - NIB * OCD * OTT // 2))

        with nc.Block() as block:
            @block.gpsimd
            def _(g):
                g.load_library(LIB_MLP)
                g.memset(ident[:], 1.0)
                g.memset(eps_t[:], LN_EPS)
                g.memset(ebias[:], EXP_SHIFT)
                g.memset(vdiag[:], 1.0)
                g.drain()
                g.affine_select(ident[:], ident[:], [[-1, 128]], ALU.is_equal,
                                0.0, base=0, channel_multiplier=1)

            @block.sync
            def _(s):
                s.dma_start(out=s_valid[:], in_=validin[:]).then_inc(sem("ld_valid"), 16)
                s.dma_start(out=masks[:], in_=maskin[:]).then_inc(sem("ld_valid"), 16)

        with contextlib.ExitStack() as scA:
            sbA = lambda n, sh, dt: scA.enter_context(nc.sbuf_tensor(n, sh, dt))
            x_sb = sbA("x_sb", [128, 2, D], F32)
            stats = sbA("stats", [128, D // 512, 6], F32)
            mv = sbA("mv", [128, 2, 2], F32)
            rstd = sbA("rstd", [128, 2, 1], F32)
            negm = sbA("negm", [128, 2, 1], F32)
            xn_sb = sbA("xn_sb", [128, 2, D], BF16)
            xnT_sb = sbA("xnT_sb", [128, HCH, NQ], BF16)
            wqs = sbA("wqs", [128, 2, HCH, 128], BF16)
            qkv_ev = sbA("qkv_ev", [128, 2, NQ], BF16)
            vT_tmp = sbA("vT_tmp", [128, KVH, OWN], BF16)
            kvp = sbA("kvp", [128, 2, 128], BF16)
            pin32 = sbA("pin32", [128, 2, 1024], I32)
            pout16 = sbA("pout16", [128, 2, 1024], I16)
            poutf = sbA("poutf", [128, 2, 1024], F32)

            with nc.Block() as block:
                @block.gpsimd
                def _(g):
                    for hc in range(HCH):
                        g.dma_start(out=wq_bf[hc * 128:(hc + 1) * 128, :],
                                    in_=wqkvT[hc * 128:(hc + 1) * 128, :]
                                    ).then_inc(sem("st_wqbf"), 16)
                    for hc in range(QDCH):
                        g.dma_start(out=wd_bf[hc * 128:(hc + 1) * 128, :],
                                    in_=wdenT[hc * 128:(hc + 1) * 128, :]
                                    ).then_inc(sem("st_wdbf"), 16)
                    for k, (kind, w, c0, cn, doff) in enumerate(prep_chunks):
                        if k >= 2:
                            g.wait_ge(sem("v_prep"), k - 1)
                        src = idx16[w] if kind == "hi" else idx128[w]
                        g.dma_start(out=pin32[:, k % 2, 0:cn],
                                    in_=src[:, c0:c0 + cn]).then_inc(sem("g_pld"), 16)
                        g.wait_ge(sem("v_prep"), k + 1)
                        if kind == "hi":
                            f4w = {"g": GJ * IFF // 64, "u": GJ * IFF // 64,
                                   "d": DJ * D // 64}[w]
                            for i4 in range(4):
                                for b4 in range(4):
                                    g.dma_start(
                                        out=hi_dram[b4 * 32:(b4 + 1) * 32,
                                                    doff + i4 * f4w + c0:
                                                    doff + i4 * f4w + c0 + cn],
                                        in_=pout16[i4 * 32:(i4 + 1) * 32, k % 2, 0:cn]
                                    ).then_inc(sem("g_pst%d" % (k % 2)), 16)
                        else:
                            g.dma_start(out=lo_dram[:, doff + c0:doff + c0 + cn],
                                        in_=poutf[:, k % 2, 0:cn]
                                        ).then_inc(sem("g_pst%d" % (k % 2)), 16)

                @block.sync
                def _(s):
                    for tt in range(OTT):
                        if tt >= 2:
                            s.wait_ge(sem("v_xn"), tt - 1)
                        s.dma_start(out=x_sb[:, tt % 2, :],
                                    in_=x_own[tt * 128:(tt + 1) * 128, :]
                                    ).then_inc(sem("ld_x%d" % (tt % 2)), 16)
                    s.wait_ge(sem("st_wqbf"), 16 * HCH)
                    for oc in range(QKVD // 128):
                        if oc >= 2:
                            s.wait_ge(sem("t_qkv"), oc - 1)
                        s.dma_start(out=wqs[:, oc % 2, :, :],
                                    in_=wq_bf[:, oc * 128:(oc + 1) * 128].rearrange(
                                        "(hc p) o -> p hc o", p=128)
                                    ).then_inc(sem("ld_wq%d" % (oc % 2)), 16)

                @block.scalar
                def _(sc):
                    for tt in range(OTT):
                        sc.wait_ge(sem("v_mv"), tt + 1)
                        sc.activation(rstd[:, tt % 2, :], mv[:, tt % 2, 1:2],
                                      ACTF.Sqrt, bias=eps_t[:]
                                      ).then_inc(sem("a_rstd"), 1)
                    for oc in range(QDCH):
                        sc.wait_ge(sem("v_qkve"), oc + 1)
                        sc.dma_start(out=qT_dram[oc * 128:(oc + 1) * 128, :],
                                     in_=qkv_ev[:, oc % 2, :]
                                     ).then_inc(sem("st_q%d" % (oc % 2)), 16)
                    for j in range(2 * KVH * OTT):
                        sc.wait_ge(sem("v_pse"), tpi(("kvp", j)) + 1)
                        kvoc, tt = j // OTT, j % OTT
                        isv = kvoc >= KVH
                        hd = kvoc - KVH if isv else kvoc
                        tgt = v_own if isv else k_own
                        sc.dma_start(out=tgt[tt * 128:(tt + 1) * 128,
                                             hd * 128:(hd + 1) * 128],
                                     in_=kvp[:, j % 2, :]
                                     ).then_inc(sem("st_kv%d" % (j % 2)), 16)

                @block.vector
                def _(v):
                    def evac_xnT(tt):
                        for hc in range(HCH):
                            idx = tpi(("xnT", tt, hc))
                            v.wait_ge(sem("t_ps"), idx + 1)
                            v.tensor_copy(xnT_sb[:, hc, tt * 128:(tt + 1) * 128],
                                          psd_slot(idx)).then_inc(sem("v_pse"), 1)
                    for tt in range(OTT):
                        v.wait_ge(sem("ld_x%d" % (tt % 2)), 16 * (tt // 2 + 1))
                        xt = x_sb[:, tt % 2, :]
                        for cc in range(D // 512):
                            v.bn_stats(stats[:, cc, :], xt[:, cc * 512:(cc + 1) * 512])
                        v.drain()
                        v.bn_aggr(mv[:, tt % 2, :],
                                  stats.ap().rearrange("p a b -> p (a b)"))
                        v.drain()
                        v.tensor_scalar(negm[:, tt % 2, :], mv[:, tt % 2, 0:1],
                                        -1.0, None, ALU.mult).then_inc(sem("v_mv"), 1)
                        v.wait_ge(sem("a_rstd"), tt + 1)
                        v.reciprocal(rstd[:, tt % 2, :], rstd[:, tt % 2, :])
                        v.drain()
                        v.tensor_scalar(xn_sb[:, tt % 2, :], xt,
                                        negm[:, tt % 2, :], rstd[:, tt % 2, :],
                                        ALU.add, ALU.mult).then_inc(sem("v_xn"), 1)
                        if tt >= 1:
                            evac_xnT(tt - 1)
                    evac_xnT(OTT - 1)
                    prep_cur = [0]

                    def emit_prep():
                        if prep_cur[0] >= NPREP:
                            return
                        k = prep_cur[0]
                        prep_cur[0] += 1
                        kind, w, c0, cn, doff = prep_chunks[k]
                        v.wait_ge(sem("g_pld"), 16 * (k + 1))
                        if k >= 2:
                            v.wait_ge(sem("g_pst%d" % (k % 2)),
                                      pst_after[k - 2][k % 2])
                        if kind == "hi":
                            v.tensor_single_scalar(pin32[:, k % 2, 0:cn],
                                                   pin32[:, k % 2, 0:cn],
                                                   1, ALU.logical_shift_right)
                            v.drain()
                            v.tensor_copy(pout16[:, k % 2, 0:cn],
                                          pin32[:, k % 2, 0:cn]
                                          ).then_inc(sem("v_prep"), 1)
                        else:
                            v.tensor_single_scalar(pin32[:, k % 2, 0:cn],
                                                   pin32[:, k % 2, 0:cn],
                                                   1, ALU.bitwise_and)
                            v.drain()
                            v.tensor_copy(poutf[:, k % 2, 0:cn],
                                          pin32[:, k % 2, 0:cn]
                                          ).then_inc(sem("v_prep"), 1)
                        v.drain()

                    for oc in range(QKVD // 128):
                        emit_prep()
                        v.wait_ge(sem("t_qkv"), oc + 1)
                        src = psum_ab(oc)[:, 0:NQ]
                        if oc < QDCH:
                            if oc >= 2:
                                v.wait_ge(sem("st_q%d" % (oc % 2)), 16 * (oc // 2))
                            v.tensor_copy(qkv_ev[:, oc % 2, :], src
                                          ).then_inc(sem("v_qkve"), 1)
                        elif oc < QDCH + KVH:
                            v.tensor_copy(kT_own[:, oc - QDCH, :], src
                                          ).then_inc(sem("v_qkve"), 1)
                        else:
                            v.tensor_copy(vT_tmp[:, oc - QDCH - KVH, :], src
                                          ).then_inc(sem("v_qkve"), 1)
                    for j in range(2 * KVH * OTT):
                        idx = tpi(("kvp", j))
                        v.wait_ge(sem("t_ps"), idx + 1)
                        if j >= 2:
                            v.wait_ge(sem("st_kv%d" % (j % 2)), 16 * (j // 2))
                        v.tensor_copy(kvp[:, j % 2, :], psd_slot(idx)
                                      ).then_inc(sem("v_pse"), 1)
                    while prep_cur[0] < NPREP:
                        emit_prep()

                @block.tensor
                def _(t):
                    for tt in range(OTT):
                        t.wait_ge(sem("v_xn"), tt + 1)
                        for hc in range(HCH):
                            idx = tpi(("xnT", tt, hc))
                            if idx >= NSLOT:
                                t.wait_ge(sem("v_pse"), idx - NSLOT + 1)
                            t.transpose(psd_slot(idx),
                                        xn_sb[:, tt % 2, hc * 128:(hc + 1) * 128],
                                        ident[:]).then_inc(sem("t_ps"), 1)
                    t.wait_ge(sem("v_pse"), OTT * HCH)
                    for oc in range(QKVD // 128):
                        t.wait_ge(sem("ld_wq%d" % (oc % 2)), 16 * (oc // 2 + 1))
                        if oc >= 2:
                            t.wait_ge(sem("v_qkve"), oc - 1)
                        pdst = psum_ab(oc)[:, 0:NQ]
                        i = None
                        for hc in range(HCH):
                            for h5 in range(NQH):
                                i = t.matmul(pdst[:, h5 * 512:(h5 + 1) * 512],
                                             wqs[:, oc % 2, hc, :],
                                             xnT_sb[:, hc, h5 * 512:(h5 + 1) * 512],
                                             start=(hc == 0), stop=(hc == HCH - 1))
                        i.then_inc(sem("t_qkv"), 1)
                    for j in range(2 * KVH * OTT):
                        kvoc, tt = j // OTT, j % OTT
                        src = (kT_own[:, kvoc, tt * 128:(tt + 1) * 128]
                               if kvoc < KVH else
                               vT_tmp[:, kvoc - KVH, tt * 128:(tt + 1) * 128])
                        t.wait_ge(sem("v_qkve"), QDCH + kvoc + 1)
                        idx = tpi(("kvp", j))
                        t.wait_ge(sem("v_pse"), idx - NSLOT + 1)
                        t.transpose(psd_slot(idx), src, ident[:]).then_inc(sem("t_ps"), 1)

        if phases < 3:
            dbg_terminal(2)
            mybir.codegen_inst_isa_subclasses(nc)
            return nc
        with contextlib.ExitStack() as scB:
            sbB = lambda n, sh, dt: scB.enter_context(nc.sbuf_tensor(n, sh, dt))
            ktb = sbB("ktb", [128, S], BF16)
            vbuf = sbB("vbuf", [128, SKC, 129], BF16)
            vrect = sbB("vrect", [128, SKC, 129], BF16)
            qtb = sbB("qtb", [128, 2, NQ], BF16)
            pT = sbB("pT", [128, 4, 512], BF16)
            rec = sbB("rec", [128, 2, 1], F32)
            at_ev = sbB("at_ev", [128, 2, 128], BF16)
            atT_row = sbB("atT_row", [128, 2, NQ], BF16)
            RAWC = max(ICH, HCH)
            N16MAX = max(IFF, D) // 16
            d_raw = sbB("d_raw", [128, DQR, RAWC, 64], F32)
            d_hi = sbB("d_hi", [128, DQR, N16MAX], I16)
            d_lo = sbB("d_lo", [128, DQR, RAWC], F32)
            d_dif = sbB("d_dif", [128, RAWC, 8], F32)
            d_asm = sbB("d_asm", [128, 2, RAWC, 16, 8], BF16)
            d_row = sbB("d_row", [128, max(IFF, D)], BF16)

            with nc.Block() as block:
                @block.gpsimd
                def _(g):
                    nidx_regs = {}
                    for chs in dq_chunks:
                        for r0x, crawcx in chs:
                            if crawcx * 128 not in nidx_regs:
                                nidx_regs[crawcx * 128] = g.to_reg(crawcx * 128)
                    g.wait_ge(sem("st_q0"), 16 * ((QDCH + 1) // 2))
                    g.wait_ge(sem("st_q1"), 16 * (QDCH // 2))
                    g.wait_ge(sem("st_kv0"), 16 * KVH * OTT)
                    g.wait_ge(sem("st_kv1"), 16 * KVH * OTT)
                    g.memset(d_hi[:], 0)
                    g.drain()
                    g.memset(vbuf[:, :, 128:129], 1.0).then_inc(sem("g_vones"), 1)
                    g.collective_compute(
                        "AllGather", ALU.bypass,
                        replica_groups=[[0, 1, 2, 3], [4, 5, 6, 7]],
                        ins=[k_own.opt()], outs=[k_all.opt()]).then_inc(sem("cc_kv"), 1)
                    g.collective_compute(
                        "AllGather", ALU.bypass,
                        replica_groups=[[0, 1, 2, 3], [4, 5, 6, 7]],
                        ins=[v_own.opt()], outs=[v_all.opt()]).then_inc(sem("cc_kv"), 1)
                    if NPREP:
                        g.wait_ge(sem("v_prep"), NPREP)
                        g.wait_ge(sem("g_pst0"), pst_after[NPREP - 1][0])
                        g.wait_ge(sem("g_pst1"), pst_after[NPREP - 1][1])
                    stc = 0
                    gch = 0
                    for ci, (w, j, nidx, rawc) in enumerate(dq_calls):
                        par = ci % DQR
                        n16 = nidx // 16
                        if ci >= DQR:
                            g.wait_ge(sem("g_sel"), ci - DQR + 1)
                        grp = call_group[ci]
                        if grp >= 2 and call_group[ci - 1] != grp:
                            pg = grp - 2
                            g.wait_ge(sem("t_ps"),
                                      tpi(("dq", pg, dq_groups[pg][3] - 1)) + 1)
                        g.dma_start(out=d_hi[0:128, par, 0:n16],
                                    in_=hi_dram[:, off16[ci]:off16[ci] + n16]
                                    ).then_inc(sem("g_ld"), 16)
                        g.dma_start(out=d_lo[:, par, 0:rawc],
                                    in_=lo_dram[:, offL[ci]:offL[ci] + rawc]
                                    ).then_inc(sem("g_ld"), 16)
                        g.wait_ge(sem("g_ld"), 32 * (ci + 1))
                        for (r0, crawc), qn in zip(dq_chunks[ci], call_qnums[ci]):
                            cn = crawc * 128
                            g.dma_gather(d_raw[:, par, r0:r0 + crawc, :], cbp[:],
                                         d_hi[:, par, r0 * 8:r0 * 8 + cn // 16],
                                         num_idxs=cn,
                                         num_idxs_reg=nidx_regs[cn],
                                         elem_size=64, elem_step=64,
                                         queue_num=qn
                                         ).then_inc(
                                sem("g_gq%d_%d" % (qn, ci % DQR)), 16)
                        for w2 in ("g", "u", "d"):
                            if ag_issue_after[w2] == ci + 1:
                                stc2 = 16 * sum(ngrp_w[w3] for w3 in ("g", "u", "d")
                                                if ("gud".index(w3) <= "gud".index(w2)))
                                g.wait_ge(sem("st_dq"), stc2)
                                g.collective_compute(
                                    "AllGather", ALU.bypass,
                                    replica_groups=[list(range(8))],
                                    ins=[ag_in[w2].opt()], outs=[ag_out[w2].opt()]
                                ).then_inc(sem("cc_w"), 1)

                @block.sync
                def _(s):
                    s.wait_ge(sem("cc_kv"), 2)
                    for kv in range(KVH):
                        s.dma_start(out=vdiag[:, kv, :, 0:128],
                                    in_=v_own[:, kv * 128:(kv + 1) * 128].rearrange(
                                        "(c p) d -> p c d", p=128)
                                    ).then_inc(sem("ld_vd"), 16)
                    for kvh in range(KVH):
                        if kvh >= 1:
                            s.wait_ge(sem("t_av"), head_last_entry[kvh * G - 1])
                        s.dma_start_transpose(ktb[:],
                                              k_all[:, kvh * 128:(kvh + 1) * 128]
                                              ).then_inc(sem("ld_kt"), 16)
                        if kvh >= 1:
                            s.wait_ge(sem("v_vr"), kvh)
                        s.dma_start(out=vbuf[:, :, 0:128],
                                    in_=v_all[:, kvh * 128:(kvh + 1) * 128].rearrange(
                                        "(c p) d -> p c d", p=128)
                                    ).then_inc(sem("ld_v"), 16)
                        for gq in range(G):
                            qh = kvh * G + gq
                            if qh >= 2:
                                s.wait_ge(sem("t_av"), head_last_entry[qh - 2])
                            s.dma_start(out=qtb[:, qh % 2, :],
                                        in_=qT_dram[qh * 128:(qh + 1) * 128, :]
                                        ).then_inc(sem("ld_qt%d" % (qh % 2)), 16)

                @block.vector
                def _(v):
                    v.wait_ge(sem("ld_valid"), 32)
                    v.wait_ge(sem("g_vones"), 1)
                    for kvh in range(KVH):
                        v.wait_ge(sem("ld_v"), 16 * (kvh + 1))
                        if kvh >= 1:
                            v.wait_ge(sem("t_av"), head_last_entry[kvh * G - 1])
                        vb_ = s_valid.ap()[:, :, None].broadcast_to([128, SKC, 129])
                        v.tensor_tensor(vrect[:], vbuf[:], vb_, ALU.mult
                                        ).then_inc(sem("v_vr"), 1)
                        for gq in range(G):
                            qh = kvh * G + gq
                            for ci in head_calls[qh]:
                                w, j, nidx, rawc = dq_calls[ci]
                                par = ci % DQR
                                grp = call_group[ci]
                                if grp >= 2 and (ci == 0 or call_group[ci - 1] != grp):
                                    pg = grp - 2
                                    v.wait_ge(sem("t_ps"),
                                              tpi(("dq", pg, dq_groups[pg][3] - 1)) + 1)
                                for qq, rr, cc in call_qwaits[ci]:
                                    v.wait_ge(sem("g_gq%d_%d" % (qq, rr)),
                                              16 * cc)
                                lob = d_lo.ap()[:, par, 0:rawc, None].broadcast_to(
                                    [128, rawc, 8])
                                v.tensor_tensor(d_dif[:, 0:rawc, :],
                                                d_raw[:, par, 0:rawc, 8:16],
                                                d_raw[:, par, 0:rawc, 0:8],
                                                ALU.subtract)
                                v.drain()
                                v.tensor_tensor(d_dif[:, 0:rawc, :],
                                                d_dif[:, 0:rawc, :], lob, ALU.mult)
                                v.drain()
                                v.tensor_tensor(d_asm[:, grp % 2, 0:rawc, j % 16, :],
                                                d_dif[:, 0:rawc, :],
                                                d_raw[:, par, 0:rawc, 0:8],
                                                ALU.add).then_inc(sem("g_sel"), 1)
                                v.drain()
                            for h5 in range(NQH5):
                                g5 = qh * NQH5 + h5
                                for qt in range(QT5):
                                    qtg = h5 * QT5 + qt
                                    v.wait_ge(sem("t_av"), att_group_last[g5] + 1)
                                    slot = psBC_att(qt)
                                    v.reciprocal(rec[:, qt % 2, :], slot[:, 128:129])
                                    v.drain()
                                    pidx = qh * OTT + qtg
                                    if pidx >= 2:
                                        prev = tpi(("at", (pidx - 2) // OTT,
                                                    (pidx - 2) % OTT))
                                        v.wait_ge(sem("t_ps"), prev + 1)
                                    v.tensor_scalar(at_ev[:, qtg % 2, :],
                                                    slot[:, 0:128],
                                                    rec[:, qt % 2, :], None, ALU.mult
                                                    ).then_inc(sem("v_at"), 1)
                                for qt in range(QT5):
                                    qtg = h5 * QT5 + qt
                                    idx = tpi(("at", qh, qtg))
                                    v.wait_ge(sem("t_ps"), idx + 1)
                                    if h5 == 0 and qt == 0 and qh >= 2:
                                        v.wait_ge(sem("st_at%d" % (qh % 2)),
                                                  16 * (qh // 2))
                                    v.tensor_copy(
                                        atT_row[:, qh % 2, qtg * 128:(qtg + 1) * 128],
                                        psd_slot(idx)).then_inc(sem("v_pse"), 1)
                            for k in head_groups[qh]:
                                wg, j0g, gg, rawcg, rbg = dq_groups[k]
                                rows = gg * 8
                                for ic in range(rawcg):
                                    idx = tpi(("dq", k, ic))
                                    v.wait_ge(sem("t_ps"), idx + 1)
                                    if ic == 0 and k >= 1:
                                        v.wait_ge(sem("st_dq"), 16 * k)
                                    v.tensor_copy(d_row[0:rows, ic * 128:(ic + 1) * 128],
                                                  psd_slot(idx, rows)
                                                  ).then_inc(sem("v_pse"), 1)

                @block.scalar
                def _(sc):
                    for kvh in range(KVH):
                        for gq in range(G):
                            qh = kvh * G + gq
                            for h5 in range(NQH5):
                                ents5 = [att_entry[(qh, h5, kc)]
                                         for kc in range(KC_ALL)
                                         if (qh, h5, kc) in att_entry]
                                z = 0
                                while z < len(ents5):
                                    ei = ents5[z]
                                    if ei % 2 == 0 and z + 1 < len(ents5):
                                        sc.wait_ge(sem("t_s"), ei + 2)
                                        if ei >= 3:
                                            sc.wait_ge(sem("t_av"), ei - 2)
                                        sc.activation(pT[:, ei % 4:ei % 4 + 2, :],
                                                      psA[:, 0:1024],
                                                      ACTF.Exp, bias=ebias[:],
                                                      scale=SCALE
                                                      ).then_inc(sem("a_exp"), 1)
                                        z += 2
                                        continue
                                    sc.wait_ge(sem("t_s"), ei + 1)
                                    if ei >= 4:
                                        sc.wait_ge(sem("t_av"), ei - 3)
                                    sc.activation(pT[:, ei % 4, :],
                                                  psA[:, (ei % 2) * 512:
                                                      (ei % 2) * 512 + 512],
                                                  ACTF.Exp, bias=ebias[:],
                                                  scale=SCALE).then_inc(sem("a_exp"), 1)
                                    z += 1
                            sc.wait_ge(sem("v_pse"), tpi(("at", qh, OTT - 1)) + 1)
                            sc.dma_start(out=attnT_dram[qh * 128:(qh + 1) * 128, :],
                                         in_=atT_row[:, qh % 2, :]
                                         ).then_inc(sem("st_at%d" % (qh % 2)), 16)
                            for k in head_groups[qh]:
                                wg, j0g, gg, rawcg, rbg = dq_groups[k]
                                rows = gg * 8
                                sc.wait_ge(sem("v_pse"), tpi(("dq", k, rawcg - 1)) + 1)
                                sc.dma_start(out=ag_in[wg][rbg:rbg + rows, :],
                                             in_=d_row[0:rows, 0:rawcg * 128]
                                             ).then_inc(sem("st_dq"), 16)

                @block.tensor
                def _(t):
                    t.wait_ge(sem("ld_vd"), 16 * KVH)
                    for kvh in range(KVH):
                        t.wait_ge(sem("ld_kt"), 16 * (kvh + 1))
                        for gq in range(G):
                            qh = kvh * G + gq
                            t.wait_ge(sem("ld_qt%d" % (qh % 2)), 16 * (qh // 2 + 1))
                            if gq == 0:
                                t.wait_ge(sem("v_vr"), kvh + 1)
                            for h5 in range(NQH5):
                                g5 = qh * NQH5 + h5
                                ents = [(att_entry[(qh, h5, kc)], kc)
                                        for kc in range(KC_ALL)
                                        if (qh, h5, kc) in att_entry]

                                def qk(ei, kc):
                                    if ei >= 2:
                                        t.wait_ge(sem("a_exp"),
                                                  expop_after[ei - 2])
                                    pdst = psA[:, (ei % 2) * 512:(ei % 2) * 512 + 512]
                                    lhs = (ktb[:, kc * 128:(kc + 1) * 128]
                                           if kc < SKC else
                                           kT_own[:, kvh, (kc - SKC) * 128:
                                                  (kc - SKC + 1) * 128])
                                    part = att_sched[ei][3] == "part"
                                    i = t.matmul(pdst, lhs,
                                                 qtb[:, qh % 2, h5 * 512:(h5 + 1) * 512],
                                                 start=True, stop=not part)
                                    if part:
                                        mloc = (kc - SKC) - 4 * h5
                                        i = t.matmul(pdst, ident[:],
                                                     masks[:, mloc, :],
                                                     start=False, stop=True)
                                    i.then_inc(sem("t_s"), 1)

                                def av(ei, kc):
                                    t.wait_ge(sem("a_exp"), expop_after[ei])
                                    if ei == att_group_first[g5]:
                                        t.wait_ge(sem("v_at"), g5 * QT5)
                                    rhs_t = (vrect[:, kc, :] if kc < SKC
                                             else vdiag[:, kvh, kc - SKC, :])
                                    i = None
                                    for qt in range(QT5):
                                        i = t.matmul(psBC_att(qt),
                                                     pT[:, ei % 4,
                                                        qt * 128:(qt + 1) * 128],
                                                     rhs_t,
                                                     start=(ei == att_group_first[g5]),
                                                     stop=(ei == att_group_last[g5]))
                                    i.then_inc(sem("t_av"), 1)

                                qk(*ents[0])
                                for z in range(1, len(ents)):
                                    qk(*ents[z])
                                    av(*ents[z - 1])
                                av(*ents[-1])
                                for qt in range(QT5):
                                    qtg = h5 * QT5 + qt
                                    t.wait_ge(sem("v_at"), qh * OTT + qtg + 1)
                                    idx = tpi(("at", qh, qtg))
                                    t.wait_ge(sem("v_pse"), idx - NSLOT + 1)
                                    t.transpose(psd_slot(idx), at_ev[:, qtg % 2, :],
                                                ident[:]).then_inc(sem("t_ps"), 1)
                            for k in head_groups[qh]:
                                wg, j0g, gg, rawcg, rbg = dq_groups[k]
                                rows = gg * 8
                                t.wait_ge(sem("g_sel"), grp_last_call[k])
                                for ic in range(rawcg):
                                    idx = tpi(("dq", k, ic))
                                    t.wait_ge(sem("v_pse"), idx - NSLOT + 1)
                                    t.transpose(
                                        psd_slot(idx, rows),
                                        d_asm.ap()[:, k % 2, ic, 0:gg, :].rearrange(
                                            "p a b -> p (a b)"),
                                        ident[:]).then_inc(sem("t_ps"), 1)

        scAB.close()
        if phases < 5:
            dbg_terminal(3)
            mybir.codegen_inst_isa_subclasses(nc)
            return nc

        OC1 = D // 128
        NPASS = OTT // 2
        with contextlib.ExitStack() as scC2:
            sbC2 = lambda n, sh, dt: scC2.enter_context(nc.sbuf_tensor(n, sh, dt))
            xn2T_sb = sbC2("xn2T_sb", [128, HCH, NQ], BF16)

            with contextlib.ExitStack() as scC:
                sbC = lambda n, sh, dt: scC.enter_context(nc.sbuf_tensor(n, sh, dt))
                atT_p = sbC("atT_p", [128, QDCH, 256], BF16)
                wds = sbC("wds", [128, 2, QDCH, 128], BF16)
                y_sb = sbC("y_sb", [128, 2, D], BF16)
                lnbuf = sbC("lnbuf", [128, D], F32)
                xf_buf = sbC("xf_buf", [128, D], F32)
                x2tile = sbC("x2tile", [128, D], F32)
                xn2_sb = sbC("xn2_sb", [128, D], BF16)
                x1b = sbC("x1b", [128, D], BF16)
                stats2 = sbC("stats2", [128, D // 512, 6], F32)
                mv2 = sbC("mv2", [128, 2], F32)
                rstd2 = sbC("rstd2", [128, 1], F32)
                negm2 = sbC("negm2", [128, 1], F32)

                with nc.Block() as block:
                    @block.sync
                    def _(s):
                        s.wait_ge(sem("st_wdbf"), 16 * QDCH)
                        s.wait_ge(sem("st_dq"), 16 * NGRP)
                        s.wait_ge(sem("st_at0"), 16 * (NH // 2))
                        s.wait_ge(sem("st_at1"), 16 * (NH // 2))
                        for ps in range(NPASS):
                            if ps >= 1:
                                s.wait_ge(sem("t_den"), ps * OC1 * 2)
                            s.dma_start(out=atT_p[:],
                                        in_=attnT_dram[:, ps * 256:(ps + 1) * 256]
                                        .rearrange("(fc p) q -> p fc q", p=128)
                                        ).then_inc(sem("ld_ath"), 16)
                            for oc in range(OC1):
                                gidx = ps * OC1 + oc
                                if gidx >= 2:
                                    s.wait_ge(sem("t_den"), (gidx - 1) * 2)
                                s.dma_start(out=wds[:, gidx % 2, :, :],
                                            in_=wd_bf[:, oc * 128:(oc + 1) * 128]
                                            .rearrange("(hc p) o -> p hc o", p=128)
                                            ).then_inc(sem("ld_wd%d" % (gidx % 2)), 16)
                            for t2 in range(2):
                                row = ps * 2 + t2
                                if row >= 1:
                                    s.wait_ge(sem("v_x1"), row)
                                s.dma_start(out=x2tile[:],
                                            in_=x_own[row * 128:(row + 1) * 128, :]
                                            ).then_inc(sem("ld_x2"), 16)

                    @block.tensor
                    def _(t):
                        for ps in range(NPASS):
                            t.wait_ge(sem("ld_ath"), 16 * (ps + 1))
                            for oc in range(OC1):
                                gidx = ps * OC1 + oc
                                t.wait_ge(sem("ld_wd%d" % (gidx % 2)),
                                          16 * (gidx // 2 + 1))
                                for t2 in range(2):
                                    didx = gidx * 2 + t2
                                    if didx >= 4:
                                        t.wait_ge(sem("v_den"), didx - 3)
                                    sl = didx % 4
                                    pb = psA if sl < 2 else psB
                                    pdst = pb[:, (sl % 2) * 512:(sl % 2) * 512 + 128]
                                    i = None
                                    for fc in range(QDCH):
                                        i = t.matmul(pdst,
                                                     atT_p[:, fc, t2 * 128:(t2 + 1) * 128],
                                                     wds[:, gidx % 2, fc, :],
                                                     start=(fc == 0),
                                                     stop=(fc == QDCH - 1))
                                    i.then_inc(sem("t_den"), 1)
                            for t2 in range(2):
                                row = ps * 2 + t2
                                t.wait_ge(sem("v_xn2"), row + 1)
                                for hc in range(HCH):
                                    idx = tpi(("x2T", row, hc))
                                    t.wait_ge(sem("v_pse"), idx - NSLOT + 1)
                                    t.transpose(psd_slot(idx),
                                                xn2_sb[:, hc * 128:(hc + 1) * 128],
                                                ident[:]).then_inc(sem("t_ps"), 1)

                    @block.vector
                    def _(v):
                        for ps in range(NPASS):
                            for oc in range(OC1):
                                gidx = ps * OC1 + oc
                                for t2 in range(2):
                                    didx = gidx * 2 + t2
                                    v.wait_ge(sem("t_den"), didx + 1)
                                    sl = didx % 4
                                    pb = psA if sl < 2 else psB
                                    v.tensor_copy(
                                        y_sb[:, t2, oc * 128:(oc + 1) * 128],
                                        pb[:, (sl % 2) * 512:(sl % 2) * 512 + 128]
                                    ).then_inc(sem("v_den"), 1)
                            v.drain()
                            for t2 in range(2):
                                row = ps * 2 + t2
                                for cc in range(D // 512):
                                    v.bn_stats(stats2[:, cc, :],
                                               y_sb[:, t2, cc * 512:(cc + 1) * 512])
                                v.drain()
                                v.bn_aggr(mv2[:],
                                          stats2.ap().rearrange("p a b -> p (a b)"))
                                v.drain()
                                v.tensor_scalar(negm2[:], mv2[:, 0:1], -1.0, None,
                                                ALU.mult).then_inc(sem("v_mv2"), 1)
                                v.wait_ge(sem("a_rstd2"), 2 * row + 1)
                                v.reciprocal(rstd2[:], rstd2[:])
                                v.drain()
                                v.tensor_scalar(lnbuf[:], y_sb[:, t2, :], negm2[:],
                                                rstd2[:], ALU.add, ALU.mult)
                                v.drain()
                                v.wait_ge(sem("ld_x2"), 16 * (row + 1))
                                v.tensor_tensor(xf_buf[:], x2tile[:],
                                                lnbuf[:], ALU.add)
                                v.drain()
                                if row >= 1:
                                    v.wait_ge(sem("st_x1"), 16 * row)
                                v.tensor_copy(x1b[:], xf_buf[:]
                                              ).then_inc(sem("v_x1"), 1)
                                for cc in range(D // 512):
                                    v.bn_stats(stats2[:, cc, :],
                                               xf_buf[:, cc * 512:(cc + 1) * 512])
                                v.drain()
                                v.bn_aggr(mv2[:],
                                          stats2.ap().rearrange("p a b -> p (a b)"))
                                v.drain()
                                v.tensor_scalar(negm2[:], mv2[:, 0:1], -1.0, None,
                                                ALU.mult).then_inc(sem("v_mv2"), 1)
                                v.wait_ge(sem("a_rstd2"), 2 * row + 2)
                                v.reciprocal(rstd2[:], rstd2[:])
                                v.drain()
                                v.tensor_scalar(xn2_sb[:], xf_buf[:], negm2[:],
                                                rstd2[:], ALU.add, ALU.mult
                                                ).then_inc(sem("v_xn2"), 1)
                                for hc in range(HCH):
                                    idx = tpi(("x2T", row, hc))
                                    v.wait_ge(sem("t_ps"), idx + 1)
                                    v.tensor_copy(
                                        xn2T_sb[:, hc, row * 128:(row + 1) * 128],
                                        psd_slot(idx)).then_inc(sem("v_pse"), 1)

                    @block.scalar
                    def _(sc):
                        for row in range(OTT):
                            for half in range(2):
                                sc.wait_ge(sem("v_mv2"), 2 * row + half + 1)
                                sc.activation(rstd2[:], mv2[:, 1:2], ACTF.Sqrt,
                                              bias=eps_t[:]).then_inc(sem("a_rstd2"), 1)
                            sc.wait_ge(sem("v_x1"), row + 1)
                            sc.dma_start(out=x1_dram[row * 128:(row + 1) * 128, :],
                                         in_=x1b[:]).then_inc(sem("st_x1"), 16)

            if phases < 6:
                dbg_terminal(5)
                mybir.codegen_inst_isa_subclasses(nc)
                return nc
            with contextlib.ExitStack() as scD:
                sbD = lambda n, sh, dt: scD.enter_context(nc.sbuf_tensor(n, sh, dt))
                IBMAX = max(IBLOCKS)
                gus = sbD("gus", [128, 2, 2, HCH, 256], BF16)
                gs_ev = sbD("gs_ev", [128, 2, 256], BF16)
                h_ev = sbD("h_ev", [128, 2, 256], BF16)
                hT_ib = sbD("hT_ib", [128, IBMAX, NQ], BF16)
                wdt = sbD("wdt", [128, 2, 512], BF16)
                y2row = sbD("y2row", [128, 2, 512], BF16)
                first_gu_of_ib = {}
                for (gidx, ib, ri, ic0, c0, r, tt) in gu_plan:
                    if ib not in first_gu_of_ib:
                        first_gu_of_ib[ib] = gidx
                last_hT_of_gidx = {g0: ("hT", ib, c0 + r - 1, tt)
                                   for (g0, ib, ri, ic0, c0, r, tt) in gu_plan}

                with nc.Block() as block:
                    @block.sync
                    def _(s):
                        s.wait_ge(sem("cc_w"), 2)
                        s.wait_ge(sem("st_x1"), 16 * OTT)
                        dn = 0
                        NTH = max(1, OTT // 4)
                        for ib in range(NIB):
                            icb = sum(IBLOCKS[:ib])
                            nch = IBLOCKS[ib]
                            for ri, (ib2, run_i, ic0, c0, r) in enumerate(ffn_runs):
                                if ib2 != ib:
                                    continue
                                if ri >= 2:
                                    s.wait_ge(sem("t_gu"), (ri - 1) * OTT)
                                s.dma_start(out=gus[:, ri % 2, 0, :, 0:r * 128],
                                            in_=ag_out["g"][:, ic0 * 128:(ic0 + r) * 128]
                                            .rearrange("(hc p) i -> p hc i", p=128)
                                            ).then_inc(sem("ld_gu%d" % (ri % 2)), 16)
                                s.dma_start(out=gus[:, ri % 2, 1, :, 0:r * 128],
                                            in_=ag_out["u"][:, ic0 * 128:(ic0 + r) * 128]
                                            .rearrange("(hc p) i -> p hc i", p=128)
                                            ).then_inc(sem("ld_gu%d" % (ri % 2)), 16)
                            if ib == 0:
                                s.wait_ge(sem("cc_w"), 3)
                            for oc in range(OCD):
                                for th in range(NTH):
                                    for icr in range(nch):
                                        if dn >= 2:
                                            s.wait_ge(sem("t_dn"), dn - 1)
                                        s.dma_start(
                                            out=wdt[:, dn % 2, :],
                                            in_=ag_out["d"][(icb + icr) * 128:
                                                            (icb + icr + 1) * 128,
                                                            oc * 512:(oc + 1) * 512]
                                        ).then_inc(sem("ld_dn%d" % (dn % 2)), 16)
                                        dn += 1

                    @block.tensor
                    def _(t):
                        t.wait_ge(sem("v_pse"), tpi(("x2T", OTT - 1, HCH - 1)) + 1)
                        t.wait_ge(sem("v_den"), (D // 128) * OTT)
                        dn = 0
                        for ib in range(NIB):
                            nch = IBLOCKS[ib]
                            for (gidx, ib2, ri, ic0, c0, r, tt) in gu_plan:
                                if ib2 != ib:
                                    continue
                                if tt == 0:
                                    t.wait_ge(sem("ld_gu%d" % (ri % 2)),
                                              32 * (ri // 2 + 1))
                                if gidx >= 2:
                                    t.wait_ge(sem("v_h"), gidx - 1)
                                if ib >= 1 and gidx == first_gu_of_ib[ib]:
                                    t.wait_ge(sem("v_dn"), ib * OCD * OTT)
                                pd = psum_ab(gidx)
                                i = None
                                for hc in range(HCH):
                                    i = t.matmul(pd[:, 0:r * 128],
                                                 xn2T_sb[:, hc, tt * 128:(tt + 1) * 128],
                                                 gus[:, ri % 2, 0, hc, 0:r * 128],
                                                 start=(hc == 0), stop=(hc == HCH - 1))
                                for hc in range(HCH):
                                    i = t.matmul(pd[:, 512:512 + r * 128],
                                                 xn2T_sb[:, hc, tt * 128:(tt + 1) * 128],
                                                 gus[:, ri % 2, 1, hc, 0:r * 128],
                                                 start=(hc == 0), stop=(hc == HCH - 1))
                                i.then_inc(sem("t_gu"), 1)
                                t.wait_ge(sem("v_h"), gidx + 1)
                                for rc in range(r):
                                    idx = tpi(("hT", ib, c0 + rc, tt))
                                    t.wait_ge(sem("v_pse"), idx - NSLOT + 1)
                                    t.transpose(psd_slot(idx),
                                                h_ev[:, gidx % 2, rc * 128:(rc + 1) * 128],
                                                ident[:]).then_inc(sem("t_ps"), 1)
                            t.wait_ge(sem("v_pse"), tpi(("hT", ib, nch - 1, OTT - 1)) + 1)
                            NTH = max(1, OTT // 4)
                            for oc in range(OCD):
                                for th in range(NTH):
                                    gdi = (ib * OCD + oc) * NTH + th
                                    if gdi >= 1:
                                        t.wait_ge(sem("v_dn"), gdi * 4)
                                    for icr in range(nch):
                                        t.wait_ge(sem("ld_dn%d" % (dn % 2)),
                                                  16 * (dn // 2 + 1))
                                        i = None
                                        for tq in range(4):
                                            tt = th * 4 + tq
                                            i = t.matmul(psBC_slot256(tq),
                                                         hT_ib[:, icr,
                                                               tt * 128:(tt + 1) * 128],
                                                         wdt[:, dn % 2, :],
                                                         start=(icr == 0),
                                                         stop=(icr == nch - 1))
                                        i.then_inc(sem("t_dn"), 1)
                                        dn += 1

                    @block.scalar
                    def _(sc):
                        for ib in range(NIB):
                            for (gidx, ib2, ri, ic0, c0, r, tt) in gu_plan:
                                if ib2 != ib:
                                    continue
                                sc.wait_ge(sem("t_gu"), gidx + 1)
                                if gidx >= 2:
                                    sc.wait_ge(sem("v_h"), gidx - 1)
                                sc.activation(gs_ev[:, gidx % 2, 0:r * 128],
                                              psum_ab(gidx)[:, 0:r * 128],
                                              ACTF.Silu).then_inc(sem("a_silu"), 1)
                            NTH = max(1, OTT // 4)
                            for oc in range(OCD):
                                for th in range(NTH):
                                    for tq in range(4):
                                        tt = th * 4 + tq
                                        di = ((ib * OCD + oc) * NTH + th) * 4 + tq
                                        sc.wait_ge(sem("v_dn"), di + 1)
                                        sc.dma_start(
                                            out=y2p[ib, tt * 128:(tt + 1) * 128,
                                                    oc * 512:(oc + 1) * 512],
                                            in_=y2row[:, di % 2, :]
                                            ).then_inc(sem("st_y2%d" % (di % 2)), 16)

                    @block.vector
                    def _(v):
                        for ib in range(NIB):
                            nch = IBLOCKS[ib]
                            for (gidx, ib2, ri, ic0, c0, r, tt) in gu_plan:
                                if ib2 != ib:
                                    continue
                                v.wait_ge(sem("a_silu"), gidx + 1)
                                if gidx >= 2:
                                    v.wait_ge(sem("t_ps"),
                                              tpi(last_hT_of_gidx[gidx - 2]) + 1)
                                v.tensor_tensor(h_ev[:, gidx % 2, 0:r * 128],
                                                psum_ab(gidx)[:, 512:512 + r * 128],
                                                gs_ev[:, gidx % 2, 0:r * 128],
                                                ALU.mult).then_inc(sem("v_h"), 1)
                                for rc in range(r):
                                    idx = tpi(("hT", ib, c0 + rc, tt))
                                    v.wait_ge(sem("t_ps"), idx + 1)
                                    v.tensor_copy(
                                        hT_ib[:, c0 + rc, tt * 128:(tt + 1) * 128],
                                        psd_slot(idx)).then_inc(sem("v_pse"), 1)
                            NTH = max(1, OTT // 4)
                            for oc in range(OCD):
                                for th in range(NTH):
                                    for tq in range(4):
                                        tt = th * 4 + tq
                                        di = ((ib * OCD + oc) * NTH + th) * 4 + tq
                                        v.wait_ge(sem("t_dn"),
                                                  (sum(IBLOCKS[:ib]) * OCD * NTH
                                                   + (oc * NTH + th + 1) * nch))
                                        if di >= 2:
                                            v.wait_ge(sem("st_y2%d" % (di % 2)),
                                                      16 * (di // 2))
                                        v.tensor_copy(y2row[:, di % 2, :],
                                                      psBC_slot256(tq)
                                                      ).then_inc(sem("v_dn"), 1)

        if phases < 7:
            dbg_terminal(6)
            mybir.codegen_inst_isa_subclasses(nc)
            return nc
        with contextlib.ExitStack() as scE:
            sbE = lambda n, sh, dt: scE.enter_context(nc.sbuf_tensor(n, sh, dt))
            fin = sbE("fin", [128, NIB, D], BF16)
            x1f = sbE("x1f", [128, D], BF16)
            acc = sbE("acc", [128, D], F32)
            acc2 = sbE("acc2", [128, D], F32)
            outb = sbE("outb", [128, 2, D], F32)
            stats3 = sbE("stats3", [128, D // 512, 6], F32)
            mv3 = sbE("mv3", [128, 2], F32)
            rstd3 = sbE("rstd3", [128, 1], F32)
            negm3 = sbE("negm3", [128, 1], F32)

            with nc.Block() as block:
                @block.sync
                def _(s):
                    s.wait_ge(sem("st_y20"), 16 * (NIB * OCD * OTT // 2))
                    s.wait_ge(sem("st_y21"), 16 * (NIB * OCD * OTT - NIB * OCD * OTT // 2))
                    s.wait_ge(sem("st_x1"), 16 * OTT)
                    for tt in range(OTT):
                        if tt >= 1:
                            s.wait_ge(sem("v_fin"), tt)
                        for ib in range(NIB):
                            s.dma_start(out=fin[:, ib, :],
                                        in_=y2p[ib, tt * 128:(tt + 1) * 128, :]
                                        ).then_inc(sem("ld_fin"), 16)
                        s.dma_start(out=x1f[:],
                                    in_=x1_dram[tt * 128:(tt + 1) * 128, :]
                                    ).then_inc(sem("ld_fin"), 16)

                @block.vector
                def _(v):
                    for tt in range(OTT):
                        v.wait_ge(sem("ld_fin"), 16 * (NIB + 1) * (tt + 1))
                        if NIB >= 2:
                            v.tensor_tensor(acc[:], fin[:, 0, :], fin[:, 1, :], ALU.add)
                        else:
                            v.tensor_copy(acc[:], fin[:, 0, :])
                        v.drain()
                        cur = acc
                        for ib in range(2, NIB):
                            dst = acc2 if cur is acc else acc
                            v.tensor_tensor(dst[:], cur[:], fin[:, ib, :], ALU.add)
                            v.drain()
                            cur = dst
                        for cc in range(D // 512):
                            v.bn_stats(stats3[:, cc, :], cur[:, cc * 512:(cc + 1) * 512])
                        v.drain()
                        v.bn_aggr(mv3[:], stats3.ap().rearrange("p a b -> p (a b)"))
                        v.drain()
                        v.tensor_scalar(negm3[:], mv3[:, 0:1], -1.0, None, ALU.mult
                                        ).then_inc(sem("v_mv3"), 1)
                        v.wait_ge(sem("a_rstd3"), tt + 1)
                        v.reciprocal(rstd3[:], rstd3[:])
                        v.drain()
                        other = acc2 if cur is acc else acc
                        v.tensor_scalar(other[:], cur[:], negm3[:], rstd3[:],
                                        ALU.add, ALU.mult)
                        v.drain()
                        if tt >= 2:
                            v.wait_ge(sem("st_out%d" % (tt % 2)), 16 * (tt // 2))
                        v.tensor_tensor(outb[:, tt % 2, :], other[:], x1f[:],
                                        ALU.add).then_inc(sem("v_fin"), 1)

                @block.scalar
                def _(sc):
                    for tt in range(OTT):
                        sc.wait_ge(sem("v_mv3"), tt + 1)
                        sc.activation(rstd3[:], mv3[:, 1:2], ACTF.Sqrt,
                                      bias=eps_t[:]).then_inc(sem("a_rstd3"), 1)
                        sc.wait_ge(sem("v_fin"), tt + 1)
                        sc.dma_start(out=out[tt * 128:(tt + 1) * 128, :],
                                     in_=outb[:, tt % 2, :]
                                     ).then_inc(sem("st_out%d" % (tt % 2)), 16)
                    sc.wait_ge(sem("st_out0"), 16 * (OTT - OTT // 2))
                    sc.wait_ge(sem("st_out1"), 16 * (OTT // 2))

    mybir.codegen_inst_isa_subclasses(nc)
    return nc



def _wrap16x2(flat):
    w = flat.reshape(-1, 16).T
    w32 = np.concatenate([w, w], axis=0)
    return np.ascontiguousarray(
        np.vstack(np.split(w32, 4, axis=1)))


def _wrap128(flat):
    return np.ascontiguousarray(flat.reshape(-1, 128).T)


def make_in_maps(cfg, x, w_qkv, w_dense, codebook, idx_gate, idx_up, idx_down):
    C = derive(cfg)
    D, CBN = C["D"], C["CBN"]
    OWN, SKC, GJ, DJ = C["OWN"], C["SKC"], C["GJ"], C["DJ"]
    wqkvT = np.ascontiguousarray(np.asarray(w_qkv, dtype=np.float32).T)
    wdenT = np.ascontiguousarray(np.asarray(w_dense, dtype=np.float32).T)
    cbp = np.zeros((CBN // 2, 64), dtype=np.float32)
    cbp[:, :16] = np.asarray(codebook, dtype=np.float32).reshape(CBN // 2, 16)
    x = np.asarray(x, dtype=np.float32)
    idx_gate = np.asarray(idx_gate, dtype=np.int32)
    idx_up = np.asarray(idx_up, dtype=np.int32)
    idx_down = np.asarray(idx_down, dtype=np.int32)

    in_maps = []
    for c in range(NCORES):
        b, q = c // 4, c % 4
        xo = np.ascontiguousarray(x[b, q * OWN:(q + 1) * OWN, :])
        kk = np.arange(128)[:, None] + 128 * np.arange(SKC)[None, :]
        valid = (kk < OWN * q).astype(np.float32)
        pp = np.arange(128)[:, None]
        ff = np.arange(512)[None, :]
        maskin = np.stack([np.where(pp <= ff - 128 * mm, 0.0, -10000.0)
                           for mm in range(4)],
                          axis=1).astype(ml_dtypes.bfloat16)
        m = {"x_own": xo, "wqkvT": wqkvT, "wdenT": wdenT, "cbp": cbp,
             "validin": np.ascontiguousarray(valid),
             "maskin": np.ascontiguousarray(maskin)}
        gsl = idx_gate[:, GJ * c:GJ * (c + 1)].T.reshape(-1)
        usl = idx_up[:, GJ * c:GJ * (c + 1)].T.reshape(-1)
        dsl = idx_down[:, DJ * c:DJ * (c + 1)].T.reshape(-1)
        m["idxg16"], m["idxg128"] = _wrap16x2(gsl), _wrap128(gsl)
        m["idxu16"], m["idxu128"] = _wrap16x2(usl), _wrap128(usl)
        m["idxd16"], m["idxd128"] = _wrap16x2(dsl), _wrap128(dsl)
        in_maps.append(m)
    return in_maps


def kernel(L0_attn_q, L0_attn_w_qkv, L0_attn_w_dense, L0_ffn_codebook,
           L0_ffn_idx_gate, L0_ffn_idx_up, L0_ffn_idx_down):
    cfg = FULL
    C = derive(cfg)
    nc = build_program(cfg)
    in_maps = make_in_maps(cfg, L0_attn_q, L0_attn_w_qkv, L0_attn_w_dense,
                           L0_ffn_codebook, L0_ffn_idx_gate, L0_ffn_idx_up,
                           L0_ffn_idx_down)
    res = run_bass_kernel_spmd(nc, in_maps, core_ids=list(range(NCORES)))
    OWN, S, D = C["OWN"], C["S"], C["D"]
    full = np.empty((cfg["B"], S, D), dtype=np.float32)
    for c in range(NCORES):
        b, q = c // 4, c % 4
        full[b, q * OWN:(q + 1) * OWN, :] = np.asarray(
            res.results[c]["out"]).reshape(OWN, D)
    return full


# version 11
# speedup vs baseline: 1.0053x; 1.0053x over previous
import contextlib
import numpy as np
import ml_dtypes

import concourse.bass as bass
import concourse.mybir as mybir
from concourse.bass_utils import run_bass_kernel_spmd
from concourse.library_config import mlp as LIB_MLP

F32 = mybir.dt.float32
BF16 = mybir.dt.bfloat16
I32 = mybir.dt.int32
I16 = mybir.dt.int16
ALU = mybir.AluOpType
ACTF = mybir.ActivationFunctionType

NCORES = 8
FULL = dict(B=2, S=4096, D=4096, KVH=8, G=4, IFF=11008, CBN=65536)
MINI = dict(B=2, S=2048, D=1024, KVH=2, G=2, IFF=2048, CBN=512)
EXP_SHIFT = -30.0
LN_EPS = 1e-5


def derive(cfg):
    c = dict(cfg)
    c["NH"] = c["KVH"] * c["G"]
    c["QD"] = c["NH"] * 128
    c["KVD"] = c["KVH"] * 128
    c["QKVD"] = c["QD"] + 2 * c["KVD"]
    c["OWN"] = c["S"] // 4
    c["NQ"] = c["OWN"]
    c["HCH"] = c["D"] // 128
    c["QDCH"] = c["QD"] // 128
    c["OTT"] = c["OWN"] // 128
    c["SKC"] = c["S"] // 128
    c["OKC"] = c["OWN"] // 128
    c["ICH"] = c["IFF"] // 128
    c["NQH"] = max(1, c["NQ"] // 512)
    c["GJ"] = c["D"] // 8 // NCORES
    c["DJ"] = c["IFF"] // 8 // NCORES
    assert c["GJ"] * 8 * NCORES == c["D"] and c["DJ"] * 8 * NCORES == c["IFF"]
    ich = c["ICH"]
    nb = max(1, (ich + 21) // 22)
    base, rem = ich // nb, ich % nb
    c["IBLOCKS"] = [base + (1 if i < rem else 0) for i in range(nb)]
    return c


def runs_of(n, r):
    out = []
    while n > 0:
        out.append(min(r, n))
        n -= out[-1]
    return out


def build_program(cfg, phases=9, dbg_no_prep=False):
    C = derive(cfg)
    S, D = C["S"], C["D"]
    KVH, G, NH = C["KVH"], C["G"], C["NH"]
    QD, KVD, QKVD, QDCH = C["QD"], C["KVD"], C["QKVD"], C["QDCH"]
    OWN, NQ, HCH, OTT = C["OWN"], C["NQ"], C["HCH"], C["OTT"]
    SKC, OKC, ICH, IFF = C["SKC"], C["OKC"], C["ICH"], C["IFF"]
    NQH, GJ, DJ, CBN = C["NQH"], C["GJ"], C["DJ"], C["CBN"]
    IBLOCKS = C["IBLOCKS"]
    NIB = len(IBLOCKS)
    KC_ALL = SKC + OKC
    SCALE = float(1.0 / np.sqrt(128.0))
    OCD = D // 512
    NQ2 = NQ // 2
    NROW = NQ // 512

    dq_calls = []
    for w, nj, nidx in (("g", GJ, IFF), ("u", GJ, IFF)):
        for j in range(nj):
            dq_calls.append((w, j, nidx, nidx // 128))
    for j in range(DJ):
        dq_calls.append(("d", j, D, D // 128))
    NDQ = len(dq_calls)
    GMAX = 1024
    dq_chunks = []
    for w, j, nidx, rawc in dq_calls:
        chs, r0 = [], 0
        while r0 < rawc:
            crawc = min(GMAX // 128, rawc - r0)
            chs.append((r0, crawc))
            r0 += crawc
        dq_chunks.append(chs)
    NGQ = 4
    DQR = 2
    call_qnums = []
    _gch = 0
    for ci in range(NDQ):
        qs = []
        for _ in dq_chunks[ci]:
            qs.append(_gch % NGQ)
            _gch += 1
        call_qnums.append(qs)
    _qr_cum = {}
    call_qwaits = []
    for ci in range(NDQ):
        r = ci % DQR
        per_q = {}
        for q in call_qnums[ci]:
            _qr_cum[(q, r)] = _qr_cum.get((q, r), 0) + 1
            per_q[q] = _qr_cum[(q, r)]
        call_qwaits.append([(q, r, c) for q, c in sorted(per_q.items())])
    gqr_tot = dict(_qr_cum)
    dq_groups = []
    for w, nj, rawc in (("g", GJ, IFF // 128), ("u", GJ, IFF // 128),
                        ("d", DJ, D // 128)):
        j0 = 0
        for gg in runs_of(nj, 16):
            dq_groups.append((w, j0, gg, rawc, j0 * 8))
            j0 += gg
    NGRP = len(dq_groups)
    grp_last_call = []
    ci0 = 0
    for w, j0, gg, rawc, rb in dq_groups:
        ci0 += gg
        grp_last_call.append(ci0)
    call_group = []
    for k, (w, j0, gg, rawc, rb) in enumerate(dq_groups):
        call_group += [k] * gg
    call_head = [min(NH - 1, (ci * NH) // NDQ) for ci in range(NDQ)]
    grp_head = [call_head[grp_last_call[k] - 1] for k in range(NGRP)]
    for _k in range(2, NGRP):
        assert call_head[grp_last_call[_k - 1]] > grp_head[_k - 2], \
            "dq schedule too tight for double-buffered d_asm"
    head_calls = {h: [ci for ci in range(NDQ) if call_head[ci] == h]
                  for h in range(NH)}
    head_groups = {h: [k for k in range(NGRP) if grp_head[k] == h]
                   for h in range(NH)}
    NQH5 = NQ // 512
    QT5 = 4
    att_sched = []
    att_pm = {}
    att_group_first = {}
    att_group_last = {}
    head_last_entry = {}
    for qh_ in range(NH):
        for h5_ in range(NQH5):
            g5 = qh_ * NQH5 + h5_
            for kc_ in range(KC_ALL):
                if kc_ < SKC:
                    cat = "rect"
                else:
                    m_ = (kc_ - SKC) - 4 * h5_
                    if m_ < 0:
                        cat = "full"
                    elif m_ <= 3:
                        cat = "part"
                    else:
                        continue
                ei = len(att_sched)
                if cat == "part":
                    att_pm[ei] = len(att_pm)
                if g5 not in att_group_first:
                    att_group_first[g5] = ei
                att_group_last[g5] = ei
                att_sched.append((qh_, h5_, kc_, cat))
        head_last_entry[qh_] = len(att_sched)
    att_entry = {}
    for ei, (qh_, h5_, kc_, cat) in enumerate(att_sched):
        att_entry[(qh_, h5_, kc_)] = ei
    exp_ops = []
    expop_after = {}
    for qh_ in range(NH):
        for h5_ in range(NQH5):
            ents5_ = [att_entry[(qh_, h5_, kc_)] for kc_ in range(KC_ALL)
                      if (qh_, h5_, kc_) in att_entry]
            z_ = 0
            while z_ < len(ents5_):
                ei_ = ents5_[z_]
                n_ = 2 if (ei_ % 2 == 0 and z_ + 1 < len(ents5_)) else 1
                exp_ops.append((ei_, n_))
                for e2 in range(ei_, ei_ + n_):
                    expop_after[e2] = len(exp_ops)
                z_ += n_

    ag_issue_after = {"g": 2 * GJ, "u": 2 * GJ + DJ // 2, "d": NDQ}
    ngrp_w = {w: len([1 for gw in dq_groups if gw[0] == w]) for w in ("g", "u", "d")}

    NSLOT = 2
    tp_order = []
    tp_index = {}

    def tp_add(key):
        tp_index[key] = len(tp_order)
        tp_order.append(key)

    for tt in range(OTT):
        for hc in range(HCH):
            tp_add(("xnT", tt, hc))
    for j in range(2 * KVH * OTT):
        tp_add(("kvp", j))
    for qh in range(NH):
        for qt in range(OTT):
            tp_add(("at", qh, qt))
        for k in head_groups[qh]:
            for ic in range(dq_groups[k][3]):
                tp_add(("dq", k, ic))
    for tt in range(OTT):
        for hc in range(HCH):
            tp_add(("x2T", tt, hc))
    ffn_runs = []
    for ib in range(NIB):
        icb = sum(IBLOCKS[:ib])
        c0 = 0
        for run_i, r in enumerate(runs_of(IBLOCKS[ib], 2)):
            ffn_runs.append((ib, run_i, icb + c0, c0, r))
            c0 += r
    gu_plan = []
    for ri, (ib, run_i, ic0, c0, r) in enumerate(ffn_runs):
        for tt in range(OTT):
            gu_plan.append((len(gu_plan), ib, ri, ic0, c0, r, tt))
    for (gidx, ib, ri, ic0, c0, r, tt) in gu_plan:
        for rc in range(r):
            tp_add(("hT", ib, c0 + rc, tt))

    def tpi(key):
        return tp_index[key]

    nc = bass.Bass("TRN2", target_bir_lowering=False, num_swdge_queues=NGQ)

    x_own = nc.dram_tensor("x_own", [OWN, D], F32, kind="ExternalInput").ap()
    wqkvT = nc.dram_tensor("wqkvT", [D, QKVD], F32, kind="ExternalInput").ap()
    wdenT = nc.dram_tensor("wdenT", [QD, D], F32, kind="ExternalInput").ap()
    cbp = nc.dram_tensor("cbp", [CBN // 2, 64], F32, kind="ExternalInput").ap()
    idx16 = {}
    idx128 = {}
    for w, nj, nidx in (("g", GJ, IFF), ("u", GJ, IFF), ("d", DJ, D)):
        idx16[w] = nc.dram_tensor("idx%s16" % w, [128, nj * nidx // 64], I32,
                                  kind="ExternalInput").ap()
        idx128[w] = nc.dram_tensor("idx%s128" % w, [128, nj * nidx // 128], I32,
                                   kind="ExternalInput").ap()
    TOT16 = (2 * GJ * IFF + DJ * D) // 16
    TOTL = (2 * GJ * IFF + DJ * D) // 128
    hi_dram = nc.dram_tensor("hi_dram", [128, TOT16], I16).ap()
    lo_dram = nc.dram_tensor("lo_dram", [128, TOTL], F32).ap()
    off16 = []
    offL = []
    o16 = oL = 0
    for w, j, nidx, rawc in dq_calls:
        off16.append(o16)
        offL.append(oL)
        o16 += nidx // 16
        oL += rawc
    PCH = 1024
    prep_chunks = []
    wo16 = 0
    for w, nj, nidx in (("g", GJ, IFF), ("u", GJ, IFF), ("d", DJ, D)):
        f4 = nj * nidx // 64
        c0 = 0
        for cn in runs_of(f4, PCH):
            prep_chunks.append(("hi", w, c0, cn, wo16))
            c0 += cn
        wo16 += nj * nidx // 16
    woL = 0
    for w, nj, nidx in (("g", GJ, IFF), ("u", GJ, IFF), ("d", DJ, D)):
        fl = nj * nidx // 128
        c0 = 0
        for cn in runs_of(fl, PCH):
            prep_chunks.append(("lo", w, c0, cn, woL))
            c0 += cn
        woL += fl
    if dbg_no_prep:
        prep_chunks = []
    NPREP = len(prep_chunks)
    pst_after = []
    acc = {0: 0, 1: 0}
    for k, (kind, w, c0, cn, doff) in enumerate(prep_chunks):
        acc[k % 2] += 256 if kind == "hi" else 16
        pst_after.append(dict(acc))
    validin = nc.dram_tensor("validin", [128, SKC], F32, kind="ExternalInput").ap()
    maskin = nc.dram_tensor("maskin", [128, 4, 512], BF16, kind="ExternalInput").ap()
    out = nc.dram_tensor("out", [OWN, D], F32, kind="ExternalOutput").ap()

    wq_bf = nc.dram_tensor("wq_bf", [D, QKVD], BF16).ap()
    wd_bf = nc.dram_tensor("wd_bf", [QD, D], BF16).ap()
    qT_dram = nc.dram_tensor("qT_dram", [QD, NQ], BF16).ap()
    k_own = nc.dram_tensor("k_own", [OWN, KVD], BF16).ap()
    v_own = nc.dram_tensor("v_own", [OWN, KVD], BF16).ap()
    k_all = nc.dram_tensor("k_all", [S, KVD], BF16).ap()
    v_all = nc.dram_tensor("v_all", [S, KVD], BF16).ap()
    attnT_dram = nc.dram_tensor("attnT_dram", [QD, NQ], BF16).ap()
    x1_dram = nc.dram_tensor("x1_dram", [OWN, D], BF16).ap()
    ag_in = {"g": nc.dram_tensor("g_in", [D // 8, IFF], BF16).ap(),
             "u": nc.dram_tensor("u_in", [D // 8, IFF], BF16).ap(),
             "d": nc.dram_tensor("d_in", [IFF // 8, D], BF16).ap()}
    ag_out = {"g": nc.dram_tensor("gT_all", [D, IFF], BF16).ap(),
              "u": nc.dram_tensor("uT_all", [D, IFF], BF16).ap(),
              "d": nc.dram_tensor("dT_all", [IFF, D], BF16).ap()}
    y2p = nc.dram_tensor("y2p", [NIB, OWN, D], BF16).ap()

    stack = contextlib.ExitStack()
    sem_tab = {}

    def sem(name):
        if name not in sem_tab:
            sem_tab[name] = stack.enter_context(nc.semaphore(name))
        return sem_tab[name]

    def sbuf(name, shape, dt):
        return stack.enter_context(nc.sbuf_tensor(name, shape, dt))

    with stack:
        psA = stack.enter_context(nc.psum_tensor("psA", [128, 1024], F32))
        psB = stack.enter_context(nc.psum_tensor("psB", [128, 1024], F32))
        psC = stack.enter_context(nc.psum_tensor("psC", [128, 1024], F32))
        psD2 = stack.enter_context(nc.psum_tensor("psD2", [128, 128], BF16))
        psD = stack.enter_context(nc.psum_tensor("psD", [128, 128], BF16))

        def psum_ab(par):
            return psA if par % 2 == 0 else psB

        def psd_slot(idx, rows=128):
            if idx % 2 == 0:
                return psD[0:rows, 0:128]
            return psD2[0:rows, 0:128]

        def psBC_slot256(t):
            assert t < 4
            if t < 2:
                return psB[:, t * 512:t * 512 + 512]
            return psC[:, (t - 2) * 512:(t - 2) * 512 + 512]

        def psBC_att(qt):
            assert qt < 4
            if qt < 2:
                return psB[:, qt * 512:qt * 512 + 129]
            return psC[:, (qt - 2) * 512:(qt - 2) * 512 + 129]

        ident = sbuf("ident", [128, 128], BF16)
        eps_t = sbuf("eps_t", [128, 1], F32)
        ebias = sbuf("ebias", [128, 1], F32)
        scAB = stack.enter_context(contextlib.ExitStack())
        sbAB = lambda n, sh, dt: scAB.enter_context(nc.sbuf_tensor(n, sh, dt))
        s_valid = sbAB("s_valid", [128, SKC], F32)
        masks = sbAB("masks", [128, 4, 512], BF16)
        kT_own = sbAB("kT_own", [128, KVH, OWN], BF16)
        vdiag = sbAB("vdiag", [128, KVH, OKC, 129], BF16)


        def dbg_terminal(level):
            with nc.Block() as blk:
                @blk.scalar
                def _(sc):
                    sc.wait_ge(sem("st_wqbf"), 16 * HCH)
                    sc.wait_ge(sem("st_wdbf"), 16 * QDCH)
                    sc.wait_ge(sem("st_q0"), 16 * ((QDCH + 1) // 2))
                    sc.wait_ge(sem("st_q1"), 16 * (QDCH // 2))
                    sc.wait_ge(sem("st_kv0"), 16 * KVH * OTT)
                    sc.wait_ge(sem("st_kv1"), 16 * KVH * OTT)
                    if NPREP:
                        sc.wait_ge(sem("g_pst0"), pst_after[NPREP - 1][0])
                        sc.wait_ge(sem("g_pst1"), pst_after[NPREP - 1][1])
                    sc.wait_ge(sem("ld_valid"), 32)
                    if level >= 3:
                        sc.wait_ge(sem("st_at0"), 16 * (NH // 2))
                        sc.wait_ge(sem("st_at1"), 16 * (NH // 2))
                        sc.wait_ge(sem("st_dq"), 16 * NGRP)
                        sc.wait_ge(sem("cc_w"), 3)
                        sc.wait_ge(sem("cc_kv"), 2)
                        for (qq, rr), tt2 in sorted(gqr_tot.items()):
                            sc.wait_ge(sem("g_gq%d_%d" % (qq, rr)), 16 * tt2)
                    if level >= 5:
                        sc.wait_ge(sem("st_x1"), 16 * OTT)
                    if level >= 6:
                        sc.wait_ge(sem("st_y20"), 16 * (NIB * OCD * OTT // 2))
                        sc.wait_ge(sem("st_y21"),
                                   16 * (NIB * OCD * OTT - NIB * OCD * OTT // 2))

        with nc.Block() as block:
            @block.gpsimd
            def _(g):
                g.load_library(LIB_MLP)
                g.memset(ident[:], 1.0)
                g.memset(eps_t[:], LN_EPS)
                g.memset(ebias[:], EXP_SHIFT)
                g.memset(vdiag[:], 1.0)
                g.drain()
                g.affine_select(ident[:], ident[:], [[-1, 128]], ALU.is_equal,
                                0.0, base=0, channel_multiplier=1)

            @block.sync
            def _(s):
                s.dma_start(out=s_valid[:], in_=validin[:]).then_inc(sem("ld_valid"), 16)
                s.dma_start(out=masks[:], in_=maskin[:]).then_inc(sem("ld_valid"), 16)

        with contextlib.ExitStack() as scA:
            sbA = lambda n, sh, dt: scA.enter_context(nc.sbuf_tensor(n, sh, dt))
            x_sb = sbA("x_sb", [128, 2, D], F32)
            stats = sbA("stats", [128, D // 512, 6], F32)
            mv = sbA("mv", [128, 2, 2], F32)
            rstd = sbA("rstd", [128, 2, 1], F32)
            negm = sbA("negm", [128, 2, 1], F32)
            xn_sb = sbA("xn_sb", [128, 2, D], BF16)
            xnT_sb = sbA("xnT_sb", [128, HCH, NQ], BF16)
            wqs = sbA("wqs", [128, 2, HCH, 128], BF16)
            qkv_ev = sbA("qkv_ev", [128, 2, NQ], BF16)
            vT_tmp = sbA("vT_tmp", [128, KVH, OWN], BF16)
            kvp = sbA("kvp", [128, 2, 128], BF16)
            pin32 = sbA("pin32", [128, 2, 1024], I32)
            pout16 = sbA("pout16", [128, 2, 1024], I16)
            poutf = sbA("poutf", [128, 2, 1024], F32)

            with nc.Block() as block:
                @block.gpsimd
                def _(g):
                    for hc in range(HCH):
                        g.dma_start(out=wq_bf[hc * 128:(hc + 1) * 128, :],
                                    in_=wqkvT[hc * 128:(hc + 1) * 128, :]
                                    ).then_inc(sem("st_wqbf"), 16)
                    for hc in range(QDCH):
                        g.dma_start(out=wd_bf[hc * 128:(hc + 1) * 128, :],
                                    in_=wdenT[hc * 128:(hc + 1) * 128, :]
                                    ).then_inc(sem("st_wdbf"), 16)
                    for k, (kind, w, c0, cn, doff) in enumerate(prep_chunks):
                        if k >= 2:
                            g.wait_ge(sem("v_prep"), k - 1)
                        src = idx16[w] if kind == "hi" else idx128[w]
                        g.dma_start(out=pin32[:, k % 2, 0:cn],
                                    in_=src[:, c0:c0 + cn]).then_inc(sem("g_pld"), 16)
                        g.wait_ge(sem("v_prep"), k + 1)
                        if kind == "hi":
                            f4w = {"g": GJ * IFF // 64, "u": GJ * IFF // 64,
                                   "d": DJ * D // 64}[w]
                            for i4 in range(4):
                                for b4 in range(4):
                                    g.dma_start(
                                        out=hi_dram[b4 * 32:(b4 + 1) * 32,
                                                    doff + i4 * f4w + c0:
                                                    doff + i4 * f4w + c0 + cn],
                                        in_=pout16[i4 * 32:(i4 + 1) * 32, k % 2, 0:cn]
                                    ).then_inc(sem("g_pst%d" % (k % 2)), 16)
                        else:
                            g.dma_start(out=lo_dram[:, doff + c0:doff + c0 + cn],
                                        in_=poutf[:, k % 2, 0:cn]
                                        ).then_inc(sem("g_pst%d" % (k % 2)), 16)

                @block.sync
                def _(s):
                    for tt in range(OTT):
                        if tt >= 2:
                            s.wait_ge(sem("v_xn"), tt - 1)
                        s.dma_start(out=x_sb[:, tt % 2, :],
                                    in_=x_own[tt * 128:(tt + 1) * 128, :]
                                    ).then_inc(sem("ld_x%d" % (tt % 2)), 16)
                    s.wait_ge(sem("st_wqbf"), 16 * HCH)
                    for oc in range(QKVD // 128):
                        if oc >= 2:
                            s.wait_ge(sem("t_qkv"), oc - 1)
                        s.dma_start(out=wqs[:, oc % 2, :, :],
                                    in_=wq_bf[:, oc * 128:(oc + 1) * 128].rearrange(
                                        "(hc p) o -> p hc o", p=128)
                                    ).then_inc(sem("ld_wq%d" % (oc % 2)), 16)

                @block.scalar
                def _(sc):
                    for tt in range(OTT):
                        sc.wait_ge(sem("v_mv"), tt + 1)
                        sc.activation(rstd[:, tt % 2, :], mv[:, tt % 2, 1:2],
                                      ACTF.Sqrt, bias=eps_t[:]
                                      ).then_inc(sem("a_rstd"), 1)
                    for oc in range(QDCH):
                        sc.wait_ge(sem("v_qkve"), oc + 1)
                        sc.dma_start(out=qT_dram[oc * 128:(oc + 1) * 128, :],
                                     in_=qkv_ev[:, oc % 2, :]
                                     ).then_inc(sem("st_q%d" % (oc % 2)), 16)
                    for j in range(2 * KVH * OTT):
                        sc.wait_ge(sem("v_pse"), tpi(("kvp", j)) + 1)
                        kvoc, tt = j // OTT, j % OTT
                        isv = kvoc >= KVH
                        hd = kvoc - KVH if isv else kvoc
                        tgt = v_own if isv else k_own
                        sc.dma_start(out=tgt[tt * 128:(tt + 1) * 128,
                                             hd * 128:(hd + 1) * 128],
                                     in_=kvp[:, j % 2, :]
                                     ).then_inc(sem("st_kv%d" % (j % 2)), 16)

                @block.vector
                def _(v):
                    def evac_xnT(tt):
                        for hc in range(HCH):
                            idx = tpi(("xnT", tt, hc))
                            v.wait_ge(sem("t_ps"), idx + 1)
                            v.tensor_copy(xnT_sb[:, hc, tt * 128:(tt + 1) * 128],
                                          psd_slot(idx)).then_inc(sem("v_pse"), 1)
                    for tt in range(OTT):
                        v.wait_ge(sem("ld_x%d" % (tt % 2)), 16 * (tt // 2 + 1))
                        xt = x_sb[:, tt % 2, :]
                        for cc in range(D // 512):
                            v.bn_stats(stats[:, cc, :], xt[:, cc * 512:(cc + 1) * 512])
                        v.drain()
                        v.bn_aggr(mv[:, tt % 2, :],
                                  stats.ap().rearrange("p a b -> p (a b)"))
                        v.drain()
                        v.tensor_scalar(negm[:, tt % 2, :], mv[:, tt % 2, 0:1],
                                        -1.0, None, ALU.mult).then_inc(sem("v_mv"), 1)
                        v.wait_ge(sem("a_rstd"), tt + 1)
                        v.reciprocal(rstd[:, tt % 2, :], rstd[:, tt % 2, :])
                        v.drain()
                        v.tensor_scalar(xn_sb[:, tt % 2, :], xt,
                                        negm[:, tt % 2, :], rstd[:, tt % 2, :],
                                        ALU.add, ALU.mult).then_inc(sem("v_xn"), 1)
                        if tt >= 1:
                            evac_xnT(tt - 1)
                    evac_xnT(OTT - 1)
                    prep_cur = [0]

                    def emit_prep():
                        if prep_cur[0] >= NPREP:
                            return
                        k = prep_cur[0]
                        prep_cur[0] += 1
                        kind, w, c0, cn, doff = prep_chunks[k]
                        v.wait_ge(sem("g_pld"), 16 * (k + 1))
                        if k >= 2:
                            v.wait_ge(sem("g_pst%d" % (k % 2)),
                                      pst_after[k - 2][k % 2])
                        if kind == "hi":
                            v.tensor_single_scalar(pin32[:, k % 2, 0:cn],
                                                   pin32[:, k % 2, 0:cn],
                                                   1, ALU.logical_shift_right)
                            v.drain()
                            v.tensor_copy(pout16[:, k % 2, 0:cn],
                                          pin32[:, k % 2, 0:cn]
                                          ).then_inc(sem("v_prep"), 1)
                        else:
                            v.tensor_single_scalar(pin32[:, k % 2, 0:cn],
                                                   pin32[:, k % 2, 0:cn],
                                                   1, ALU.bitwise_and)
                            v.drain()
                            v.tensor_copy(poutf[:, k % 2, 0:cn],
                                          pin32[:, k % 2, 0:cn]
                                          ).then_inc(sem("v_prep"), 1)
                        v.drain()

                    for oc in range(QKVD // 128):
                        emit_prep()
                        v.wait_ge(sem("t_qkv"), oc + 1)
                        src = psum_ab(oc)[:, 0:NQ]
                        if oc < QDCH:
                            if oc >= 2:
                                v.wait_ge(sem("st_q%d" % (oc % 2)), 16 * (oc // 2))
                            v.tensor_copy(qkv_ev[:, oc % 2, :], src
                                          ).then_inc(sem("v_qkve"), 1)
                        elif oc < QDCH + KVH:
                            v.tensor_copy(kT_own[:, oc - QDCH, :], src
                                          ).then_inc(sem("v_qkve"), 1)
                        else:
                            v.tensor_copy(vT_tmp[:, oc - QDCH - KVH, :], src
                                          ).then_inc(sem("v_qkve"), 1)
                    for j in range(2 * KVH * OTT):
                        idx = tpi(("kvp", j))
                        v.wait_ge(sem("t_ps"), idx + 1)
                        if j >= 2:
                            v.wait_ge(sem("st_kv%d" % (j % 2)), 16 * (j // 2))
                        v.tensor_copy(kvp[:, j % 2, :], psd_slot(idx)
                                      ).then_inc(sem("v_pse"), 1)
                    while prep_cur[0] < NPREP:
                        emit_prep()

                @block.tensor
                def _(t):
                    for tt in range(OTT):
                        t.wait_ge(sem("v_xn"), tt + 1)
                        for hc in range(HCH):
                            idx = tpi(("xnT", tt, hc))
                            if idx >= NSLOT:
                                t.wait_ge(sem("v_pse"), idx - NSLOT + 1)
                            t.transpose(psd_slot(idx),
                                        xn_sb[:, tt % 2, hc * 128:(hc + 1) * 128],
                                        ident[:]).then_inc(sem("t_ps"), 1)
                    t.wait_ge(sem("v_pse"), OTT * HCH)
                    for oc in range(QKVD // 128):
                        t.wait_ge(sem("ld_wq%d" % (oc % 2)), 16 * (oc // 2 + 1))
                        if oc >= 2:
                            t.wait_ge(sem("v_qkve"), oc - 1)
                        pdst = psum_ab(oc)[:, 0:NQ]
                        i = None
                        for hc in range(HCH):
                            for h5 in range(NQH):
                                i = t.matmul(pdst[:, h5 * 512:(h5 + 1) * 512],
                                             wqs[:, oc % 2, hc, :],
                                             xnT_sb[:, hc, h5 * 512:(h5 + 1) * 512],
                                             start=(hc == 0), stop=(hc == HCH - 1))
                        i.then_inc(sem("t_qkv"), 1)
                    for j in range(2 * KVH * OTT):
                        kvoc, tt = j // OTT, j % OTT
                        src = (kT_own[:, kvoc, tt * 128:(tt + 1) * 128]
                               if kvoc < KVH else
                               vT_tmp[:, kvoc - KVH, tt * 128:(tt + 1) * 128])
                        t.wait_ge(sem("v_qkve"), QDCH + kvoc + 1)
                        idx = tpi(("kvp", j))
                        t.wait_ge(sem("v_pse"), idx - NSLOT + 1)
                        t.transpose(psd_slot(idx), src, ident[:]).then_inc(sem("t_ps"), 1)

        if phases < 3:
            dbg_terminal(2)
            mybir.codegen_inst_isa_subclasses(nc)
            return nc
        with contextlib.ExitStack() as scB:
            sbB = lambda n, sh, dt: scB.enter_context(nc.sbuf_tensor(n, sh, dt))
            ktb = sbB("ktb", [128, S], BF16)
            vbuf = sbB("vbuf", [128, SKC, 129], BF16)
            vrect = sbB("vrect", [128, SKC, 129], BF16)
            qtb = sbB("qtb", [128, 2, NQ], BF16)
            pT = sbB("pT", [128, 4, 512], BF16)
            rec = sbB("rec", [128, 2, 1], F32)
            at_ev = sbB("at_ev", [128, 2, 128], BF16)
            atT_row = sbB("atT_row", [128, 2, NQ], BF16)
            RAWC = max(ICH, HCH)
            N16MAX = max(IFF, D) // 16
            d_raw = sbB("d_raw", [128, DQR, RAWC, 64], F32)
            d_hi = sbB("d_hi", [128, DQR, N16MAX], I16)
            d_lo = sbB("d_lo", [128, DQR, RAWC], F32)
            d_dif = sbB("d_dif", [128, RAWC, 8], F32)
            d_asm = sbB("d_asm", [128, 2, RAWC, 16, 8], BF16)
            d_row = sbB("d_row", [128, max(IFF, D)], BF16)

            with nc.Block() as block:
                @block.gpsimd
                def _(g):
                    nidx_regs = {}
                    for chs in dq_chunks:
                        for r0x, crawcx in chs:
                            if crawcx * 128 not in nidx_regs:
                                nidx_regs[crawcx * 128] = g.to_reg(crawcx * 128)
                    g.wait_ge(sem("st_q0"), 16 * ((QDCH + 1) // 2))
                    g.wait_ge(sem("st_q1"), 16 * (QDCH // 2))
                    g.wait_ge(sem("st_kv0"), 16 * KVH * OTT)
                    g.wait_ge(sem("st_kv1"), 16 * KVH * OTT)
                    g.memset(d_hi[:], 0)
                    g.drain()
                    g.memset(vbuf[:, :, 128:129], 1.0).then_inc(sem("g_vones"), 1)
                    g.collective_compute(
                        "AllGather", ALU.bypass,
                        replica_groups=[[0, 1, 2, 3], [4, 5, 6, 7]],
                        ins=[k_own.opt()], outs=[k_all.opt()]).then_inc(sem("cc_kv"), 1)
                    g.collective_compute(
                        "AllGather", ALU.bypass,
                        replica_groups=[[0, 1, 2, 3], [4, 5, 6, 7]],
                        ins=[v_own.opt()], outs=[v_all.opt()]).then_inc(sem("cc_kv"), 1)
                    if NPREP:
                        g.wait_ge(sem("v_prep"), NPREP)
                        g.wait_ge(sem("g_pst0"), pst_after[NPREP - 1][0])
                        g.wait_ge(sem("g_pst1"), pst_after[NPREP - 1][1])
                    stc = 0

                    def dq_loads(ci):
                        if ci >= DQR:
                            g.wait_ge(sem("g_sel"), ci - DQR + 1)
                        par2 = ci % DQR
                        n16b = dq_calls[ci][2] // 16
                        g.dma_start(out=d_hi[0:128, par2, 0:n16b],
                                    in_=hi_dram[:, off16[ci]:off16[ci] + n16b]
                                    ).then_inc(sem("g_ld"), 16)
                        g.dma_start(out=d_lo[:, par2, 0:dq_calls[ci][3]],
                                    in_=lo_dram[:, offL[ci]:offL[ci] + dq_calls[ci][3]]
                                    ).then_inc(sem("g_ld"), 16)

                    dq_loads(0)
                    for ci, (w, j, nidx, rawc) in enumerate(dq_calls):
                        par = ci % DQR
                        n16 = nidx // 16
                        grp = call_group[ci]
                        if grp >= 2 and call_group[ci - 1] != grp:
                            pg = grp - 2
                            g.wait_ge(sem("t_ps"),
                                      tpi(("dq", pg, dq_groups[pg][3] - 1)) + 1)
                        g.wait_ge(sem("g_ld"), 32 * (ci + 1))
                        for (r0, crawc), qn in zip(dq_chunks[ci], call_qnums[ci]):
                            cn = crawc * 128
                            g.dma_gather(d_raw[:, par, r0:r0 + crawc, :], cbp[:],
                                         d_hi[:, par, r0 * 8:r0 * 8 + cn // 16],
                                         num_idxs=cn,
                                         num_idxs_reg=nidx_regs[cn],
                                         elem_size=64, elem_step=64,
                                         queue_num=qn
                                         ).then_inc(
                                sem("g_gq%d_%d" % (qn, ci % DQR)), 16)
                        if ci + 1 < NDQ:
                            dq_loads(ci + 1)
                        for w2 in ("g", "u", "d"):
                            if ag_issue_after[w2] == ci + 1:
                                stc2 = 16 * sum(ngrp_w[w3] for w3 in ("g", "u", "d")
                                                if ("gud".index(w3) <= "gud".index(w2)))
                                g.wait_ge(sem("st_dq"), stc2)
                                g.collective_compute(
                                    "AllGather", ALU.bypass,
                                    replica_groups=[list(range(8))],
                                    ins=[ag_in[w2].opt()], outs=[ag_out[w2].opt()]
                                ).then_inc(sem("cc_w"), 1)

                @block.sync
                def _(s):
                    s.wait_ge(sem("cc_kv"), 2)
                    for kv in range(KVH):
                        s.dma_start(out=vdiag[:, kv, :, 0:128],
                                    in_=v_own[:, kv * 128:(kv + 1) * 128].rearrange(
                                        "(c p) d -> p c d", p=128)
                                    ).then_inc(sem("ld_vd"), 16)
                    for kvh in range(KVH):
                        if kvh >= 1:
                            s.wait_ge(sem("t_av"), head_last_entry[kvh * G - 1])
                        s.dma_start_transpose(ktb[:],
                                              k_all[:, kvh * 128:(kvh + 1) * 128]
                                              ).then_inc(sem("ld_kt"), 16)
                        if kvh >= 1:
                            s.wait_ge(sem("v_vr"), kvh)
                        s.dma_start(out=vbuf[:, :, 0:128],
                                    in_=v_all[:, kvh * 128:(kvh + 1) * 128].rearrange(
                                        "(c p) d -> p c d", p=128)
                                    ).then_inc(sem("ld_v"), 16)
                        for gq in range(G):
                            qh = kvh * G + gq
                            if qh >= 2:
                                s.wait_ge(sem("t_av"), head_last_entry[qh - 2])
                            s.dma_start(out=qtb[:, qh % 2, :],
                                        in_=qT_dram[qh * 128:(qh + 1) * 128, :]
                                        ).then_inc(sem("ld_qt%d" % (qh % 2)), 16)

                @block.vector
                def _(v):
                    v.wait_ge(sem("ld_valid"), 32)
                    v.wait_ge(sem("g_vones"), 1)
                    for kvh in range(KVH):
                        v.wait_ge(sem("ld_v"), 16 * (kvh + 1))
                        if kvh >= 1:
                            v.wait_ge(sem("t_av"), head_last_entry[kvh * G - 1])
                        vb_ = s_valid.ap()[:, :, None].broadcast_to([128, SKC, 129])
                        v.tensor_tensor(vrect[:], vbuf[:], vb_, ALU.mult
                                        ).then_inc(sem("v_vr"), 1)
                        for gq in range(G):
                            qh = kvh * G + gq
                            for ci in head_calls[qh]:
                                w, j, nidx, rawc = dq_calls[ci]
                                par = ci % DQR
                                grp = call_group[ci]
                                if grp >= 2 and (ci == 0 or call_group[ci - 1] != grp):
                                    pg = grp - 2
                                    v.wait_ge(sem("t_ps"),
                                              tpi(("dq", pg, dq_groups[pg][3] - 1)) + 1)
                                for qq, rr, cc in call_qwaits[ci]:
                                    v.wait_ge(sem("g_gq%d_%d" % (qq, rr)),
                                              16 * cc)
                                lob = d_lo.ap()[:, par, 0:rawc, None].broadcast_to(
                                    [128, rawc, 8])
                                v.tensor_tensor(d_dif[:, 0:rawc, :],
                                                d_raw[:, par, 0:rawc, 8:16],
                                                d_raw[:, par, 0:rawc, 0:8],
                                                ALU.subtract)
                                v.drain()
                                v.tensor_tensor(d_dif[:, 0:rawc, :],
                                                d_dif[:, 0:rawc, :], lob, ALU.mult)
                                v.drain()
                                v.tensor_tensor(d_asm[:, grp % 2, 0:rawc, j % 16, :],
                                                d_dif[:, 0:rawc, :],
                                                d_raw[:, par, 0:rawc, 0:8],
                                                ALU.add).then_inc(sem("g_sel"), 1)
                                v.drain()
                            for h5 in range(NQH5):
                                g5 = qh * NQH5 + h5
                                for qt in range(QT5):
                                    qtg = h5 * QT5 + qt
                                    v.wait_ge(sem("t_av"), att_group_last[g5] + 1)
                                    slot = psBC_att(qt)
                                    v.reciprocal(rec[:, qt % 2, :], slot[:, 128:129])
                                    v.drain()
                                    pidx = qh * OTT + qtg
                                    if pidx >= 2:
                                        prev = tpi(("at", (pidx - 2) // OTT,
                                                    (pidx - 2) % OTT))
                                        v.wait_ge(sem("t_ps"), prev + 1)
                                    v.tensor_scalar(at_ev[:, qtg % 2, :],
                                                    slot[:, 0:128],
                                                    rec[:, qt % 2, :], None, ALU.mult
                                                    ).then_inc(sem("v_at"), 1)
                                for qt in range(QT5):
                                    qtg = h5 * QT5 + qt
                                    idx = tpi(("at", qh, qtg))
                                    v.wait_ge(sem("t_ps"), idx + 1)
                                    if h5 == 0 and qt == 0 and qh >= 2:
                                        v.wait_ge(sem("st_at%d" % (qh % 2)),
                                                  16 * (qh // 2))
                                    v.tensor_copy(
                                        atT_row[:, qh % 2, qtg * 128:(qtg + 1) * 128],
                                        psd_slot(idx)).then_inc(sem("v_pse"), 1)
                            for k in head_groups[qh]:
                                wg, j0g, gg, rawcg, rbg = dq_groups[k]
                                rows = gg * 8
                                for ic in range(rawcg):
                                    idx = tpi(("dq", k, ic))
                                    v.wait_ge(sem("t_ps"), idx + 1)
                                    if ic == 0 and k >= 1:
                                        v.wait_ge(sem("st_dq"), 16 * k)
                                    v.tensor_copy(d_row[0:rows, ic * 128:(ic + 1) * 128],
                                                  psd_slot(idx, rows)
                                                  ).then_inc(sem("v_pse"), 1)

                @block.scalar
                def _(sc):
                    for kvh in range(KVH):
                        for gq in range(G):
                            qh = kvh * G + gq
                            for h5 in range(NQH5):
                                ents5 = [att_entry[(qh, h5, kc)]
                                         for kc in range(KC_ALL)
                                         if (qh, h5, kc) in att_entry]
                                z = 0
                                while z < len(ents5):
                                    ei = ents5[z]
                                    if ei % 2 == 0 and z + 1 < len(ents5):
                                        sc.wait_ge(sem("t_s"), ei + 2)
                                        if ei >= 3:
                                            sc.wait_ge(sem("t_av"), ei - 2)
                                        sc.activation(pT[:, ei % 4:ei % 4 + 2, :],
                                                      psA[:, 0:1024],
                                                      ACTF.Exp, bias=ebias[:],
                                                      scale=SCALE
                                                      ).then_inc(sem("a_exp"), 1)
                                        z += 2
                                        continue
                                    sc.wait_ge(sem("t_s"), ei + 1)
                                    if ei >= 4:
                                        sc.wait_ge(sem("t_av"), ei - 3)
                                    sc.activation(pT[:, ei % 4, :],
                                                  psA[:, (ei % 2) * 512:
                                                      (ei % 2) * 512 + 512],
                                                  ACTF.Exp, bias=ebias[:],
                                                  scale=SCALE).then_inc(sem("a_exp"), 1)
                                    z += 1
                            sc.wait_ge(sem("v_pse"), tpi(("at", qh, OTT - 1)) + 1)
                            sc.dma_start(out=attnT_dram[qh * 128:(qh + 1) * 128, :],
                                         in_=atT_row[:, qh % 2, :]
                                         ).then_inc(sem("st_at%d" % (qh % 2)), 16)
                            for k in head_groups[qh]:
                                wg, j0g, gg, rawcg, rbg = dq_groups[k]
                                rows = gg * 8
                                sc.wait_ge(sem("v_pse"), tpi(("dq", k, rawcg - 1)) + 1)
                                sc.dma_start(out=ag_in[wg][rbg:rbg + rows, :],
                                             in_=d_row[0:rows, 0:rawcg * 128]
                                             ).then_inc(sem("st_dq"), 16)

                @block.tensor
                def _(t):
                    t.wait_ge(sem("ld_vd"), 16 * KVH)
                    for kvh in range(KVH):
                        t.wait_ge(sem("ld_kt"), 16 * (kvh + 1))
                        for gq in range(G):
                            qh = kvh * G + gq
                            t.wait_ge(sem("ld_qt%d" % (qh % 2)), 16 * (qh // 2 + 1))
                            if gq == 0:
                                t.wait_ge(sem("v_vr"), kvh + 1)
                            for h5 in range(NQH5):
                                g5 = qh * NQH5 + h5
                                ents = [(att_entry[(qh, h5, kc)], kc)
                                        for kc in range(KC_ALL)
                                        if (qh, h5, kc) in att_entry]

                                def qk(ei, kc):
                                    if ei >= 2:
                                        t.wait_ge(sem("a_exp"),
                                                  expop_after[ei - 2])
                                    pdst = psA[:, (ei % 2) * 512:(ei % 2) * 512 + 512]
                                    lhs = (ktb[:, kc * 128:(kc + 1) * 128]
                                           if kc < SKC else
                                           kT_own[:, kvh, (kc - SKC) * 128:
                                                  (kc - SKC + 1) * 128])
                                    part = att_sched[ei][3] == "part"
                                    i = t.matmul(pdst, lhs,
                                                 qtb[:, qh % 2, h5 * 512:(h5 + 1) * 512],
                                                 start=True, stop=not part)
                                    if part:
                                        mloc = (kc - SKC) - 4 * h5
                                        i = t.matmul(pdst, ident[:],
                                                     masks[:, mloc, :],
                                                     start=False, stop=True)
                                    i.then_inc(sem("t_s"), 1)

                                def av(ei, kc):
                                    t.wait_ge(sem("a_exp"), expop_after[ei])
                                    if ei == att_group_first[g5]:
                                        t.wait_ge(sem("v_at"), g5 * QT5)
                                    rhs_t = (vrect[:, kc, :] if kc < SKC
                                             else vdiag[:, kvh, kc - SKC, :])
                                    i = None
                                    for qt in range(QT5):
                                        i = t.matmul(psBC_att(qt),
                                                     pT[:, ei % 4,
                                                        qt * 128:(qt + 1) * 128],
                                                     rhs_t,
                                                     start=(ei == att_group_first[g5]),
                                                     stop=(ei == att_group_last[g5]))
                                    i.then_inc(sem("t_av"), 1)

                                qk(*ents[0])
                                for z in range(1, len(ents)):
                                    qk(*ents[z])
                                    av(*ents[z - 1])
                                av(*ents[-1])
                                for qt in range(QT5):
                                    qtg = h5 * QT5 + qt
                                    t.wait_ge(sem("v_at"), qh * OTT + qtg + 1)
                                    idx = tpi(("at", qh, qtg))
                                    t.wait_ge(sem("v_pse"), idx - NSLOT + 1)
                                    t.transpose(psd_slot(idx), at_ev[:, qtg % 2, :],
                                                ident[:]).then_inc(sem("t_ps"), 1)
                            for k in head_groups[qh]:
                                wg, j0g, gg, rawcg, rbg = dq_groups[k]
                                rows = gg * 8
                                t.wait_ge(sem("g_sel"), grp_last_call[k])
                                for ic in range(rawcg):
                                    idx = tpi(("dq", k, ic))
                                    t.wait_ge(sem("v_pse"), idx - NSLOT + 1)
                                    t.transpose(
                                        psd_slot(idx, rows),
                                        d_asm.ap()[:, k % 2, ic, 0:gg, :].rearrange(
                                            "p a b -> p (a b)"),
                                        ident[:]).then_inc(sem("t_ps"), 1)

        scAB.close()
        if phases < 5:
            dbg_terminal(3)
            mybir.codegen_inst_isa_subclasses(nc)
            return nc

        OC1 = D // 128
        NPASS = OTT // 2
        with contextlib.ExitStack() as scC2:
            sbC2 = lambda n, sh, dt: scC2.enter_context(nc.sbuf_tensor(n, sh, dt))
            xn2T_sb = sbC2("xn2T_sb", [128, HCH, NQ], BF16)

            with contextlib.ExitStack() as scC:
                sbC = lambda n, sh, dt: scC.enter_context(nc.sbuf_tensor(n, sh, dt))
                atT_p = sbC("atT_p", [128, QDCH, 256], BF16)
                wds = sbC("wds", [128, 2, QDCH, 128], BF16)
                y_sb = sbC("y_sb", [128, 2, D], BF16)
                lnbuf = sbC("lnbuf", [128, D], F32)
                xf_buf = sbC("xf_buf", [128, D], F32)
                x2tile = sbC("x2tile", [128, D], F32)
                xn2_sb = sbC("xn2_sb", [128, D], BF16)
                x1b = sbC("x1b", [128, D], BF16)
                stats2 = sbC("stats2", [128, D // 512, 6], F32)
                mv2 = sbC("mv2", [128, 2], F32)
                rstd2 = sbC("rstd2", [128, 1], F32)
                negm2 = sbC("negm2", [128, 1], F32)

                with nc.Block() as block:
                    @block.sync
                    def _(s):
                        s.wait_ge(sem("st_wdbf"), 16 * QDCH)
                        s.wait_ge(sem("st_dq"), 16 * NGRP)
                        s.wait_ge(sem("st_at0"), 16 * (NH // 2))
                        s.wait_ge(sem("st_at1"), 16 * (NH // 2))
                        for ps in range(NPASS):
                            if ps >= 1:
                                s.wait_ge(sem("t_den"), ps * OC1 * 2)
                            s.dma_start(out=atT_p[:],
                                        in_=attnT_dram[:, ps * 256:(ps + 1) * 256]
                                        .rearrange("(fc p) q -> p fc q", p=128)
                                        ).then_inc(sem("ld_ath"), 16)
                            for oc in range(OC1):
                                gidx = ps * OC1 + oc
                                if gidx >= 2:
                                    s.wait_ge(sem("t_den"), (gidx - 1) * 2)
                                s.dma_start(out=wds[:, gidx % 2, :, :],
                                            in_=wd_bf[:, oc * 128:(oc + 1) * 128]
                                            .rearrange("(hc p) o -> p hc o", p=128)
                                            ).then_inc(sem("ld_wd%d" % (gidx % 2)), 16)
                            for t2 in range(2):
                                row = ps * 2 + t2
                                if row >= 1:
                                    s.wait_ge(sem("v_x1"), row)
                                s.dma_start(out=x2tile[:],
                                            in_=x_own[row * 128:(row + 1) * 128, :]
                                            ).then_inc(sem("ld_x2"), 16)

                    @block.tensor
                    def _(t):
                        for ps in range(NPASS):
                            t.wait_ge(sem("ld_ath"), 16 * (ps + 1))
                            for oc in range(OC1):
                                gidx = ps * OC1 + oc
                                t.wait_ge(sem("ld_wd%d" % (gidx % 2)),
                                          16 * (gidx // 2 + 1))
                                for t2 in range(2):
                                    didx = gidx * 2 + t2
                                    if didx >= 4:
                                        t.wait_ge(sem("v_den"), didx - 3)
                                    sl = didx % 4
                                    pb = psA if sl < 2 else psB
                                    pdst = pb[:, (sl % 2) * 512:(sl % 2) * 512 + 128]
                                    i = None
                                    for fc in range(QDCH):
                                        i = t.matmul(pdst,
                                                     atT_p[:, fc, t2 * 128:(t2 + 1) * 128],
                                                     wds[:, gidx % 2, fc, :],
                                                     start=(fc == 0),
                                                     stop=(fc == QDCH - 1))
                                    i.then_inc(sem("t_den"), 1)
                            for t2 in range(2):
                                row = ps * 2 + t2
                                t.wait_ge(sem("v_xn2"), row + 1)
                                for hc in range(HCH):
                                    idx = tpi(("x2T", row, hc))
                                    t.wait_ge(sem("v_pse"), idx - NSLOT + 1)
                                    t.transpose(psd_slot(idx),
                                                xn2_sb[:, hc * 128:(hc + 1) * 128],
                                                ident[:]).then_inc(sem("t_ps"), 1)

                    @block.vector
                    def _(v):
                        for ps in range(NPASS):
                            for oc in range(OC1):
                                gidx = ps * OC1 + oc
                                for t2 in range(2):
                                    didx = gidx * 2 + t2
                                    v.wait_ge(sem("t_den"), didx + 1)
                                    sl = didx % 4
                                    pb = psA if sl < 2 else psB
                                    v.tensor_copy(
                                        y_sb[:, t2, oc * 128:(oc + 1) * 128],
                                        pb[:, (sl % 2) * 512:(sl % 2) * 512 + 128]
                                    ).then_inc(sem("v_den"), 1)
                            v.drain()
                            for t2 in range(2):
                                row = ps * 2 + t2
                                for cc in range(D // 512):
                                    v.bn_stats(stats2[:, cc, :],
                                               y_sb[:, t2, cc * 512:(cc + 1) * 512])
                                v.drain()
                                v.bn_aggr(mv2[:],
                                          stats2.ap().rearrange("p a b -> p (a b)"))
                                v.drain()
                                v.tensor_scalar(negm2[:], mv2[:, 0:1], -1.0, None,
                                                ALU.mult).then_inc(sem("v_mv2"), 1)
                                v.wait_ge(sem("a_rstd2"), 2 * row + 1)
                                v.reciprocal(rstd2[:], rstd2[:])
                                v.drain()
                                v.tensor_scalar(lnbuf[:], y_sb[:, t2, :], negm2[:],
                                                rstd2[:], ALU.add, ALU.mult)
                                v.drain()
                                v.wait_ge(sem("ld_x2"), 16 * (row + 1))
                                v.tensor_tensor(xf_buf[:], x2tile[:],
                                                lnbuf[:], ALU.add)
                                v.drain()
                                if row >= 1:
                                    v.wait_ge(sem("st_x1"), 16 * row)
                                v.tensor_copy(x1b[:], xf_buf[:]
                                              ).then_inc(sem("v_x1"), 1)
                                for cc in range(D // 512):
                                    v.bn_stats(stats2[:, cc, :],
                                               xf_buf[:, cc * 512:(cc + 1) * 512])
                                v.drain()
                                v.bn_aggr(mv2[:],
                                          stats2.ap().rearrange("p a b -> p (a b)"))
                                v.drain()
                                v.tensor_scalar(negm2[:], mv2[:, 0:1], -1.0, None,
                                                ALU.mult).then_inc(sem("v_mv2"), 1)
                                v.wait_ge(sem("a_rstd2"), 2 * row + 2)
                                v.reciprocal(rstd2[:], rstd2[:])
                                v.drain()
                                v.tensor_scalar(xn2_sb[:], xf_buf[:], negm2[:],
                                                rstd2[:], ALU.add, ALU.mult
                                                ).then_inc(sem("v_xn2"), 1)
                                for hc in range(HCH):
                                    idx = tpi(("x2T", row, hc))
                                    v.wait_ge(sem("t_ps"), idx + 1)
                                    v.tensor_copy(
                                        xn2T_sb[:, hc, row * 128:(row + 1) * 128],
                                        psd_slot(idx)).then_inc(sem("v_pse"), 1)

                    @block.scalar
                    def _(sc):
                        for row in range(OTT):
                            for half in range(2):
                                sc.wait_ge(sem("v_mv2"), 2 * row + half + 1)
                                sc.activation(rstd2[:], mv2[:, 1:2], ACTF.Sqrt,
                                              bias=eps_t[:]).then_inc(sem("a_rstd2"), 1)
                            sc.wait_ge(sem("v_x1"), row + 1)
                            sc.dma_start(out=x1_dram[row * 128:(row + 1) * 128, :],
                                         in_=x1b[:]).then_inc(sem("st_x1"), 16)

            if phases < 6:
                dbg_terminal(5)
                mybir.codegen_inst_isa_subclasses(nc)
                return nc
            with contextlib.ExitStack() as scD:
                sbD = lambda n, sh, dt: scD.enter_context(nc.sbuf_tensor(n, sh, dt))
                IBMAX = max(IBLOCKS)
                gus = sbD("gus", [128, 2, 2, HCH, 256], BF16)
                gs_ev = sbD("gs_ev", [128, 2, 256], BF16)
                h_ev = sbD("h_ev", [128, 2, 256], BF16)
                hT_ib = sbD("hT_ib", [128, IBMAX, NQ], BF16)
                wdt = sbD("wdt", [128, 2, 512], BF16)
                y2row = sbD("y2row", [128, 2, 512], BF16)
                first_gu_of_ib = {}
                for (gidx, ib, ri, ic0, c0, r, tt) in gu_plan:
                    if ib not in first_gu_of_ib:
                        first_gu_of_ib[ib] = gidx
                last_hT_of_gidx = {g0: ("hT", ib, c0 + r - 1, tt)
                                   for (g0, ib, ri, ic0, c0, r, tt) in gu_plan}

                with nc.Block() as block:
                    @block.sync
                    def _(s):
                        s.wait_ge(sem("cc_w"), 2)
                        s.wait_ge(sem("st_x1"), 16 * OTT)
                        dn = 0
                        NTH = max(1, OTT // 4)
                        for ib in range(NIB):
                            icb = sum(IBLOCKS[:ib])
                            nch = IBLOCKS[ib]
                            for ri, (ib2, run_i, ic0, c0, r) in enumerate(ffn_runs):
                                if ib2 != ib:
                                    continue
                                if ri >= 2:
                                    s.wait_ge(sem("t_gu"), (ri - 1) * OTT)
                                s.dma_start(out=gus[:, ri % 2, 0, :, 0:r * 128],
                                            in_=ag_out["g"][:, ic0 * 128:(ic0 + r) * 128]
                                            .rearrange("(hc p) i -> p hc i", p=128)
                                            ).then_inc(sem("ld_gu%d" % (ri % 2)), 16)
                                s.dma_start(out=gus[:, ri % 2, 1, :, 0:r * 128],
                                            in_=ag_out["u"][:, ic0 * 128:(ic0 + r) * 128]
                                            .rearrange("(hc p) i -> p hc i", p=128)
                                            ).then_inc(sem("ld_gu%d" % (ri % 2)), 16)
                            if ib == 0:
                                s.wait_ge(sem("cc_w"), 3)
                            for oc in range(OCD):
                                for th in range(NTH):
                                    for icr in range(nch):
                                        if dn >= 2:
                                            s.wait_ge(sem("t_dn"), dn - 1)
                                        s.dma_start(
                                            out=wdt[:, dn % 2, :],
                                            in_=ag_out["d"][(icb + icr) * 128:
                                                            (icb + icr + 1) * 128,
                                                            oc * 512:(oc + 1) * 512]
                                        ).then_inc(sem("ld_dn%d" % (dn % 2)), 16)
                                        dn += 1

                    @block.tensor
                    def _(t):
                        t.wait_ge(sem("v_pse"), tpi(("x2T", OTT - 1, HCH - 1)) + 1)
                        t.wait_ge(sem("v_den"), (D // 128) * OTT)
                        dn = 0
                        for ib in range(NIB):
                            nch = IBLOCKS[ib]
                            for (gidx, ib2, ri, ic0, c0, r, tt) in gu_plan:
                                if ib2 != ib:
                                    continue
                                if tt == 0:
                                    t.wait_ge(sem("ld_gu%d" % (ri % 2)),
                                              32 * (ri // 2 + 1))
                                if gidx >= 2:
                                    t.wait_ge(sem("v_h"), gidx - 1)
                                if ib >= 1 and gidx == first_gu_of_ib[ib]:
                                    t.wait_ge(sem("v_dn"), ib * OCD * OTT)
                                pd = psum_ab(gidx)
                                i = None
                                for hc in range(HCH):
                                    i = t.matmul(pd[:, 0:r * 128],
                                                 xn2T_sb[:, hc, tt * 128:(tt + 1) * 128],
                                                 gus[:, ri % 2, 0, hc, 0:r * 128],
                                                 start=(hc == 0), stop=(hc == HCH - 1))
                                for hc in range(HCH):
                                    i = t.matmul(pd[:, 512:512 + r * 128],
                                                 xn2T_sb[:, hc, tt * 128:(tt + 1) * 128],
                                                 gus[:, ri % 2, 1, hc, 0:r * 128],
                                                 start=(hc == 0), stop=(hc == HCH - 1))
                                i.then_inc(sem("t_gu"), 1)
                                t.wait_ge(sem("v_h"), gidx + 1)
                                for rc in range(r):
                                    idx = tpi(("hT", ib, c0 + rc, tt))
                                    t.wait_ge(sem("v_pse"), idx - NSLOT + 1)
                                    t.transpose(psd_slot(idx),
                                                h_ev[:, gidx % 2, rc * 128:(rc + 1) * 128],
                                                ident[:]).then_inc(sem("t_ps"), 1)
                            t.wait_ge(sem("v_pse"), tpi(("hT", ib, nch - 1, OTT - 1)) + 1)
                            NTH = max(1, OTT // 4)
                            for oc in range(OCD):
                                for th in range(NTH):
                                    gdi = (ib * OCD + oc) * NTH + th
                                    if gdi >= 1:
                                        t.wait_ge(sem("v_dn"), gdi * 4)
                                    for icr in range(nch):
                                        t.wait_ge(sem("ld_dn%d" % (dn % 2)),
                                                  16 * (dn // 2 + 1))
                                        i = None
                                        for tq in range(4):
                                            tt = th * 4 + tq
                                            i = t.matmul(psBC_slot256(tq),
                                                         hT_ib[:, icr,
                                                               tt * 128:(tt + 1) * 128],
                                                         wdt[:, dn % 2, :],
                                                         start=(icr == 0),
                                                         stop=(icr == nch - 1))
                                        i.then_inc(sem("t_dn"), 1)
                                        dn += 1

                    @block.scalar
                    def _(sc):
                        for ib in range(NIB):
                            for (gidx, ib2, ri, ic0, c0, r, tt) in gu_plan:
                                if ib2 != ib:
                                    continue
                                sc.wait_ge(sem("t_gu"), gidx + 1)
                                if gidx >= 2:
                                    sc.wait_ge(sem("v_h"), gidx - 1)
                                sc.activation(gs_ev[:, gidx % 2, 0:r * 128],
                                              psum_ab(gidx)[:, 0:r * 128],
                                              ACTF.Silu).then_inc(sem("a_silu"), 1)
                            NTH = max(1, OTT // 4)
                            for oc in range(OCD):
                                for th in range(NTH):
                                    for tq in range(4):
                                        tt = th * 4 + tq
                                        di = ((ib * OCD + oc) * NTH + th) * 4 + tq
                                        sc.wait_ge(sem("v_dn"), di + 1)
                                        sc.dma_start(
                                            out=y2p[ib, tt * 128:(tt + 1) * 128,
                                                    oc * 512:(oc + 1) * 512],
                                            in_=y2row[:, di % 2, :]
                                            ).then_inc(sem("st_y2%d" % (di % 2)), 16)

                    @block.vector
                    def _(v):
                        for ib in range(NIB):
                            nch = IBLOCKS[ib]
                            for (gidx, ib2, ri, ic0, c0, r, tt) in gu_plan:
                                if ib2 != ib:
                                    continue
                                v.wait_ge(sem("a_silu"), gidx + 1)
                                if gidx >= 2:
                                    v.wait_ge(sem("t_ps"),
                                              tpi(last_hT_of_gidx[gidx - 2]) + 1)
                                v.tensor_tensor(h_ev[:, gidx % 2, 0:r * 128],
                                                psum_ab(gidx)[:, 512:512 + r * 128],
                                                gs_ev[:, gidx % 2, 0:r * 128],
                                                ALU.mult).then_inc(sem("v_h"), 1)
                                for rc in range(r):
                                    idx = tpi(("hT", ib, c0 + rc, tt))
                                    v.wait_ge(sem("t_ps"), idx + 1)
                                    v.tensor_copy(
                                        hT_ib[:, c0 + rc, tt * 128:(tt + 1) * 128],
                                        psd_slot(idx)).then_inc(sem("v_pse"), 1)
                            NTH = max(1, OTT // 4)
                            for oc in range(OCD):
                                for th in range(NTH):
                                    for tq in range(4):
                                        tt = th * 4 + tq
                                        di = ((ib * OCD + oc) * NTH + th) * 4 + tq
                                        v.wait_ge(sem("t_dn"),
                                                  (sum(IBLOCKS[:ib]) * OCD * NTH
                                                   + (oc * NTH + th + 1) * nch))
                                        if di >= 2:
                                            v.wait_ge(sem("st_y2%d" % (di % 2)),
                                                      16 * (di // 2))
                                        v.tensor_copy(y2row[:, di % 2, :],
                                                      psBC_slot256(tq)
                                                      ).then_inc(sem("v_dn"), 1)

        if phases < 7:
            dbg_terminal(6)
            mybir.codegen_inst_isa_subclasses(nc)
            return nc
        with contextlib.ExitStack() as scE:
            sbE = lambda n, sh, dt: scE.enter_context(nc.sbuf_tensor(n, sh, dt))
            fin = sbE("fin", [128, NIB, D], BF16)
            x1f = sbE("x1f", [128, D], BF16)
            acc = sbE("acc", [128, D], F32)
            acc2 = sbE("acc2", [128, D], F32)
            outb = sbE("outb", [128, 2, D], F32)
            stats3 = sbE("stats3", [128, D // 512, 6], F32)
            mv3 = sbE("mv3", [128, 2], F32)
            rstd3 = sbE("rstd3", [128, 1], F32)
            negm3 = sbE("negm3", [128, 1], F32)

            with nc.Block() as block:
                @block.sync
                def _(s):
                    s.wait_ge(sem("st_y20"), 16 * (NIB * OCD * OTT // 2))
                    s.wait_ge(sem("st_y21"), 16 * (NIB * OCD * OTT - NIB * OCD * OTT // 2))
                    s.wait_ge(sem("st_x1"), 16 * OTT)
                    for tt in range(OTT):
                        if tt >= 1:
                            s.wait_ge(sem("v_fin"), tt)
                        for ib in range(NIB):
                            s.dma_start(out=fin[:, ib, :],
                                        in_=y2p[ib, tt * 128:(tt + 1) * 128, :]
                                        ).then_inc(sem("ld_fin"), 16)
                        s.dma_start(out=x1f[:],
                                    in_=x1_dram[tt * 128:(tt + 1) * 128, :]
                                    ).then_inc(sem("ld_fin"), 16)

                @block.vector
                def _(v):
                    for tt in range(OTT):
                        v.wait_ge(sem("ld_fin"), 16 * (NIB + 1) * (tt + 1))
                        if NIB >= 2:
                            v.tensor_tensor(acc[:], fin[:, 0, :], fin[:, 1, :], ALU.add)
                        else:
                            v.tensor_copy(acc[:], fin[:, 0, :])
                        v.drain()
                        cur = acc
                        for ib in range(2, NIB):
                            dst = acc2 if cur is acc else acc
                            v.tensor_tensor(dst[:], cur[:], fin[:, ib, :], ALU.add)
                            v.drain()
                            cur = dst
                        for cc in range(D // 512):
                            v.bn_stats(stats3[:, cc, :], cur[:, cc * 512:(cc + 1) * 512])
                        v.drain()
                        v.bn_aggr(mv3[:], stats3.ap().rearrange("p a b -> p (a b)"))
                        v.drain()
                        v.tensor_scalar(negm3[:], mv3[:, 0:1], -1.0, None, ALU.mult
                                        ).then_inc(sem("v_mv3"), 1)
                        v.wait_ge(sem("a_rstd3"), tt + 1)
                        v.reciprocal(rstd3[:], rstd3[:])
                        v.drain()
                        other = acc2 if cur is acc else acc
                        v.tensor_scalar(other[:], cur[:], negm3[:], rstd3[:],
                                        ALU.add, ALU.mult)
                        v.drain()
                        if tt >= 2:
                            v.wait_ge(sem("st_out%d" % (tt % 2)), 16 * (tt // 2))
                        v.tensor_tensor(outb[:, tt % 2, :], other[:], x1f[:],
                                        ALU.add).then_inc(sem("v_fin"), 1)

                @block.scalar
                def _(sc):
                    for tt in range(OTT):
                        sc.wait_ge(sem("v_mv3"), tt + 1)
                        sc.activation(rstd3[:], mv3[:, 1:2], ACTF.Sqrt,
                                      bias=eps_t[:]).then_inc(sem("a_rstd3"), 1)
                        sc.wait_ge(sem("v_fin"), tt + 1)
                        sc.dma_start(out=out[tt * 128:(tt + 1) * 128, :],
                                     in_=outb[:, tt % 2, :]
                                     ).then_inc(sem("st_out%d" % (tt % 2)), 16)
                    sc.wait_ge(sem("st_out0"), 16 * (OTT - OTT // 2))
                    sc.wait_ge(sem("st_out1"), 16 * (OTT // 2))

    mybir.codegen_inst_isa_subclasses(nc)
    return nc



def _wrap16x2(flat):
    w = flat.reshape(-1, 16).T
    w32 = np.concatenate([w, w], axis=0)
    return np.ascontiguousarray(
        np.vstack(np.split(w32, 4, axis=1)))


def _wrap128(flat):
    return np.ascontiguousarray(flat.reshape(-1, 128).T)


def make_in_maps(cfg, x, w_qkv, w_dense, codebook, idx_gate, idx_up, idx_down):
    C = derive(cfg)
    D, CBN = C["D"], C["CBN"]
    OWN, SKC, GJ, DJ = C["OWN"], C["SKC"], C["GJ"], C["DJ"]
    wqkvT = np.ascontiguousarray(np.asarray(w_qkv, dtype=np.float32).T)
    wdenT = np.ascontiguousarray(np.asarray(w_dense, dtype=np.float32).T)
    cbp = np.zeros((CBN // 2, 64), dtype=np.float32)
    cbp[:, :16] = np.asarray(codebook, dtype=np.float32).reshape(CBN // 2, 16)
    x = np.asarray(x, dtype=np.float32)
    idx_gate = np.asarray(idx_gate, dtype=np.int32)
    idx_up = np.asarray(idx_up, dtype=np.int32)
    idx_down = np.asarray(idx_down, dtype=np.int32)

    in_maps = []
    for c in range(NCORES):
        b, q = c // 4, c % 4
        xo = np.ascontiguousarray(x[b, q * OWN:(q + 1) * OWN, :])
        kk = np.arange(128)[:, None] + 128 * np.arange(SKC)[None, :]
        valid = (kk < OWN * q).astype(np.float32)
        pp = np.arange(128)[:, None]
        ff = np.arange(512)[None, :]
        maskin = np.stack([np.where(pp <= ff - 128 * mm, 0.0, -10000.0)
                           for mm in range(4)],
                          axis=1).astype(ml_dtypes.bfloat16)
        m = {"x_own": xo, "wqkvT": wqkvT, "wdenT": wdenT, "cbp": cbp,
             "validin": np.ascontiguousarray(valid),
             "maskin": np.ascontiguousarray(maskin)}
        gsl = idx_gate[:, GJ * c:GJ * (c + 1)].T.reshape(-1)
        usl = idx_up[:, GJ * c:GJ * (c + 1)].T.reshape(-1)
        dsl = idx_down[:, DJ * c:DJ * (c + 1)].T.reshape(-1)
        m["idxg16"], m["idxg128"] = _wrap16x2(gsl), _wrap128(gsl)
        m["idxu16"], m["idxu128"] = _wrap16x2(usl), _wrap128(usl)
        m["idxd16"], m["idxd128"] = _wrap16x2(dsl), _wrap128(dsl)
        in_maps.append(m)
    return in_maps


def kernel(L0_attn_q, L0_attn_w_qkv, L0_attn_w_dense, L0_ffn_codebook,
           L0_ffn_idx_gate, L0_ffn_idx_up, L0_ffn_idx_down):
    cfg = FULL
    C = derive(cfg)
    nc = build_program(cfg)
    in_maps = make_in_maps(cfg, L0_attn_q, L0_attn_w_qkv, L0_attn_w_dense,
                           L0_ffn_codebook, L0_ffn_idx_gate, L0_ffn_idx_up,
                           L0_ffn_idx_down)
    res = run_bass_kernel_spmd(nc, in_maps, core_ids=list(range(NCORES)))
    OWN, S, D = C["OWN"], C["S"], C["D"]
    full = np.empty((cfg["B"], S, D), dtype=np.float32)
    for c in range(NCORES):
        b, q = c // 4, c % 4
        full[b, q * OWN:(q + 1) * OWN, :] = np.asarray(
            res.results[c]["out"]).reshape(OWN, D)
    return full


# version 12
# speedup vs baseline: 1.0853x; 1.0796x over previous
import contextlib
import numpy as np
import ml_dtypes

import concourse.bass as bass
import concourse.mybir as mybir
from concourse.bass_utils import run_bass_kernel_spmd
from concourse.library_config import mlp as LIB_MLP

F32 = mybir.dt.float32
BF16 = mybir.dt.bfloat16
I32 = mybir.dt.int32
I16 = mybir.dt.int16
ALU = mybir.AluOpType
ACTF = mybir.ActivationFunctionType

NCORES = 8
FULL = dict(B=2, S=4096, D=4096, KVH=8, G=4, IFF=11008, CBN=65536)
MINI = dict(B=2, S=2048, D=1024, KVH=2, G=2, IFF=2048, CBN=512)
EXP_SHIFT = -30.0
LN_EPS = 1e-5


def derive(cfg):
    c = dict(cfg)
    c["NH"] = c["KVH"] * c["G"]
    c["QD"] = c["NH"] * 128
    c["KVD"] = c["KVH"] * 128
    c["QKVD"] = c["QD"] + 2 * c["KVD"]
    c["OWN"] = c["S"] // 4
    c["NQ"] = c["OWN"]
    c["HCH"] = c["D"] // 128
    c["QDCH"] = c["QD"] // 128
    c["OTT"] = c["OWN"] // 128
    c["SKC"] = c["S"] // 128
    c["OKC"] = c["OWN"] // 128
    c["ICH"] = c["IFF"] // 128
    c["NQH"] = max(1, c["NQ"] // 512)
    c["GJ"] = c["D"] // 8 // NCORES
    c["DJ"] = c["IFF"] // 8 // NCORES
    assert c["GJ"] * 8 * NCORES == c["D"] and c["DJ"] * 8 * NCORES == c["IFF"]
    ich = c["ICH"]
    nb = max(1, (ich + 21) // 22)
    base, rem = ich // nb, ich % nb
    c["IBLOCKS"] = [base + (1 if i < rem else 0) for i in range(nb)]
    return c


def runs_of(n, r):
    out = []
    while n > 0:
        out.append(min(r, n))
        n -= out[-1]
    return out


def build_program(cfg, phases=9, dbg_no_prep=False):
    C = derive(cfg)
    S, D = C["S"], C["D"]
    KVH, G, NH = C["KVH"], C["G"], C["NH"]
    QD, KVD, QKVD, QDCH = C["QD"], C["KVD"], C["QKVD"], C["QDCH"]
    OWN, NQ, HCH, OTT = C["OWN"], C["NQ"], C["HCH"], C["OTT"]
    SKC, OKC, ICH, IFF = C["SKC"], C["OKC"], C["ICH"], C["IFF"]
    NQH, GJ, DJ, CBN = C["NQH"], C["GJ"], C["DJ"], C["CBN"]
    IBLOCKS = C["IBLOCKS"]
    NIB = len(IBLOCKS)
    KC_ALL = SKC + OKC
    SCALE = float(1.0 / np.sqrt(128.0))
    OCD = D // 512
    NQ2 = NQ // 2
    NROW = NQ // 512

    dq_calls = []
    for w, nj, nidx in (("g", GJ, IFF), ("u", GJ, IFF)):
        for j in range(nj):
            dq_calls.append((w, j, nidx, nidx // 128))
    for j in range(DJ):
        dq_calls.append(("d", j, D, D // 128))
    NDQ = len(dq_calls)
    GMAX = 1024
    dq_chunks = []
    for w, j, nidx, rawc in dq_calls:
        chs, r0 = [], 0
        while r0 < rawc:
            crawc = min(GMAX // 128, rawc - r0)
            chs.append((r0, crawc))
            r0 += crawc
        dq_chunks.append(chs)
    NGQ = 4
    DQR = 2
    call_qnums = []
    _gch = 0
    for ci in range(NDQ):
        qs = []
        for _ in dq_chunks[ci]:
            qs.append(_gch % NGQ)
            _gch += 1
        call_qnums.append(qs)
    _qr_cum = {}
    call_qwaits = []
    for ci in range(NDQ):
        r = ci % DQR
        per_q = {}
        for q in call_qnums[ci]:
            _qr_cum[(q, r)] = _qr_cum.get((q, r), 0) + 1
            per_q[q] = _qr_cum[(q, r)]
        call_qwaits.append([(q, r, c) for q, c in sorted(per_q.items())])
    gqr_tot = dict(_qr_cum)
    dq_groups = []
    for w, nj, rawc in (("g", GJ, IFF // 128), ("u", GJ, IFF // 128),
                        ("d", DJ, D // 128)):
        j0 = 0
        for gg in runs_of(nj, 16):
            dq_groups.append((w, j0, gg, rawc, j0 * 8))
            j0 += gg
    NGRP = len(dq_groups)
    grp_last_call = []
    ci0 = 0
    for w, j0, gg, rawc, rb in dq_groups:
        ci0 += gg
        grp_last_call.append(ci0)
    call_group = []
    for k, (w, j0, gg, rawc, rb) in enumerate(dq_groups):
        call_group += [k] * gg
    call_head = [min(NH - 1, (ci * NH) // NDQ) for ci in range(NDQ)]
    grp_head = [call_head[grp_last_call[k] - 1] for k in range(NGRP)]
    for _k in range(2, NGRP):
        assert call_head[grp_last_call[_k - 1]] > grp_head[_k - 2], \
            "dq schedule too tight for double-buffered d_asm"
    head_calls = {h: [ci for ci in range(NDQ) if call_head[ci] == h]
                  for h in range(NH)}
    head_groups = {h: [k for k in range(NGRP) if grp_head[k] == h]
                   for h in range(NH)}
    NQH5 = NQ // 512
    QT5 = 4
    att_sched = []
    att_pm = {}
    att_group_first = {}
    att_group_last = {}
    head_last_entry = {}
    for qh_ in range(NH):
        for h5_ in range(NQH5):
            g5 = qh_ * NQH5 + h5_
            for kc_ in range(KC_ALL):
                if kc_ < SKC:
                    cat = "rect"
                else:
                    m_ = (kc_ - SKC) - 4 * h5_
                    if m_ < 0:
                        cat = "full"
                    elif m_ <= 3:
                        cat = "part"
                    else:
                        continue
                ei = len(att_sched)
                if cat == "part":
                    att_pm[ei] = len(att_pm)
                if g5 not in att_group_first:
                    att_group_first[g5] = ei
                att_group_last[g5] = ei
                att_sched.append((qh_, h5_, kc_, cat))
        head_last_entry[qh_] = len(att_sched)
    att_entry = {}
    for ei, (qh_, h5_, kc_, cat) in enumerate(att_sched):
        att_entry[(qh_, h5_, kc_)] = ei
    exp_ops = []
    expop_after = {}
    for qh_ in range(NH):
        for h5_ in range(NQH5):
            ents5_ = [att_entry[(qh_, h5_, kc_)] for kc_ in range(KC_ALL)
                      if (qh_, h5_, kc_) in att_entry]
            z_ = 0
            while z_ < len(ents5_):
                ei_ = ents5_[z_]
                n_ = 2 if (ei_ % 2 == 0 and z_ + 1 < len(ents5_)) else 1
                exp_ops.append((ei_, n_))
                for e2 in range(ei_, ei_ + n_):
                    expop_after[e2] = len(exp_ops)
                z_ += n_

    ag_issue_after = {"g": 2 * GJ, "u": 2 * GJ + DJ // 2, "d": NDQ}
    ngrp_w = {w: len([1 for gw in dq_groups if gw[0] == w]) for w in ("g", "u", "d")}

    NSLOT = 2
    tp_order = []
    tp_index = {}

    def tp_add(key):
        tp_index[key] = len(tp_order)
        tp_order.append(key)

    for tt in range(OTT):
        for hc in range(HCH):
            tp_add(("xnT", tt, hc))
    for j in range(2 * KVH * OTT):
        tp_add(("kvp", j))
    for qh in range(NH):
        for qt in range(OTT):
            tp_add(("at", qh, qt))
        for k in head_groups[qh]:
            for ic in range(dq_groups[k][3]):
                tp_add(("dq", k, ic))
    for tt in range(OTT):
        for hc in range(HCH):
            tp_add(("x2T", tt, hc))
    ffn_runs = []
    for ib in range(NIB):
        icb = sum(IBLOCKS[:ib])
        c0 = 0
        for run_i, r in enumerate(runs_of(IBLOCKS[ib], 2)):
            ffn_runs.append((ib, run_i, icb + c0, c0, r))
            c0 += r
    gu_plan = []
    for ri, (ib, run_i, ic0, c0, r) in enumerate(ffn_runs):
        for tt in range(OTT):
            gu_plan.append((len(gu_plan), ib, ri, ic0, c0, r, tt))
    for (gidx, ib, ri, ic0, c0, r, tt) in gu_plan:
        for rc in range(r):
            tp_add(("hT", ib, c0 + rc, tt))

    def tpi(key):
        return tp_index[key]

    nc = bass.Bass("TRN2", target_bir_lowering=False, num_swdge_queues=NGQ)

    x_own = nc.dram_tensor("x_own", [OWN, D], F32, kind="ExternalInput").ap()
    wqkvT = nc.dram_tensor("wqkvT", [D, QKVD], F32, kind="ExternalInput").ap()
    wdenT = nc.dram_tensor("wdenT", [QD, D], F32, kind="ExternalInput").ap()
    cbp = nc.dram_tensor("cbp", [CBN // 2, 64], F32, kind="ExternalInput").ap()
    idx16 = {}
    idx128 = {}
    for w, nj, nidx in (("g", GJ, IFF), ("u", GJ, IFF), ("d", DJ, D)):
        idx16[w] = nc.dram_tensor("idx%s16" % w, [128, nj * nidx // 64], I32,
                                  kind="ExternalInput").ap()
        idx128[w] = nc.dram_tensor("idx%s128" % w, [128, nj * nidx // 128], I32,
                                   kind="ExternalInput").ap()
    TOT16 = (2 * GJ * IFF + DJ * D) // 16
    TOTL = (2 * GJ * IFF + DJ * D) // 128
    hi_dram = nc.dram_tensor("hi_dram", [128, TOT16], I16).ap()
    lo_dram = nc.dram_tensor("lo_dram", [128, TOTL], F32).ap()
    off16 = []
    offL = []
    o16 = oL = 0
    for w, j, nidx, rawc in dq_calls:
        off16.append(o16)
        offL.append(oL)
        o16 += nidx // 16
        oL += rawc
    PCH = 1024
    prep_chunks = []
    wo16 = 0
    for w, nj, nidx in (("g", GJ, IFF), ("u", GJ, IFF), ("d", DJ, D)):
        f4 = nj * nidx // 64
        c0 = 0
        for cn in runs_of(f4, PCH):
            prep_chunks.append(("hi", w, c0, cn, wo16))
            c0 += cn
        wo16 += nj * nidx // 16
    woL = 0
    for w, nj, nidx in (("g", GJ, IFF), ("u", GJ, IFF), ("d", DJ, D)):
        fl = nj * nidx // 128
        c0 = 0
        for cn in runs_of(fl, PCH):
            prep_chunks.append(("lo", w, c0, cn, woL))
            c0 += cn
        woL += fl
    if dbg_no_prep:
        prep_chunks = []
    NPREP = len(prep_chunks)
    pst_after = []
    acc = {0: 0, 1: 0}
    for k, (kind, w, c0, cn, doff) in enumerate(prep_chunks):
        acc[k % 2] += 256 if kind == "hi" else 16
        pst_after.append(dict(acc))
    validin = nc.dram_tensor("validin", [128, SKC], F32, kind="ExternalInput").ap()
    maskin = nc.dram_tensor("maskin", [128, 4, 512], BF16, kind="ExternalInput").ap()
    out = nc.dram_tensor("out", [OWN, D], F32, kind="ExternalOutput").ap()

    wq_bf = nc.dram_tensor("wq_bf", [D, QKVD], BF16).ap()
    wd_bf = nc.dram_tensor("wd_bf", [QD, D], BF16).ap()
    qT_dram = nc.dram_tensor("qT_dram", [QD, NQ], BF16).ap()
    k_own = nc.dram_tensor("k_own", [OWN, KVD], BF16).ap()
    v_own = nc.dram_tensor("v_own", [OWN, KVD], BF16).ap()
    k_all = nc.dram_tensor("k_all", [S, KVD], BF16).ap()
    v_all = nc.dram_tensor("v_all", [S, KVD], BF16).ap()
    attnT_dram = nc.dram_tensor("attnT_dram", [QD, NQ], BF16).ap()
    x1_dram = nc.dram_tensor("x1_dram", [OWN, D], BF16).ap()
    ag_in = {"g": nc.dram_tensor("g_in", [D // 8, IFF], BF16).ap(),
             "u": nc.dram_tensor("u_in", [D // 8, IFF], BF16).ap(),
             "d": nc.dram_tensor("d_in", [IFF // 8, D], BF16).ap()}
    ag_out = {"g": nc.dram_tensor("gT_all", [D, IFF], BF16).ap(),
              "u": nc.dram_tensor("uT_all", [D, IFF], BF16).ap(),
              "d": nc.dram_tensor("dT_all", [IFF, D], BF16).ap()}
    y2p = nc.dram_tensor("y2p", [NIB, OWN, D], BF16).ap()

    stack = contextlib.ExitStack()
    sem_tab = {}

    def sem(name):
        if name not in sem_tab:
            sem_tab[name] = stack.enter_context(nc.semaphore(name))
        return sem_tab[name]

    def sbuf(name, shape, dt):
        return stack.enter_context(nc.sbuf_tensor(name, shape, dt))

    with stack:
        psA = stack.enter_context(nc.psum_tensor("psA", [128, 1024], F32))
        psB = stack.enter_context(nc.psum_tensor("psB", [128, 1024], F32))
        psC = stack.enter_context(nc.psum_tensor("psC", [128, 1024], F32))
        psD2 = stack.enter_context(nc.psum_tensor("psD2", [128, 128], BF16))
        psD = stack.enter_context(nc.psum_tensor("psD", [128, 128], BF16))

        def psum_ab(par):
            return psA if par % 2 == 0 else psB

        def psd_slot(idx, rows=128):
            if idx % 2 == 0:
                return psD[0:rows, 0:128]
            return psD2[0:rows, 0:128]

        def psBC_slot256(t):
            assert t < 4
            if t < 2:
                return psB[:, t * 512:t * 512 + 512]
            return psC[:, (t - 2) * 512:(t - 2) * 512 + 512]

        def psBC_att(qt):
            assert qt < 4
            if qt < 2:
                return psB[:, qt * 512:qt * 512 + 129]
            return psC[:, (qt - 2) * 512:(qt - 2) * 512 + 129]

        ident = sbuf("ident", [128, 128], BF16)
        eps_t = sbuf("eps_t", [128, 1], F32)
        ebias = sbuf("ebias", [128, 1], F32)
        scAB = stack.enter_context(contextlib.ExitStack())
        sbAB = lambda n, sh, dt: scAB.enter_context(nc.sbuf_tensor(n, sh, dt))
        s_valid = sbAB("s_valid", [128, SKC], F32)
        masks = sbAB("masks", [128, 4, 512], BF16)
        kT_own = sbAB("kT_own", [128, KVH, OWN], BF16)
        vdiag = sbAB("vdiag", [128, KVH, OKC, 129], BF16)


        def dbg_terminal(level):
            with nc.Block() as blk:
                @blk.scalar
                def _(sc):
                    sc.wait_ge(sem("st_wqbf"), 16 * HCH)
                    sc.wait_ge(sem("st_wdbf"), 16 * QDCH)
                    sc.wait_ge(sem("st_q0"), 16 * ((QDCH + 1) // 2))
                    sc.wait_ge(sem("st_q1"), 16 * (QDCH // 2))
                    sc.wait_ge(sem("st_kv0"), 16 * KVH * OTT)
                    sc.wait_ge(sem("st_kv1"), 16 * KVH * OTT)
                    if NPREP:
                        sc.wait_ge(sem("g_pst0"), pst_after[NPREP - 1][0])
                        sc.wait_ge(sem("g_pst1"), pst_after[NPREP - 1][1])
                    sc.wait_ge(sem("ld_valid"), 32)
                    if level >= 3:
                        sc.wait_ge(sem("st_at0"), 16 * (NH // 2))
                        sc.wait_ge(sem("st_at1"), 16 * (NH // 2))
                        sc.wait_ge(sem("st_dq"), 16 * NGRP)
                        sc.wait_ge(sem("cc_w"), 3)
                        sc.wait_ge(sem("cc_kv"), 2)
                        for (qq, rr), tt2 in sorted(gqr_tot.items()):
                            sc.wait_ge(sem("g_gq%d_%d" % (qq, rr)), 16 * tt2)
                    if level >= 5:
                        sc.wait_ge(sem("st_x1"), 16 * OTT)
                    if level >= 6:
                        sc.wait_ge(sem("st_y20"), 16 * (NIB * OCD * OTT // 2))
                        sc.wait_ge(sem("st_y21"),
                                   16 * (NIB * OCD * OTT - NIB * OCD * OTT // 2))

        with nc.Block() as block:
            @block.gpsimd
            def _(g):
                g.load_library(LIB_MLP)
                g.memset(ident[:], 1.0)
                g.memset(eps_t[:], LN_EPS)
                g.memset(ebias[:], EXP_SHIFT)
                g.memset(vdiag[:], 1.0)
                g.drain()
                g.affine_select(ident[:], ident[:], [[-1, 128]], ALU.is_equal,
                                0.0, base=0, channel_multiplier=1)

            @block.sync
            def _(s):
                s.dma_start(out=s_valid[:], in_=validin[:]).then_inc(sem("ld_valid"), 16)
                s.dma_start(out=masks[:], in_=maskin[:]).then_inc(sem("ld_valid"), 16)

        with contextlib.ExitStack() as scA:
            sbA = lambda n, sh, dt: scA.enter_context(nc.sbuf_tensor(n, sh, dt))
            x_sb = sbA("x_sb", [128, 2, D], F32)
            stats = sbA("stats", [128, D // 512, 6], F32)
            mv = sbA("mv", [128, 2, 2], F32)
            rstd = sbA("rstd", [128, 2, 1], F32)
            negm = sbA("negm", [128, 2, 1], F32)
            xn_sb = sbA("xn_sb", [128, 2, D], BF16)
            xnT_sb = sbA("xnT_sb", [128, HCH, NQ], BF16)
            wqs = sbA("wqs", [128, 2, HCH, 128], BF16)
            qkv_ev = sbA("qkv_ev", [128, 2, NQ], BF16)
            vT_tmp = sbA("vT_tmp", [128, KVH, OWN], BF16)
            kvp = sbA("kvp", [128, 2, 128], BF16)
            pin32 = sbA("pin32", [128, 2, 1024], I32)
            pout16 = sbA("pout16", [128, 2, 1024], I16)
            poutf = sbA("poutf", [128, 2, 1024], F32)

            with nc.Block() as block:
                @block.gpsimd
                def _(g):
                    for hc in range(HCH):
                        g.dma_start(out=wq_bf[hc * 128:(hc + 1) * 128, :],
                                    in_=wqkvT[hc * 128:(hc + 1) * 128, :]
                                    ).then_inc(sem("st_wqbf"), 16)
                    for hc in range(QDCH):
                        g.dma_start(out=wd_bf[hc * 128:(hc + 1) * 128, :],
                                    in_=wdenT[hc * 128:(hc + 1) * 128, :]
                                    ).then_inc(sem("st_wdbf"), 16)
                    for k, (kind, w, c0, cn, doff) in enumerate(prep_chunks):
                        if k >= 2:
                            g.wait_ge(sem("v_prep"), k - 1)
                        src = idx16[w] if kind == "hi" else idx128[w]
                        g.dma_start(out=pin32[:, k % 2, 0:cn],
                                    in_=src[:, c0:c0 + cn]).then_inc(sem("g_pld"), 16)
                        g.wait_ge(sem("v_prep"), k + 1)
                        if kind == "hi":
                            f4w = {"g": GJ * IFF // 64, "u": GJ * IFF // 64,
                                   "d": DJ * D // 64}[w]
                            for i4 in range(4):
                                for b4 in range(4):
                                    g.dma_start(
                                        out=hi_dram[b4 * 32:(b4 + 1) * 32,
                                                    doff + i4 * f4w + c0:
                                                    doff + i4 * f4w + c0 + cn],
                                        in_=pout16[i4 * 32:(i4 + 1) * 32, k % 2, 0:cn]
                                    ).then_inc(sem("g_pst%d" % (k % 2)), 16)
                        else:
                            g.dma_start(out=lo_dram[:, doff + c0:doff + c0 + cn],
                                        in_=poutf[:, k % 2, 0:cn]
                                        ).then_inc(sem("g_pst%d" % (k % 2)), 16)

                @block.sync
                def _(s):
                    for tt in range(OTT):
                        if tt >= 2:
                            s.wait_ge(sem("v_xn"), tt - 1)
                        s.dma_start(out=x_sb[:, tt % 2, :],
                                    in_=x_own[tt * 128:(tt + 1) * 128, :]
                                    ).then_inc(sem("ld_x%d" % (tt % 2)), 16)
                    s.wait_ge(sem("st_wqbf"), 16 * HCH)
                    for oc in range(QKVD // 128):
                        if oc >= 2:
                            s.wait_ge(sem("t_qkv"), oc - 1)
                        s.dma_start(out=wqs[:, oc % 2, :, :],
                                    in_=wq_bf[:, oc * 128:(oc + 1) * 128].rearrange(
                                        "(hc p) o -> p hc o", p=128)
                                    ).then_inc(sem("ld_wq%d" % (oc % 2)), 16)

                @block.scalar
                def _(sc):
                    for tt in range(OTT):
                        sc.wait_ge(sem("v_mv"), tt + 1)
                        sc.activation(rstd[:, tt % 2, :], mv[:, tt % 2, 1:2],
                                      ACTF.Sqrt, bias=eps_t[:]
                                      ).then_inc(sem("a_rstd"), 1)
                    for oc in range(QDCH):
                        sc.wait_ge(sem("v_qkve"), oc + 1)
                        sc.dma_start(out=qT_dram[oc * 128:(oc + 1) * 128, :],
                                     in_=qkv_ev[:, oc % 2, :]
                                     ).then_inc(sem("st_q%d" % (oc % 2)), 16)
                    for j in range(2 * KVH * OTT):
                        sc.wait_ge(sem("v_pse"), tpi(("kvp", j)) + 1)
                        kvoc, tt = j // OTT, j % OTT
                        isv = kvoc >= KVH
                        hd = kvoc - KVH if isv else kvoc
                        tgt = v_own if isv else k_own
                        sc.dma_start(out=tgt[tt * 128:(tt + 1) * 128,
                                             hd * 128:(hd + 1) * 128],
                                     in_=kvp[:, j % 2, :]
                                     ).then_inc(sem("st_kv%d" % (j % 2)), 16)

                @block.vector
                def _(v):
                    def evac_xnT(tt):
                        for hc in range(HCH):
                            idx = tpi(("xnT", tt, hc))
                            v.wait_ge(sem("t_ps"), idx + 1)
                            v.tensor_copy(xnT_sb[:, hc, tt * 128:(tt + 1) * 128],
                                          psd_slot(idx)).then_inc(sem("v_pse"), 1)
                    for tt in range(OTT):
                        v.wait_ge(sem("ld_x%d" % (tt % 2)), 16 * (tt // 2 + 1))
                        xt = x_sb[:, tt % 2, :]
                        for cc in range(D // 512):
                            v.bn_stats(stats[:, cc, :], xt[:, cc * 512:(cc + 1) * 512])
                        v.drain()
                        v.bn_aggr(mv[:, tt % 2, :],
                                  stats.ap().rearrange("p a b -> p (a b)"))
                        v.drain()
                        v.tensor_scalar(negm[:, tt % 2, :], mv[:, tt % 2, 0:1],
                                        -1.0, None, ALU.mult).then_inc(sem("v_mv"), 1)
                        v.wait_ge(sem("a_rstd"), tt + 1)
                        v.reciprocal(rstd[:, tt % 2, :], rstd[:, tt % 2, :])
                        v.drain()
                        v.tensor_scalar(xn_sb[:, tt % 2, :], xt,
                                        negm[:, tt % 2, :], rstd[:, tt % 2, :],
                                        ALU.add, ALU.mult).then_inc(sem("v_xn"), 1)
                        if tt >= 1:
                            evac_xnT(tt - 1)
                    evac_xnT(OTT - 1)
                    prep_cur = [0]

                    def emit_prep():
                        if prep_cur[0] >= NPREP:
                            return
                        k = prep_cur[0]
                        prep_cur[0] += 1
                        kind, w, c0, cn, doff = prep_chunks[k]
                        v.wait_ge(sem("g_pld"), 16 * (k + 1))
                        if k >= 2:
                            v.wait_ge(sem("g_pst%d" % (k % 2)),
                                      pst_after[k - 2][k % 2])
                        if kind == "hi":
                            v.tensor_single_scalar(pin32[:, k % 2, 0:cn],
                                                   pin32[:, k % 2, 0:cn],
                                                   1, ALU.logical_shift_right)
                            v.drain()
                            v.tensor_copy(pout16[:, k % 2, 0:cn],
                                          pin32[:, k % 2, 0:cn]
                                          ).then_inc(sem("v_prep"), 1)
                        else:
                            v.tensor_single_scalar(pin32[:, k % 2, 0:cn],
                                                   pin32[:, k % 2, 0:cn],
                                                   1, ALU.bitwise_and)
                            v.drain()
                            v.tensor_copy(poutf[:, k % 2, 0:cn],
                                          pin32[:, k % 2, 0:cn]
                                          ).then_inc(sem("v_prep"), 1)
                        v.drain()

                    for oc in range(QKVD // 128):
                        emit_prep()
                        v.wait_ge(sem("t_qkv"), oc + 1)
                        src = psum_ab(oc)[:, 0:NQ]
                        if oc < QDCH:
                            if oc >= 2:
                                v.wait_ge(sem("st_q%d" % (oc % 2)), 16 * (oc // 2))
                            v.tensor_copy(qkv_ev[:, oc % 2, :], src
                                          ).then_inc(sem("v_qkve"), 1)
                        elif oc < QDCH + KVH:
                            v.tensor_copy(kT_own[:, oc - QDCH, :], src
                                          ).then_inc(sem("v_qkve"), 1)
                        else:
                            v.tensor_copy(vT_tmp[:, oc - QDCH - KVH, :], src
                                          ).then_inc(sem("v_qkve"), 1)
                    for j in range(2 * KVH * OTT):
                        idx = tpi(("kvp", j))
                        v.wait_ge(sem("t_ps"), idx + 1)
                        if j >= 2:
                            v.wait_ge(sem("st_kv%d" % (j % 2)), 16 * (j // 2))
                        v.tensor_copy(kvp[:, j % 2, :], psd_slot(idx)
                                      ).then_inc(sem("v_pse"), 1)
                    while prep_cur[0] < NPREP:
                        emit_prep()

                @block.tensor
                def _(t):
                    for tt in range(OTT):
                        t.wait_ge(sem("v_xn"), tt + 1)
                        for hc in range(HCH):
                            idx = tpi(("xnT", tt, hc))
                            if idx >= NSLOT:
                                t.wait_ge(sem("v_pse"), idx - NSLOT + 1)
                            t.transpose(psd_slot(idx),
                                        xn_sb[:, tt % 2, hc * 128:(hc + 1) * 128],
                                        ident[:]).then_inc(sem("t_ps"), 1)
                    t.wait_ge(sem("v_pse"), OTT * HCH)
                    for oc in range(QKVD // 128):
                        t.wait_ge(sem("ld_wq%d" % (oc % 2)), 16 * (oc // 2 + 1))
                        if oc >= 2:
                            t.wait_ge(sem("v_qkve"), oc - 1)
                        pdst = psum_ab(oc)[:, 0:NQ]
                        i = None
                        for hc in range(HCH):
                            for h5 in range(NQH):
                                i = t.matmul(pdst[:, h5 * 512:(h5 + 1) * 512],
                                             wqs[:, oc % 2, hc, :],
                                             xnT_sb[:, hc, h5 * 512:(h5 + 1) * 512],
                                             start=(hc == 0), stop=(hc == HCH - 1))
                        i.then_inc(sem("t_qkv"), 1)
                    for j in range(2 * KVH * OTT):
                        kvoc, tt = j // OTT, j % OTT
                        src = (kT_own[:, kvoc, tt * 128:(tt + 1) * 128]
                               if kvoc < KVH else
                               vT_tmp[:, kvoc - KVH, tt * 128:(tt + 1) * 128])
                        t.wait_ge(sem("v_qkve"), QDCH + kvoc + 1)
                        idx = tpi(("kvp", j))
                        t.wait_ge(sem("v_pse"), idx - NSLOT + 1)
                        t.transpose(psd_slot(idx), src, ident[:]).then_inc(sem("t_ps"), 1)

        if phases < 3:
            dbg_terminal(2)
            mybir.codegen_inst_isa_subclasses(nc)
            return nc
        with contextlib.ExitStack() as scB:
            sbB = lambda n, sh, dt: scB.enter_context(nc.sbuf_tensor(n, sh, dt))
            ktb = sbB("ktb", [128, S], BF16)
            vbuf = sbB("vbuf", [128, SKC, 129], BF16)
            vrect = sbB("vrect", [128, SKC, 129], BF16)
            qtb = sbB("qtb", [128, 2, NQ], BF16)
            pT = sbB("pT", [128, 4, 512], BF16)
            rec = sbB("rec", [128, 2, 1], F32)
            at_ev = sbB("at_ev", [128, 2, 128], BF16)
            atT_row = sbB("atT_row", [128, 2, NQ], BF16)
            RAWC = max(ICH, HCH)
            N16MAX = max(IFF, D) // 16
            d_raw = sbB("d_raw", [128, DQR, RAWC, 64], F32)
            d_hi = sbB("d_hi", [128, DQR, N16MAX], I16)
            d_lo = sbB("d_lo", [128, DQR, RAWC], F32)
            d_dif = sbB("d_dif", [128, RAWC, 8], F32)
            d_asm = sbB("d_asm", [128, 2, RAWC, 16, 8], BF16)
            d_row = sbB("d_row", [128, max(IFF, D)], BF16)

            with nc.Block() as block:
                @block.gpsimd
                def _(g):
                    nidx_regs = {}
                    for chs in dq_chunks:
                        for r0x, crawcx in chs:
                            if crawcx * 128 not in nidx_regs:
                                nidx_regs[crawcx * 128] = g.to_reg(crawcx * 128)
                    g.wait_ge(sem("st_q0"), 16 * ((QDCH + 1) // 2))
                    g.wait_ge(sem("st_q1"), 16 * (QDCH // 2))
                    g.wait_ge(sem("st_kv0"), 16 * KVH * OTT)
                    g.wait_ge(sem("st_kv1"), 16 * KVH * OTT)
                    g.memset(d_hi[:], 0)
                    g.drain()
                    g.memset(vbuf[:, :, 128:129], 1.0).then_inc(sem("g_vones"), 1)
                    g.collective_compute(
                        "AllGather", ALU.bypass,
                        replica_groups=[[0, 1, 2, 3], [4, 5, 6, 7]],
                        ins=[k_own.opt()], outs=[k_all.opt()]).then_inc(sem("cc_kv"), 1)
                    g.collective_compute(
                        "AllGather", ALU.bypass,
                        replica_groups=[[0, 1, 2, 3], [4, 5, 6, 7]],
                        ins=[v_own.opt()], outs=[v_all.opt()]).then_inc(sem("cc_kv"), 1)
                    if NPREP:
                        g.wait_ge(sem("v_prep"), NPREP)
                        g.wait_ge(sem("g_pst0"), pst_after[NPREP - 1][0])
                        g.wait_ge(sem("g_pst1"), pst_after[NPREP - 1][1])
                    stc = 0

                    def dq_loads(ci):
                        if ci >= DQR:
                            g.wait_ge(sem("g_sel"), ci - DQR + 1)
                        par2 = ci % DQR
                        n16b = dq_calls[ci][2] // 16
                        g.dma_start(out=d_hi[0:128, par2, 0:n16b],
                                    in_=hi_dram[:, off16[ci]:off16[ci] + n16b]
                                    ).then_inc(sem("g_ld"), 16)
                        g.dma_start(out=d_lo[:, par2, 0:dq_calls[ci][3]],
                                    in_=lo_dram[:, offL[ci]:offL[ci] + dq_calls[ci][3]]
                                    ).then_inc(sem("g_ld"), 16)

                    dq_loads(0)
                    for ci, (w, j, nidx, rawc) in enumerate(dq_calls):
                        par = ci % DQR
                        n16 = nidx // 16
                        grp = call_group[ci]
                        if grp >= 2 and call_group[ci - 1] != grp:
                            pg = grp - 2
                            g.wait_ge(sem("t_ps"),
                                      tpi(("dq", pg, dq_groups[pg][3] - 1)) + 1)
                        g.wait_ge(sem("g_ld"), 32 * (ci + 1))
                        for (r0, crawc), qn in zip(dq_chunks[ci], call_qnums[ci]):
                            cn = crawc * 128
                            g.dma_gather(d_raw[:, par, r0:r0 + crawc, :], cbp[:],
                                         d_hi[:, par, r0 * 8:r0 * 8 + cn // 16],
                                         num_idxs=cn,
                                         num_idxs_reg=nidx_regs[cn],
                                         elem_size=64, elem_step=64,
                                         queue_num=qn
                                         ).then_inc(
                                sem("g_gq%d_%d" % (qn, ci % DQR)), 16)
                        if ci + 1 < NDQ:
                            dq_loads(ci + 1)
                        for w2 in ("g", "u", "d"):
                            if ag_issue_after[w2] == ci + 1:
                                stc2 = 16 * sum(ngrp_w[w3] for w3 in ("g", "u", "d")
                                                if ("gud".index(w3) <= "gud".index(w2)))
                                g.wait_ge(sem("st_dq"), stc2)
                                g.collective_compute(
                                    "AllGather", ALU.bypass,
                                    replica_groups=[list(range(8))],
                                    ins=[ag_in[w2].opt()], outs=[ag_out[w2].opt()]
                                ).then_inc(sem("cc_w"), 1)

                @block.sync
                def _(s):
                    s.wait_ge(sem("cc_kv"), 2)
                    for kv in range(KVH):
                        s.dma_start(out=vdiag[:, kv, :, 0:128],
                                    in_=v_own[:, kv * 128:(kv + 1) * 128].rearrange(
                                        "(c p) d -> p c d", p=128)
                                    ).then_inc(sem("ld_vd"), 16)
                    for kvh in range(KVH):
                        if kvh >= 1:
                            s.wait_ge(sem("t_av"), head_last_entry[kvh * G - 1])
                        s.dma_start_transpose(ktb[:],
                                              k_all[:, kvh * 128:(kvh + 1) * 128]
                                              ).then_inc(sem("ld_kt"), 16)
                        if kvh >= 1:
                            s.wait_ge(sem("v_vr"), kvh)
                        s.dma_start(out=vbuf[:, :, 0:128],
                                    in_=v_all[:, kvh * 128:(kvh + 1) * 128].rearrange(
                                        "(c p) d -> p c d", p=128)
                                    ).then_inc(sem("ld_v"), 16)
                        for gq in range(G):
                            qh = kvh * G + gq
                            if qh >= 2:
                                s.wait_ge(sem("t_av"), head_last_entry[qh - 2])
                            s.dma_start(out=qtb[:, qh % 2, :],
                                        in_=qT_dram[qh * 128:(qh + 1) * 128, :]
                                        ).then_inc(sem("ld_qt%d" % (qh % 2)), 16)

                @block.vector
                def _(v):
                    v.wait_ge(sem("ld_valid"), 32)
                    v.wait_ge(sem("g_vones"), 1)
                    for kvh in range(KVH):
                        v.wait_ge(sem("ld_v"), 16 * (kvh + 1))
                        if kvh >= 1:
                            v.wait_ge(sem("t_av"), head_last_entry[kvh * G - 1])
                        vb_ = s_valid.ap()[:, :, None].broadcast_to([128, SKC, 129])
                        v.tensor_tensor(vrect[:], vbuf[:], vb_, ALU.mult
                                        ).then_inc(sem("v_vr"), 1)
                        for gq in range(G):
                            qh = kvh * G + gq
                            def dq_select(ci):
                                w, j, nidx, rawc = dq_calls[ci]
                                par = ci % DQR
                                grp = call_group[ci]
                                if grp >= 2 and (ci == 0 or call_group[ci - 1] != grp):
                                    pg = grp - 2
                                    v.wait_ge(sem("t_ps"),
                                              tpi(("dq", pg, dq_groups[pg][3] - 1)) + 1)
                                for qq, rr, cc in call_qwaits[ci]:
                                    v.wait_ge(sem("g_gq%d_%d" % (qq, rr)),
                                              16 * cc)
                                lob = d_lo.ap()[:, par, 0:rawc, None].broadcast_to(
                                    [128, rawc, 8])
                                v.tensor_tensor(d_dif[:, 0:rawc, :],
                                                d_raw[:, par, 0:rawc, 8:16],
                                                d_raw[:, par, 0:rawc, 0:8],
                                                ALU.subtract)
                                v.drain()
                                v.tensor_tensor(d_dif[:, 0:rawc, :],
                                                d_dif[:, 0:rawc, :], lob, ALU.mult)
                                v.drain()
                                v.tensor_tensor(d_asm[:, grp % 2, 0:rawc, j % 16, :],
                                                d_dif[:, 0:rawc, :],
                                                d_raw[:, par, 0:rawc, 0:8],
                                                ALU.add).then_inc(sem("g_sel"), 1)
                                v.drain()

                            hc_ = head_calls[qh]
                            nsel = len(hc_)
                            sel_batches = [hc_[0:(nsel + 2) // 3],
                                           hc_[(nsel + 2) // 3:(2 * nsel + 2) // 3],
                                           hc_[(2 * nsel + 2) // 3:]]
                            for ci in sel_batches[0]:
                                dq_select(ci)
                            for h5 in range(NQH5):
                                g5 = qh * NQH5 + h5
                                for qt in range(QT5):
                                    qtg = h5 * QT5 + qt
                                    v.wait_ge(sem("t_av"), att_group_last[g5] + 1)
                                    slot = psBC_att(qt)
                                    v.reciprocal(rec[:, qt % 2, :], slot[:, 128:129])
                                    v.drain()
                                    pidx = qh * OTT + qtg
                                    if pidx >= 2:
                                        prev = tpi(("at", (pidx - 2) // OTT,
                                                    (pidx - 2) % OTT))
                                        v.wait_ge(sem("t_ps"), prev + 1)
                                    v.tensor_scalar(at_ev[:, qtg % 2, :],
                                                    slot[:, 0:128],
                                                    rec[:, qt % 2, :], None, ALU.mult
                                                    ).then_inc(sem("v_at"), 1)
                                for qt in range(QT5):
                                    qtg = h5 * QT5 + qt
                                    idx = tpi(("at", qh, qtg))
                                    v.wait_ge(sem("t_ps"), idx + 1)
                                    if h5 == 0 and qt == 0 and qh >= 2:
                                        v.wait_ge(sem("st_at%d" % (qh % 2)),
                                                  16 * (qh // 2))
                                    v.tensor_copy(
                                        atT_row[:, qh % 2, qtg * 128:(qtg + 1) * 128],
                                        psd_slot(idx)).then_inc(sem("v_pse"), 1)
                                if h5 == 0:
                                    for ci in sel_batches[1]:
                                        dq_select(ci)
                            for ci in sel_batches[2]:
                                dq_select(ci)
                            for k in head_groups[qh]:
                                wg, j0g, gg, rawcg, rbg = dq_groups[k]
                                rows = gg * 8
                                for ic in range(rawcg):
                                    idx = tpi(("dq", k, ic))
                                    v.wait_ge(sem("t_ps"), idx + 1)
                                    if ic == 0 and k >= 1:
                                        v.wait_ge(sem("st_dq"), 16 * k)
                                    v.tensor_copy(d_row[0:rows, ic * 128:(ic + 1) * 128],
                                                  psd_slot(idx, rows)
                                                  ).then_inc(sem("v_pse"), 1)

                @block.scalar
                def _(sc):
                    for kvh in range(KVH):
                        for gq in range(G):
                            qh = kvh * G + gq
                            for h5 in range(NQH5):
                                ents5 = [att_entry[(qh, h5, kc)]
                                         for kc in range(KC_ALL)
                                         if (qh, h5, kc) in att_entry]
                                z = 0
                                while z < len(ents5):
                                    ei = ents5[z]
                                    if ei % 2 == 0 and z + 1 < len(ents5):
                                        sc.wait_ge(sem("t_s"), ei + 2)
                                        if ei >= 3:
                                            sc.wait_ge(sem("t_av"), ei - 2)
                                        sc.activation(pT[:, ei % 4:ei % 4 + 2, :],
                                                      psA[:, 0:1024],
                                                      ACTF.Exp, bias=ebias[:],
                                                      scale=SCALE
                                                      ).then_inc(sem("a_exp"), 1)
                                        z += 2
                                        continue
                                    sc.wait_ge(sem("t_s"), ei + 1)
                                    if ei >= 4:
                                        sc.wait_ge(sem("t_av"), ei - 3)
                                    sc.activation(pT[:, ei % 4, :],
                                                  psA[:, (ei % 2) * 512:
                                                      (ei % 2) * 512 + 512],
                                                  ACTF.Exp, bias=ebias[:],
                                                  scale=SCALE).then_inc(sem("a_exp"), 1)
                                    z += 1
                            sc.wait_ge(sem("v_pse"), tpi(("at", qh, OTT - 1)) + 1)
                            sc.dma_start(out=attnT_dram[qh * 128:(qh + 1) * 128, :],
                                         in_=atT_row[:, qh % 2, :]
                                         ).then_inc(sem("st_at%d" % (qh % 2)), 16)
                            for k in head_groups[qh]:
                                wg, j0g, gg, rawcg, rbg = dq_groups[k]
                                rows = gg * 8
                                sc.wait_ge(sem("v_pse"), tpi(("dq", k, rawcg - 1)) + 1)
                                sc.dma_start(out=ag_in[wg][rbg:rbg + rows, :],
                                             in_=d_row[0:rows, 0:rawcg * 128]
                                             ).then_inc(sem("st_dq"), 16)

                @block.tensor
                def _(t):
                    t.wait_ge(sem("ld_vd"), 16 * KVH)
                    for kvh in range(KVH):
                        t.wait_ge(sem("ld_kt"), 16 * (kvh + 1))
                        for gq in range(G):
                            qh = kvh * G + gq
                            t.wait_ge(sem("ld_qt%d" % (qh % 2)), 16 * (qh // 2 + 1))
                            if gq == 0:
                                t.wait_ge(sem("v_vr"), kvh + 1)
                            for h5 in range(NQH5):
                                g5 = qh * NQH5 + h5
                                ents = [(att_entry[(qh, h5, kc)], kc)
                                        for kc in range(KC_ALL)
                                        if (qh, h5, kc) in att_entry]

                                def qk(ei, kc):
                                    if ei >= 2:
                                        t.wait_ge(sem("a_exp"),
                                                  expop_after[ei - 2])
                                    pdst = psA[:, (ei % 2) * 512:(ei % 2) * 512 + 512]
                                    lhs = (ktb[:, kc * 128:(kc + 1) * 128]
                                           if kc < SKC else
                                           kT_own[:, kvh, (kc - SKC) * 128:
                                                  (kc - SKC + 1) * 128])
                                    part = att_sched[ei][3] == "part"
                                    i = t.matmul(pdst, lhs,
                                                 qtb[:, qh % 2, h5 * 512:(h5 + 1) * 512],
                                                 start=True, stop=not part)
                                    if part:
                                        mloc = (kc - SKC) - 4 * h5
                                        i = t.matmul(pdst, ident[:],
                                                     masks[:, mloc, :],
                                                     start=False, stop=True)
                                    i.then_inc(sem("t_s"), 1)

                                def av(ei, kc):
                                    t.wait_ge(sem("a_exp"), expop_after[ei])
                                    if ei == att_group_first[g5]:
                                        t.wait_ge(sem("v_at"), g5 * QT5)
                                    rhs_t = (vrect[:, kc, :] if kc < SKC
                                             else vdiag[:, kvh, kc - SKC, :])
                                    i = None
                                    for qt in range(QT5):
                                        i = t.matmul(psBC_att(qt),
                                                     pT[:, ei % 4,
                                                        qt * 128:(qt + 1) * 128],
                                                     rhs_t,
                                                     start=(ei == att_group_first[g5]),
                                                     stop=(ei == att_group_last[g5]))
                                    i.then_inc(sem("t_av"), 1)

                                qk(*ents[0])
                                for z in range(1, len(ents)):
                                    qk(*ents[z])
                                    av(*ents[z - 1])
                                av(*ents[-1])
                                for qt in range(QT5):
                                    qtg = h5 * QT5 + qt
                                    t.wait_ge(sem("v_at"), qh * OTT + qtg + 1)
                                    idx = tpi(("at", qh, qtg))
                                    t.wait_ge(sem("v_pse"), idx - NSLOT + 1)
                                    t.transpose(psd_slot(idx), at_ev[:, qtg % 2, :],
                                                ident[:]).then_inc(sem("t_ps"), 1)
                            for k in head_groups[qh]:
                                wg, j0g, gg, rawcg, rbg = dq_groups[k]
                                rows = gg * 8
                                t.wait_ge(sem("g_sel"), grp_last_call[k])
                                for ic in range(rawcg):
                                    idx = tpi(("dq", k, ic))
                                    t.wait_ge(sem("v_pse"), idx - NSLOT + 1)
                                    t.transpose(
                                        psd_slot(idx, rows),
                                        d_asm.ap()[:, k % 2, ic, 0:gg, :].rearrange(
                                            "p a b -> p (a b)"),
                                        ident[:]).then_inc(sem("t_ps"), 1)

        scAB.close()
        if phases < 5:
            dbg_terminal(3)
            mybir.codegen_inst_isa_subclasses(nc)
            return nc

        OC1 = D // 128
        NPASS = OTT // 2
        with contextlib.ExitStack() as scC2:
            sbC2 = lambda n, sh, dt: scC2.enter_context(nc.sbuf_tensor(n, sh, dt))
            xn2T_sb = sbC2("xn2T_sb", [128, HCH, NQ], BF16)

            with contextlib.ExitStack() as scC:
                sbC = lambda n, sh, dt: scC.enter_context(nc.sbuf_tensor(n, sh, dt))
                atT_p = sbC("atT_p", [128, QDCH, 256], BF16)
                wds = sbC("wds", [128, 2, QDCH, 128], BF16)
                y_sb = sbC("y_sb", [128, 2, D], BF16)
                lnbuf = sbC("lnbuf", [128, D], F32)
                xf_buf = sbC("xf_buf", [128, D], F32)
                x2tile = sbC("x2tile", [128, D], F32)
                xn2_sb = sbC("xn2_sb", [128, D], BF16)
                x1b = sbC("x1b", [128, D], BF16)
                stats2 = sbC("stats2", [128, D // 512, 6], F32)
                mv2 = sbC("mv2", [128, 2], F32)
                rstd2 = sbC("rstd2", [128, 1], F32)
                negm2 = sbC("negm2", [128, 1], F32)

                with nc.Block() as block:
                    @block.sync
                    def _(s):
                        s.wait_ge(sem("st_wdbf"), 16 * QDCH)
                        s.wait_ge(sem("st_dq"), 16 * NGRP)
                        s.wait_ge(sem("st_at0"), 16 * (NH // 2))
                        s.wait_ge(sem("st_at1"), 16 * (NH // 2))
                        for ps in range(NPASS):
                            if ps >= 1:
                                s.wait_ge(sem("t_den"), ps * OC1 * 2)
                            s.dma_start(out=atT_p[:],
                                        in_=attnT_dram[:, ps * 256:(ps + 1) * 256]
                                        .rearrange("(fc p) q -> p fc q", p=128)
                                        ).then_inc(sem("ld_ath"), 16)
                            for oc in range(OC1):
                                gidx = ps * OC1 + oc
                                if gidx >= 2:
                                    s.wait_ge(sem("t_den"), (gidx - 1) * 2)
                                s.dma_start(out=wds[:, gidx % 2, :, :],
                                            in_=wd_bf[:, oc * 128:(oc + 1) * 128]
                                            .rearrange("(hc p) o -> p hc o", p=128)
                                            ).then_inc(sem("ld_wd%d" % (gidx % 2)), 16)
                            for t2 in range(2):
                                row = ps * 2 + t2
                                if row >= 1:
                                    s.wait_ge(sem("v_x1"), row)
                                s.dma_start(out=x2tile[:],
                                            in_=x_own[row * 128:(row + 1) * 128, :]
                                            ).then_inc(sem("ld_x2"), 16)

                    @block.tensor
                    def _(t):
                        for ps in range(NPASS):
                            t.wait_ge(sem("ld_ath"), 16 * (ps + 1))
                            for oc in range(OC1):
                                gidx = ps * OC1 + oc
                                t.wait_ge(sem("ld_wd%d" % (gidx % 2)),
                                          16 * (gidx // 2 + 1))
                                for t2 in range(2):
                                    didx = gidx * 2 + t2
                                    if didx >= 4:
                                        t.wait_ge(sem("v_den"), didx - 3)
                                    sl = didx % 4
                                    pb = psA if sl < 2 else psB
                                    pdst = pb[:, (sl % 2) * 512:(sl % 2) * 512 + 128]
                                    i = None
                                    for fc in range(QDCH):
                                        i = t.matmul(pdst,
                                                     atT_p[:, fc, t2 * 128:(t2 + 1) * 128],
                                                     wds[:, gidx % 2, fc, :],
                                                     start=(fc == 0),
                                                     stop=(fc == QDCH - 1))
                                    i.then_inc(sem("t_den"), 1)
                            for t2 in range(2):
                                row = ps * 2 + t2
                                t.wait_ge(sem("v_xn2"), row + 1)
                                for hc in range(HCH):
                                    idx = tpi(("x2T", row, hc))
                                    t.wait_ge(sem("v_pse"), idx - NSLOT + 1)
                                    t.transpose(psd_slot(idx),
                                                xn2_sb[:, hc * 128:(hc + 1) * 128],
                                                ident[:]).then_inc(sem("t_ps"), 1)

                    @block.vector
                    def _(v):
                        for ps in range(NPASS):
                            for oc in range(OC1):
                                gidx = ps * OC1 + oc
                                for t2 in range(2):
                                    didx = gidx * 2 + t2
                                    v.wait_ge(sem("t_den"), didx + 1)
                                    sl = didx % 4
                                    pb = psA if sl < 2 else psB
                                    v.tensor_copy(
                                        y_sb[:, t2, oc * 128:(oc + 1) * 128],
                                        pb[:, (sl % 2) * 512:(sl % 2) * 512 + 128]
                                    ).then_inc(sem("v_den"), 1)
                            v.drain()
                            for t2 in range(2):
                                row = ps * 2 + t2
                                for cc in range(D // 512):
                                    v.bn_stats(stats2[:, cc, :],
                                               y_sb[:, t2, cc * 512:(cc + 1) * 512])
                                v.drain()
                                v.bn_aggr(mv2[:],
                                          stats2.ap().rearrange("p a b -> p (a b)"))
                                v.drain()
                                v.tensor_scalar(negm2[:], mv2[:, 0:1], -1.0, None,
                                                ALU.mult).then_inc(sem("v_mv2"), 1)
                                v.wait_ge(sem("a_rstd2"), 2 * row + 1)
                                v.reciprocal(rstd2[:], rstd2[:])
                                v.drain()
                                v.tensor_scalar(lnbuf[:], y_sb[:, t2, :], negm2[:],
                                                rstd2[:], ALU.add, ALU.mult)
                                v.drain()
                                v.wait_ge(sem("ld_x2"), 16 * (row + 1))
                                v.tensor_tensor(xf_buf[:], x2tile[:],
                                                lnbuf[:], ALU.add)
                                v.drain()
                                if row >= 1:
                                    v.wait_ge(sem("st_x1"), 16 * row)
                                v.tensor_copy(x1b[:], xf_buf[:]
                                              ).then_inc(sem("v_x1"), 1)
                                for cc in range(D // 512):
                                    v.bn_stats(stats2[:, cc, :],
                                               xf_buf[:, cc * 512:(cc + 1) * 512])
                                v.drain()
                                v.bn_aggr(mv2[:],
                                          stats2.ap().rearrange("p a b -> p (a b)"))
                                v.drain()
                                v.tensor_scalar(negm2[:], mv2[:, 0:1], -1.0, None,
                                                ALU.mult).then_inc(sem("v_mv2"), 1)
                                v.wait_ge(sem("a_rstd2"), 2 * row + 2)
                                v.reciprocal(rstd2[:], rstd2[:])
                                v.drain()
                                v.tensor_scalar(xn2_sb[:], xf_buf[:], negm2[:],
                                                rstd2[:], ALU.add, ALU.mult
                                                ).then_inc(sem("v_xn2"), 1)
                                for hc in range(HCH):
                                    idx = tpi(("x2T", row, hc))
                                    v.wait_ge(sem("t_ps"), idx + 1)
                                    v.tensor_copy(
                                        xn2T_sb[:, hc, row * 128:(row + 1) * 128],
                                        psd_slot(idx)).then_inc(sem("v_pse"), 1)

                    @block.scalar
                    def _(sc):
                        for row in range(OTT):
                            for half in range(2):
                                sc.wait_ge(sem("v_mv2"), 2 * row + half + 1)
                                sc.activation(rstd2[:], mv2[:, 1:2], ACTF.Sqrt,
                                              bias=eps_t[:]).then_inc(sem("a_rstd2"), 1)
                            sc.wait_ge(sem("v_x1"), row + 1)
                            sc.dma_start(out=x1_dram[row * 128:(row + 1) * 128, :],
                                         in_=x1b[:]).then_inc(sem("st_x1"), 16)

            if phases < 6:
                dbg_terminal(5)
                mybir.codegen_inst_isa_subclasses(nc)
                return nc
            with contextlib.ExitStack() as scD:
                sbD = lambda n, sh, dt: scD.enter_context(nc.sbuf_tensor(n, sh, dt))
                IBMAX = max(IBLOCKS)
                gus = sbD("gus", [128, 2, 2, HCH, 256], BF16)
                gs_ev = sbD("gs_ev", [128, 2, 256], BF16)
                h_ev = sbD("h_ev", [128, 2, 256], BF16)
                hT_ib = sbD("hT_ib", [128, IBMAX, NQ], BF16)
                wdt = sbD("wdt", [128, 2, 512], BF16)
                y2row = sbD("y2row", [128, 2, 512], BF16)
                first_gu_of_ib = {}
                for (gidx, ib, ri, ic0, c0, r, tt) in gu_plan:
                    if ib not in first_gu_of_ib:
                        first_gu_of_ib[ib] = gidx
                last_hT_of_gidx = {g0: ("hT", ib, c0 + r - 1, tt)
                                   for (g0, ib, ri, ic0, c0, r, tt) in gu_plan}

                with nc.Block() as block:
                    @block.sync
                    def _(s):
                        s.wait_ge(sem("cc_w"), 2)
                        s.wait_ge(sem("st_x1"), 16 * OTT)
                        dn = 0
                        NTH = max(1, OTT // 4)
                        for ib in range(NIB):
                            icb = sum(IBLOCKS[:ib])
                            nch = IBLOCKS[ib]
                            for ri, (ib2, run_i, ic0, c0, r) in enumerate(ffn_runs):
                                if ib2 != ib:
                                    continue
                                if ri >= 2:
                                    s.wait_ge(sem("t_gu"), (ri - 1) * OTT)
                                s.dma_start(out=gus[:, ri % 2, 0, :, 0:r * 128],
                                            in_=ag_out["g"][:, ic0 * 128:(ic0 + r) * 128]
                                            .rearrange("(hc p) i -> p hc i", p=128)
                                            ).then_inc(sem("ld_gu%d" % (ri % 2)), 16)
                                s.dma_start(out=gus[:, ri % 2, 1, :, 0:r * 128],
                                            in_=ag_out["u"][:, ic0 * 128:(ic0 + r) * 128]
                                            .rearrange("(hc p) i -> p hc i", p=128)
                                            ).then_inc(sem("ld_gu%d" % (ri % 2)), 16)
                            if ib == 0:
                                s.wait_ge(sem("cc_w"), 3)
                            for oc in range(OCD):
                                for th in range(NTH):
                                    for icr in range(nch):
                                        if dn >= 2:
                                            s.wait_ge(sem("t_dn"), dn - 1)
                                        s.dma_start(
                                            out=wdt[:, dn % 2, :],
                                            in_=ag_out["d"][(icb + icr) * 128:
                                                            (icb + icr + 1) * 128,
                                                            oc * 512:(oc + 1) * 512]
                                        ).then_inc(sem("ld_dn%d" % (dn % 2)), 16)
                                        dn += 1

                    @block.tensor
                    def _(t):
                        t.wait_ge(sem("v_pse"), tpi(("x2T", OTT - 1, HCH - 1)) + 1)
                        t.wait_ge(sem("v_den"), (D // 128) * OTT)
                        dn = 0
                        for ib in range(NIB):
                            nch = IBLOCKS[ib]
                            for (gidx, ib2, ri, ic0, c0, r, tt) in gu_plan:
                                if ib2 != ib:
                                    continue
                                if tt == 0:
                                    t.wait_ge(sem("ld_gu%d" % (ri % 2)),
                                              32 * (ri // 2 + 1))
                                if gidx >= 2:
                                    t.wait_ge(sem("v_h"), gidx - 1)
                                if ib >= 1 and gidx == first_gu_of_ib[ib]:
                                    t.wait_ge(sem("v_dn"), ib * OCD * OTT)
                                pd = psum_ab(gidx)
                                i = None
                                for hc in range(HCH):
                                    i = t.matmul(pd[:, 0:r * 128],
                                                 xn2T_sb[:, hc, tt * 128:(tt + 1) * 128],
                                                 gus[:, ri % 2, 0, hc, 0:r * 128],
                                                 start=(hc == 0), stop=(hc == HCH - 1))
                                for hc in range(HCH):
                                    i = t.matmul(pd[:, 512:512 + r * 128],
                                                 xn2T_sb[:, hc, tt * 128:(tt + 1) * 128],
                                                 gus[:, ri % 2, 1, hc, 0:r * 128],
                                                 start=(hc == 0), stop=(hc == HCH - 1))
                                i.then_inc(sem("t_gu"), 1)
                                t.wait_ge(sem("v_h"), gidx + 1)
                                for rc in range(r):
                                    idx = tpi(("hT", ib, c0 + rc, tt))
                                    t.wait_ge(sem("v_pse"), idx - NSLOT + 1)
                                    t.transpose(psd_slot(idx),
                                                h_ev[:, gidx % 2, rc * 128:(rc + 1) * 128],
                                                ident[:]).then_inc(sem("t_ps"), 1)
                            t.wait_ge(sem("v_pse"), tpi(("hT", ib, nch - 1, OTT - 1)) + 1)
                            NTH = max(1, OTT // 4)
                            for oc in range(OCD):
                                for th in range(NTH):
                                    gdi = (ib * OCD + oc) * NTH + th
                                    if gdi >= 1:
                                        t.wait_ge(sem("v_dn"), gdi * 4)
                                    for icr in range(nch):
                                        t.wait_ge(sem("ld_dn%d" % (dn % 2)),
                                                  16 * (dn // 2 + 1))
                                        i = None
                                        for tq in range(4):
                                            tt = th * 4 + tq
                                            i = t.matmul(psBC_slot256(tq),
                                                         hT_ib[:, icr,
                                                               tt * 128:(tt + 1) * 128],
                                                         wdt[:, dn % 2, :],
                                                         start=(icr == 0),
                                                         stop=(icr == nch - 1))
                                        i.then_inc(sem("t_dn"), 1)
                                        dn += 1

                    @block.scalar
                    def _(sc):
                        for ib in range(NIB):
                            for (gidx, ib2, ri, ic0, c0, r, tt) in gu_plan:
                                if ib2 != ib:
                                    continue
                                sc.wait_ge(sem("t_gu"), gidx + 1)
                                if gidx >= 2:
                                    sc.wait_ge(sem("v_h"), gidx - 1)
                                sc.activation(gs_ev[:, gidx % 2, 0:r * 128],
                                              psum_ab(gidx)[:, 0:r * 128],
                                              ACTF.Silu).then_inc(sem("a_silu"), 1)
                            NTH = max(1, OTT // 4)
                            for oc in range(OCD):
                                for th in range(NTH):
                                    for tq in range(4):
                                        tt = th * 4 + tq
                                        di = ((ib * OCD + oc) * NTH + th) * 4 + tq
                                        sc.wait_ge(sem("v_dn"), di + 1)
                                        sc.dma_start(
                                            out=y2p[ib, tt * 128:(tt + 1) * 128,
                                                    oc * 512:(oc + 1) * 512],
                                            in_=y2row[:, di % 2, :]
                                            ).then_inc(sem("st_y2%d" % (di % 2)), 16)

                    @block.vector
                    def _(v):
                        for ib in range(NIB):
                            nch = IBLOCKS[ib]
                            for (gidx, ib2, ri, ic0, c0, r, tt) in gu_plan:
                                if ib2 != ib:
                                    continue
                                v.wait_ge(sem("a_silu"), gidx + 1)
                                if gidx >= 2:
                                    v.wait_ge(sem("t_ps"),
                                              tpi(last_hT_of_gidx[gidx - 2]) + 1)
                                v.tensor_tensor(h_ev[:, gidx % 2, 0:r * 128],
                                                psum_ab(gidx)[:, 512:512 + r * 128],
                                                gs_ev[:, gidx % 2, 0:r * 128],
                                                ALU.mult).then_inc(sem("v_h"), 1)
                                for rc in range(r):
                                    idx = tpi(("hT", ib, c0 + rc, tt))
                                    v.wait_ge(sem("t_ps"), idx + 1)
                                    v.tensor_copy(
                                        hT_ib[:, c0 + rc, tt * 128:(tt + 1) * 128],
                                        psd_slot(idx)).then_inc(sem("v_pse"), 1)
                            NTH = max(1, OTT // 4)
                            for oc in range(OCD):
                                for th in range(NTH):
                                    for tq in range(4):
                                        tt = th * 4 + tq
                                        di = ((ib * OCD + oc) * NTH + th) * 4 + tq
                                        v.wait_ge(sem("t_dn"),
                                                  (sum(IBLOCKS[:ib]) * OCD * NTH
                                                   + (oc * NTH + th + 1) * nch))
                                        if di >= 2:
                                            v.wait_ge(sem("st_y2%d" % (di % 2)),
                                                      16 * (di // 2))
                                        v.tensor_copy(y2row[:, di % 2, :],
                                                      psBC_slot256(tq)
                                                      ).then_inc(sem("v_dn"), 1)

        if phases < 7:
            dbg_terminal(6)
            mybir.codegen_inst_isa_subclasses(nc)
            return nc
        with contextlib.ExitStack() as scE:
            sbE = lambda n, sh, dt: scE.enter_context(nc.sbuf_tensor(n, sh, dt))
            fin = sbE("fin", [128, NIB, D], BF16)
            x1f = sbE("x1f", [128, D], BF16)
            acc = sbE("acc", [128, D], F32)
            acc2 = sbE("acc2", [128, D], F32)
            outb = sbE("outb", [128, 2, D], F32)
            stats3 = sbE("stats3", [128, D // 512, 6], F32)
            mv3 = sbE("mv3", [128, 2], F32)
            rstd3 = sbE("rstd3", [128, 1], F32)
            negm3 = sbE("negm3", [128, 1], F32)

            with nc.Block() as block:
                @block.sync
                def _(s):
                    s.wait_ge(sem("st_y20"), 16 * (NIB * OCD * OTT // 2))
                    s.wait_ge(sem("st_y21"), 16 * (NIB * OCD * OTT - NIB * OCD * OTT // 2))
                    s.wait_ge(sem("st_x1"), 16 * OTT)
                    for tt in range(OTT):
                        if tt >= 1:
                            s.wait_ge(sem("v_fin"), tt)
                        for ib in range(NIB):
                            s.dma_start(out=fin[:, ib, :],
                                        in_=y2p[ib, tt * 128:(tt + 1) * 128, :]
                                        ).then_inc(sem("ld_fin"), 16)
                        s.dma_start(out=x1f[:],
                                    in_=x1_dram[tt * 128:(tt + 1) * 128, :]
                                    ).then_inc(sem("ld_fin"), 16)

                @block.vector
                def _(v):
                    for tt in range(OTT):
                        v.wait_ge(sem("ld_fin"), 16 * (NIB + 1) * (tt + 1))
                        if NIB >= 2:
                            v.tensor_tensor(acc[:], fin[:, 0, :], fin[:, 1, :], ALU.add)
                        else:
                            v.tensor_copy(acc[:], fin[:, 0, :])
                        v.drain()
                        cur = acc
                        for ib in range(2, NIB):
                            dst = acc2 if cur is acc else acc
                            v.tensor_tensor(dst[:], cur[:], fin[:, ib, :], ALU.add)
                            v.drain()
                            cur = dst
                        for cc in range(D // 512):
                            v.bn_stats(stats3[:, cc, :], cur[:, cc * 512:(cc + 1) * 512])
                        v.drain()
                        v.bn_aggr(mv3[:], stats3.ap().rearrange("p a b -> p (a b)"))
                        v.drain()
                        v.tensor_scalar(negm3[:], mv3[:, 0:1], -1.0, None, ALU.mult
                                        ).then_inc(sem("v_mv3"), 1)
                        v.wait_ge(sem("a_rstd3"), tt + 1)
                        v.reciprocal(rstd3[:], rstd3[:])
                        v.drain()
                        other = acc2 if cur is acc else acc
                        v.tensor_scalar(other[:], cur[:], negm3[:], rstd3[:],
                                        ALU.add, ALU.mult)
                        v.drain()
                        if tt >= 2:
                            v.wait_ge(sem("st_out%d" % (tt % 2)), 16 * (tt // 2))
                        v.tensor_tensor(outb[:, tt % 2, :], other[:], x1f[:],
                                        ALU.add).then_inc(sem("v_fin"), 1)

                @block.scalar
                def _(sc):
                    for tt in range(OTT):
                        sc.wait_ge(sem("v_mv3"), tt + 1)
                        sc.activation(rstd3[:], mv3[:, 1:2], ACTF.Sqrt,
                                      bias=eps_t[:]).then_inc(sem("a_rstd3"), 1)
                        sc.wait_ge(sem("v_fin"), tt + 1)
                        sc.dma_start(out=out[tt * 128:(tt + 1) * 128, :],
                                     in_=outb[:, tt % 2, :]
                                     ).then_inc(sem("st_out%d" % (tt % 2)), 16)
                    sc.wait_ge(sem("st_out0"), 16 * (OTT - OTT // 2))
                    sc.wait_ge(sem("st_out1"), 16 * (OTT // 2))

    mybir.codegen_inst_isa_subclasses(nc)
    return nc



def _wrap16x2(flat):
    w = flat.reshape(-1, 16).T
    w32 = np.concatenate([w, w], axis=0)
    return np.ascontiguousarray(
        np.vstack(np.split(w32, 4, axis=1)))


def _wrap128(flat):
    return np.ascontiguousarray(flat.reshape(-1, 128).T)


def make_in_maps(cfg, x, w_qkv, w_dense, codebook, idx_gate, idx_up, idx_down):
    C = derive(cfg)
    D, CBN = C["D"], C["CBN"]
    OWN, SKC, GJ, DJ = C["OWN"], C["SKC"], C["GJ"], C["DJ"]
    wqkvT = np.ascontiguousarray(np.asarray(w_qkv, dtype=np.float32).T)
    wdenT = np.ascontiguousarray(np.asarray(w_dense, dtype=np.float32).T)
    cbp = np.zeros((CBN // 2, 64), dtype=np.float32)
    cbp[:, :16] = np.asarray(codebook, dtype=np.float32).reshape(CBN // 2, 16)
    x = np.asarray(x, dtype=np.float32)
    idx_gate = np.asarray(idx_gate, dtype=np.int32)
    idx_up = np.asarray(idx_up, dtype=np.int32)
    idx_down = np.asarray(idx_down, dtype=np.int32)

    in_maps = []
    for c in range(NCORES):
        b, q = c // 4, c % 4
        xo = np.ascontiguousarray(x[b, q * OWN:(q + 1) * OWN, :])
        kk = np.arange(128)[:, None] + 128 * np.arange(SKC)[None, :]
        valid = (kk < OWN * q).astype(np.float32)
        pp = np.arange(128)[:, None]
        ff = np.arange(512)[None, :]
        maskin = np.stack([np.where(pp <= ff - 128 * mm, 0.0, -10000.0)
                           for mm in range(4)],
                          axis=1).astype(ml_dtypes.bfloat16)
        m = {"x_own": xo, "wqkvT": wqkvT, "wdenT": wdenT, "cbp": cbp,
             "validin": np.ascontiguousarray(valid),
             "maskin": np.ascontiguousarray(maskin)}
        gsl = idx_gate[:, GJ * c:GJ * (c + 1)].T.reshape(-1)
        usl = idx_up[:, GJ * c:GJ * (c + 1)].T.reshape(-1)
        dsl = idx_down[:, DJ * c:DJ * (c + 1)].T.reshape(-1)
        m["idxg16"], m["idxg128"] = _wrap16x2(gsl), _wrap128(gsl)
        m["idxu16"], m["idxu128"] = _wrap16x2(usl), _wrap128(usl)
        m["idxd16"], m["idxd128"] = _wrap16x2(dsl), _wrap128(dsl)
        in_maps.append(m)
    return in_maps


def kernel(L0_attn_q, L0_attn_w_qkv, L0_attn_w_dense, L0_ffn_codebook,
           L0_ffn_idx_gate, L0_ffn_idx_up, L0_ffn_idx_down):
    cfg = FULL
    C = derive(cfg)
    nc = build_program(cfg)
    in_maps = make_in_maps(cfg, L0_attn_q, L0_attn_w_qkv, L0_attn_w_dense,
                           L0_ffn_codebook, L0_ffn_idx_gate, L0_ffn_idx_up,
                           L0_ffn_idx_down)
    res = run_bass_kernel_spmd(nc, in_maps, core_ids=list(range(NCORES)))
    OWN, S, D = C["OWN"], C["S"], C["D"]
    full = np.empty((cfg["B"], S, D), dtype=np.float32)
    for c in range(NCORES):
        b, q = c // 4, c % 4
        full[b, q * OWN:(q + 1) * OWN, :] = np.asarray(
            res.results[c]["out"]).reshape(OWN, D)
    return full
